# version 03
# baseline (speedup 1.0000x reference)
import numpy as np
import ml_dtypes
from contextlib import ExitStack

import concourse.bass as bass
import concourse.mybir as mybir
from concourse.bass_utils import run_bass_kernel_spmd

F32 = mybir.dt.float32
BF16 = mybir.dt.bfloat16
I32 = mybir.dt.int32
AF = mybir.ActivationFunctionType
ALU = mybir.AluOpType
bfdt = ml_dtypes.bfloat16

D = 2048
T = 2048
I = 8192
KC = D // 128
NTB = 4
TB = 512
NPAIR = 16


def _consts():
    inv = 1.0 / 10000.0 ** (np.arange(0, 64, 2, dtype=np.float64) / 64.0)
    t = np.arange(T, dtype=np.float64)
    f = np.outer(t, inv)
    COS = np.tile(np.cos(f).T, (4, 1)).astype(np.float32).astype(bfdt)
    sgn = np.where((np.arange(128) // 32) % 2 == 0, -1.0, 1.0)[:, None]
    SINS = (np.tile(np.sin(f).T, (4, 1)) * sgn).astype(np.float32).astype(bfdt)
    m = np.arange(128)
    nb = (m // 64) * 64 + 2 * (m % 32) + ((m % 64) // 32)
    sigma = np.where(m % 64 < 32, m + 32, m - 32)
    PNB = np.zeros((128, 128), dtype=bfdt)
    PNB[nb, m] = 1
    PCMB = np.zeros((128, 128), dtype=bfdt)
    PCMB[nb[sigma], m] = 1
    DUPLO = np.zeros((128, 128), dtype=bfdt)
    DUPLO[m % 64, m] = 1
    DUPHI = np.zeros((128, 128), dtype=bfdt)
    DUPHI[64 + (m % 64), m] = 1
    MASKS = []
    tk = np.arange(128)[:, None]
    tq = np.arange(1024)[None, :] % 512
    for v in range(4):
        MASKS.append((tk <= tq - 128 * v).astype(np.float32).astype(bfdt))
    return COS, SINS, PNB, PCMB, DUPLO, DUPHI, MASKS


def build_nc(dbg=False):
    nc = bass.Bass("TRN2", target_bir_lowering=False)
    x_h = nc.dram_tensor("x", [T, D], F32, kind="ExternalInput").ap()
    wq_h = nc.dram_tensor("wq", [D, D], F32, kind="ExternalInput").ap()
    wk_h = nc.dram_tensor("wk", [512, D], F32, kind="ExternalInput").ap()
    wv_h = nc.dram_tensor("wv", [512, D], F32, kind="ExternalInput").ap()
    wo_h = nc.dram_tensor("wo", [D, D], F32, kind="ExternalInput").ap()
    gw_h = nc.dram_tensor("gw", [I, D], I32, kind="ExternalInput").ap()
    uw_h = nc.dram_tensor("uw", [I, D], I32, kind="ExternalInput").ap()
    dw_h = nc.dram_tensor("dw", [D, I], I32, kind="ExternalInput").ap()
    gs_h = nc.dram_tensor("gs", [I], F32, kind="ExternalInput").ap()
    us_h = nc.dram_tensor("us", [I], F32, kind="ExternalInput").ap()
    ds_h = nc.dram_tensor("ds", [D], F32, kind="ExternalInput").ap()
    out_h = nc.dram_tensor("out", [T, D], F32, kind="ExternalOutput").ap()
    if dbg:
        d_xn = nc.dram_tensor("d_xn", [T, D], BF16, kind="ExternalOutput").ap()
        d_wq = nc.dram_tensor("d_wq", [D, D], BF16, kind="ExternalOutput").ap()
        d_wv = nc.dram_tensor("d_wv", [512, D], BF16, kind="ExternalOutput").ap()
        d_g0 = nc.dram_tensor("d_g0", [1024, D], BF16, kind="ExternalOutput").ap()
        d_q = nc.dram_tensor("d_q", [128, 16, T], BF16, kind="ExternalOutput").ap()
        d_kd = nc.dram_tensor("d_kd", [128, 8, T], BF16, kind="ExternalOutput").ap()
        d_v = nc.dram_tensor("d_v", [128, 16, 520], BF16, kind="ExternalOutput").ap()
        d_afm = nc.dram_tensor("d_afm", [128, 16, 512], BF16, kind="ExternalOutput").ap()
        d_at = nc.dram_tensor("d_at", [D, T], BF16, kind="ExternalOutput").ap()
        d_h = nc.dram_tensor("d_h", [128, 64, 512], BF16, kind="ExternalOutput").ap()
    scr_wq = nc.dram_tensor("scr_wq", [D, D], BF16).ap()
    scr_wk = nc.dram_tensor("scr_wk", [512, D], BF16).ap()
    scr_wv = nc.dram_tensor("scr_wv", [512, D], BF16).ap()
    scr_wo = nc.dram_tensor("scr_wo", [D, D], BF16).ap()
    scr_g = nc.dram_tensor("scr_g", [I, D], BF16).ap()
    scr_u = nc.dram_tensor("scr_u", [I, D], BF16).ap()
    scr_d = nc.dram_tensor("scr_d", [D, I], BF16).ap()
    scr_xn = nc.dram_tensor("scr_xn", [T, D], BF16).ap()
    scr_at = nc.dram_tensor("scr_at", [D, T], BF16).ap()
    scr_ffn = nc.dram_tensor("scr_ffn", [D, T], BF16).ap()
    COS_c, SINS_c, PNB_c, PCMB_c, DLO_c, DHI_c, MASKS_c = _consts()
    cos_d = nc.inline_tensor(COS_c, name="cos_c").ap()
    sin_d = nc.inline_tensor(SINS_c, name="sin_c").ap()
    pnb_d = nc.inline_tensor(PNB_c, name="pnb_c").ap()
    pcmb_d = nc.inline_tensor(PCMB_c, name="pcmb_c").ap()
    dlo_d = nc.inline_tensor(DLO_c, name="dlo_c").ap()
    dhi_d = nc.inline_tensor(DHI_c, name="dhi_c").ap()
    msk_d = [nc.inline_tensor(MASKS_c[v], name=f"msk{v}_c").ap() for v in range(4)]

    es = ExitStack()

    class S:
        def __init__(self, name):
            self.h = es.enter_context(nc.semaphore(name))
            self.v = 0
    sl = S("sl")
    ss = S("ss")
    sg = S("sg")
    spe = S("spe")
    sac = S("sac")
    sdv = S("sdv")
    spl = S("spl")
    sout = S("sout")

    QSEM = {"sync": sl, "scalar": ss, "gpsimd": sg}
    ESEM = {"tensor": spe, "scalar": sac, "vector": sdv, "gpsimd": spl}

    ENGS = ["sync", "scalar", "vector", "tensor", "gpsimd"]
    queues = {e: [] for e in ENGS}

    def flush_block():
        with nc.Block() as blk:
            for name in ENGS:
                fns = queues[name]
                if fns:
                    getattr(blk, name)(lambda e, fns=fns: [f(e) for f in fns])
        for name in ENGS:
            queues[name] = []

    def emit(eng, f):
        queues[eng].append(f)

    def wait(eng, sem, v=None):
        vv = sem.v if v is None else v
        if vv > 0:
            emit(eng, lambda e, s=sem.h, v=vv: e.wait_ge(s, v))

    def dma(eng, out, in_, **kw):
        sem = QSEM[eng]
        sem.v += 16
        emit(eng, lambda e, o=out, i=in_, k=kw, s=sem.h: e.dma_start(out=o, in_=i, **k).then_inc(s, 16))
        return sem.v

    def inc(eng, call):
        sem = ESEM[eng]
        sem.v += 1
        emit(eng, lambda e, c=call, s=sem.h: c(e).then_inc(s, 1))
        return sem.v

    def op(eng, call):
        emit(eng, call)

    def sb_in(stack, name, shape, dt):
        return stack.enter_context(nc.sbuf_tensor(name, shape, dt))

    COS = sb_in(es, "COS", [128, T], BF16)
    SINS = sb_in(es, "SINS", [128, T], BF16)
    PNB = sb_in(es, "PNB", [128, 128], BF16)
    PCMB = sb_in(es, "PCMB", [128, 128], BF16)
    DLO = sb_in(es, "DLO", [128, 128], BF16)
    DHI = sb_in(es, "DHI", [128, 128], BF16)
    MSK = [sb_in(es, f"MSK{v}", [128, 1024], BF16) for v in range(4)]
    gs_sb = sb_in(es, "gs_sb", [128, 64], F32)
    us_sb = sb_in(es, "us_sb", [128, 64], F32)
    ds_sb = sb_in(es, "ds_sb", [128, 16], F32)
    ones64 = sb_in(es, "ones64", [1, 64], BF16)
    epsv = sb_in(es, "epsv", [128, 1], F32)
    n128v = sb_in(es, "n128v", [128, 1], F32)

    es0 = ExitStack()
    stg32 = [sb_in(es0, f"stg32_{i}", [128, I], I32) for i in range(2)]
    stg16 = [sb_in(es0, f"stg16_{i}", [128, I], BF16) for i in range(2)]
    st32_rd = [0, 0]
    st16_wr = [0, 0]

    op("gpsimd", lambda e: e.memset(ones64[:], 1.0))
    op("gpsimd", lambda e: e.memset(epsv[:], 1e-5))
    op("gpsimd", lambda e: e.memset(n128v[:], -128.0))
    dma("sync", COS[:], cos_d)
    dma("sync", SINS[:], sin_d)
    dma("sync", PNB[:], pnb_d)
    dma("sync", PCMB[:], pcmb_d)
    dma("sync", DLO[:], dlo_d)
    dma("sync", DHI[:], dhi_d)
    for v in range(4):
        dma("sync", MSK[v][:], msk_d[v])
    for c in range(64):
        dma("sync", gs_sb[:, c:c + 1], gs_h[c * 128:(c + 1) * 128][:, None])
        dma("sync", us_sb[:, c:c + 1], us_h[c * 128:(c + 1) * 128][:, None])
    for c in range(16):
        dma("sync", ds_sb[:, c:c + 1], ds_h[c * 128:(c + 1) * 128][:, None])

    ring_i = [0]

    def prep_chunk(srcs, width, scr_dst, sub128, src_dtype):
        i = ring_i[0] % 2
        ring_i[0] += 1
        s32 = stg32[i].ap().bitcast(src_dtype)[:, 0:width] if src_dtype != I32 else stg32[i][:, 0:width]
        s16 = stg16[i]
        wait("gpsimd", sac, st32_rd[i])
        for (dst_sl, src_ap) in srcs:
            dma("gpsimd", s32[dst_sl[0]:dst_sl[1], :], src_ap)
        wait("scalar", sg)
        wait("scalar", sl, st16_wr[i])
        if sub128:
            st32_rd[i] = inc("scalar", lambda e, s32=s32, s16=s16, w=width: e.activation(
                out=s16[:, 0:w], in_=s32, func=AF.Identity, bias=n128v[:]))
        else:
            st32_rd[i] = inc("scalar", lambda e, s32=s32, s16=s16, w=width: e.activation(
                out=s16[:, 0:w], in_=s32, func=AF.Copy))
        wait("sync", sac)
        st16_wr[i] = dma("sync", scr_dst, s16[:, 0:width])

    for c in range(16):
        prep_chunk([((0, 128), wq_h[c * 128:(c + 1) * 128, :])], D,
                   scr_wq[c * 128:(c + 1) * 128, :], False, F32)
    for c in range(4):
        prep_chunk([((0, 128), wk_h[c * 128:(c + 1) * 128, :])], D,
                   scr_wk[c * 128:(c + 1) * 128, :], False, F32)
    for c in range(4):
        prep_chunk([((0, 128), wv_h[c * 128:(c + 1) * 128, :])], D,
                   scr_wv[c * 128:(c + 1) * 128, :], False, F32)
    for c in range(16):
        prep_chunk([((0, 128), wo_h[c * 128:(c + 1) * 128, :])], D,
                   scr_wo[c * 128:(c + 1) * 128, :], False, F32)
    for c in range(64):
        prep_chunk([((0, 128), gw_h[c * 128:(c + 1) * 128, :])], D,
                   scr_g[c * 128:(c + 1) * 128, :], True, I32)
    for c in range(64):
        prep_chunk([((0, 128), uw_h[c * 128:(c + 1) * 128, :])], D,
                   scr_u[c * 128:(c + 1) * 128, :], True, I32)
    for c in range(16):
        prep_chunk([((0, 128), dw_h[c * 128:(c + 1) * 128, :])], I,
                   scr_d[c * 128:(c + 1) * 128, :], True, I32)
    flush_block()
    es0.close()

    es1 = ExitStack()
    q_sb = sb_in(es1, "q_sb", [128, 16, T], BF16)
    kd_sb = sb_in(es1, "kd_sb", [128, 8, T], BF16)
    v_sb = sb_in(es1, "v_sb", [128, 16, 520], BF16)

    es1b = ExitStack()
    xt = [sb_in(es1b, f"xt{i}", [128, D], F32) for i in range(2)]
    xn_tm = sb_in(es1b, "xn_tm", [128, D], BF16)
    xn_fm = sb_in(es1b, "xn_fm", [128, KC, TB], BF16)
    sumv = sb_in(es1b, "sumv", [128, 1], F32)
    sqv = sb_in(es1b, "sqv", [128, 1], F32)
    mv = sb_in(es1b, "mv", [128, 1], F32)
    ex2v = sb_in(es1b, "ex2v", [128, 1], F32)
    varv = sb_in(es1b, "varv", [128, 1], F32)
    rstdv = sb_in(es1b, "rstdv", [128, 1], F32)
    nmrv = sb_in(es1b, "nmrv", [128, 1], F32)
    wbuf = [sb_in(es1b, f"wbuf{i}", [128, 16, 512], BF16) for i in range(2)]
    qtmp = [sb_in(es1b, f"qtmp{i}", [128, 512], BF16) for i in range(2)]
    t1b = sb_in(es1b, "t1b", [128, 512], BF16)
    t2b = sb_in(es1b, "t2b", [128, 512], BF16)
    krope = sb_in(es1b, "krope", [128, 4, 512], BF16)

    pes = ExitStack()
    pp = [pes.enter_context(nc.psum_tensor(f"pp{i}", [128, 512], F32)) for i in range(2)]
    pa_ps = pes.enter_context(nc.psum_tensor("pa_ps", [128, 512], F32))
    pb_ps = pes.enter_context(nc.psum_tensor("pb_ps", [128, 512], F32))

    for tt in range(16):
        for j in range(8):
            op("vector", lambda e, tt=tt, j=j: e.memset(v_sb[:, tt, j * 65 + 64:j * 65 + 65], 1.0))

    xt_rd = [0, 0]
    xntm_rd = [0]
    sqv_rd = [0]
    qtmp_st = [(0, 0), (0, 0)]
    pp_rd = [0, 0]
    pa_rd = [0]
    pb_rd = [0]
    rope_flip = [0]

    def ln_tile(b, tau):
        t0 = (4 * b + tau) * 128
        xe = xt[tau % 2]
        wait("sync", sac, xt_rd[tau % 2])
        lv = dma("sync", xe[:], x_h[t0:t0 + 128, :])
        wait("vector", sl, lv)
        sdv_red = inc("vector", lambda e, xe=xe: e.reduce_sum(sumv[:], xe[:], axis=mybir.AxisListType.X))
        wait("scalar", sl, lv)
        wait("scalar", sdv, sqv_rd[0])
        wait("scalar", ss, xntm_rd[0])
        sac_sq = inc("scalar", lambda e, xe=xe: e.activation(
            out=xn_tm[:], in_=xe[:], func=AF.Square, accum_out=sqv[:]))
        wait("vector", sdv, sdv_red)
        sdv_mv = inc("vector", lambda e: e.tensor_scalar_mul(out=mv[:], in0=sumv[:], scalar1=1.0 / D))
        wait("vector", sac, sac_sq)
        sqv_rd[0] = inc("vector", lambda e: e.tensor_scalar_mul(out=ex2v[:], in0=sqv[:], scalar1=1.0 / D))
        wait("vector", sdv, sdv_mv)
        sdv_vm = inc("vector", lambda e: e.tensor_tensor(out=varv[:], in0=mv[:], in1=mv[:], op=ALU.mult))
        wait("vector", sdv, sdv_vm)
        sdv_var = inc("vector", lambda e: e.tensor_tensor(out=varv[:], in0=ex2v[:], in1=varv[:], op=ALU.subtract))
        wait("scalar", sdv, sdv_var)
        sac_ln1 = inc("scalar", lambda e: e.activation(out=rstdv[:], in_=varv[:], func=AF.Ln, bias=epsv[:]))
        wait("scalar", sac, sac_ln1)
        sac_r = inc("scalar", lambda e: e.activation(out=rstdv[:], in_=rstdv[:], func=AF.Exp, scale=-0.5))
        wait("vector", sac, sac_r)
        sdv_nmr = inc("vector", lambda e: e.tensor_tensor(out=nmrv[:], in0=mv[:], in1=rstdv[:], op=ALU.mult))
        wait("vector", sdv, sdv_nmr)
        sdv_nmr2 = inc("vector", lambda e: e.tensor_scalar_mul(out=nmrv[:], in0=nmrv[:], scalar1=-1.0))
        wait("scalar", sdv, sdv_nmr2)
        sac_xn = inc("scalar", lambda e, xe=xe: e.activation(
            out=xn_tm[:], in_=xe[:], func=AF.Identity, scale=rstdv[:], bias=nmrv[:]))
        xt_rd[tau % 2] = sac_xn
        wait("scalar", sac, sac_xn)
        dma("scalar", scr_xn[t0:t0 + 128, :], xn_tm[:])
        xntm_rd[0] = ss.v

    def rope_chunk(b, ps_src, ps_idx, dst_ap):
        fl = rope_flip[0]
        rope_flip[0] ^= 1
        qt = qtmp[fl]
        (g_spe, g_sdv) = qtmp_st[fl]
        wait("scalar", spe, g_spe)
        wait("scalar", sdv, g_sdv)
        sac_ev = inc("scalar", lambda e, qt=qt, p=ps_src: e.activation(out=qt[:], in_=p[:], func=AF.Copy))
        if ps_idx is not None:
            pp_rd[ps_idx] = sac_ev
        wait("tensor", sac, sac_ev)
        wait("tensor", sdv, pa_rd[0])
        spe_a = inc("tensor", lambda e, qt=qt: e.matmul(out=pa_ps[:], lhsT=PNB[:], rhs=qt[:],
                                                        start=True, stop=True))
        wait("tensor", sdv, pb_rd[0])
        spe_b = inc("tensor", lambda e, qt=qt: e.matmul(out=pb_ps[:], lhsT=PCMB[:], rhs=qt[:],
                                                        start=True, stop=True))
        bsl = slice(b * TB, (b + 1) * TB)
        wait("vector", spe, spe_a)
        sdv_t1 = inc("vector", lambda e, bsl=bsl: e.tensor_tensor(
            out=t1b[:], in0=pa_ps[:], in1=COS[:, bsl], op=ALU.mult))
        pa_rd[0] = sdv_t1
        wait("vector", spe, spe_b)
        sdv_t2 = inc("vector", lambda e, bsl=bsl: e.tensor_tensor(
            out=t2b[:], in0=pb_ps[:], in1=SINS[:, bsl], op=ALU.mult))
        pb_rd[0] = sdv_t2
        wait("vector", sdv, sdv_t2)
        sdv_o = inc("vector", lambda e, d=dst_ap: e.tensor_tensor(out=d, in0=t1b[:], in1=t2b[:], op=ALU.add))
        qtmp_st[fl] = (spe_a if False else spe_b, sdv_t1)
        return sac_ev, sdv_o

    wbuf_rd = [0, 0]

    for b in range(NTB):
        bsl = slice(b * TB, (b + 1) * TB)
        for tau in range(4):
            ln_tile(b, tau)
        wait("sync", ss, ss.v)
        for c in range(KC):
            dma("sync", xn_fm[:, c, :], scr_xn[bsl, c * 128:(c + 1) * 128], transpose=True)
        xn_ready = sl.v
        for s in range(4):
            wb = wbuf[s % 2]
            wait("sync", spe, wbuf_rd[s % 2])
            for k in range(KC):
                dma("sync", wb[:, k, :], scr_wq[s * 512:(s + 1) * 512, k * 128:(k + 1) * 128], transpose=True)
            ld_done = sl.v
            for mj in range(4):
                mu = s * 4 + mj
                pe = pp[mu % 2]
                wait("tensor", sl, ld_done)
                wait("tensor", sl, xn_ready)
                wait("tensor", sac, pp_rd[mu % 2])
                last = 0
                for k in range(KC):
                    last = inc("tensor", lambda e, k=k, pe=pe, wb=wb, mj=mj: e.matmul(
                        out=pe[:], lhsT=wb[:, k, mj * 128:(mj + 1) * 128], rhs=xn_fm[:, k, :],
                        start=(k == 0), stop=(k == KC - 1)))
                wbuf_rd[s % 2] = last
                wait("scalar", spe, last)
                rope_chunk(b, pe, mu % 2, q_sb[:, mu, bsl])
        wbk = wbuf[0]
        wait("sync", spe, wbuf_rd[0])
        for k in range(KC):
            dma("sync", wbk[:, k, 0:512], scr_wk[0:512, k * 128:(k + 1) * 128], transpose=True)
        ld_done = sl.v
        for mu in range(4):
            pe = pp[mu % 2]
            wait("tensor", sl, ld_done)
            wait("tensor", sl, xn_ready)
            wait("tensor", sac, pp_rd[mu % 2])
            last = 0
            for k in range(KC):
                last = inc("tensor", lambda e, k=k, pe=pe, mu=mu: e.matmul(
                    out=pe[:], lhsT=wbk[:, k, mu * 128:(mu + 1) * 128], rhs=xn_fm[:, k, :],
                    start=(k == 0), stop=(k == KC - 1)))
            wbuf_rd[0] = last
            wait("scalar", spe, last)
            _, sdv_o = rope_chunk(b, pe, mu % 2, krope[:, mu, :])
            wait("tensor", sdv, sdv_o)
            for half, PM in ((0, DLO), (1, DHI)):
                wait("tensor", sdv, pa_rd[0])
                spe_d = inc("tensor", lambda e, PM=PM, mu=mu: e.matmul(
                    out=pa_ps[:], lhsT=PM[:], rhs=krope[:, mu, :], start=True, stop=True))
                wait("scalar", spe, spe_d)
                sac_e = inc("scalar", lambda e, mu=mu, half=half, bsl=bsl: e.activation(
                    out=kd_sb[:, 2 * mu + half, bsl], in_=pa_ps[:], func=AF.Copy))
                pa_rd[0] = 0
                wait("tensor", sac, sac_e)
        wbv = wbuf[1]
        wait("sync", spe, wbuf_rd[1])
        for k in range(KC):
            dma("sync", wbv[:, k, 0:512], scr_wv[0:512, k * 128:(k + 1) * 128], transpose=True)
        ld_done = sl.v
        for tau in range(4):
            pe = pp[tau % 2]
            wait("tensor", sl, ld_done)
            wait("tensor", sl, xn_ready)
            wait("tensor", sac, pp_rd[tau % 2])
            last = 0
            for k in range(KC):
                last = inc("tensor", lambda e, k=k, pe=pe, tau=tau, wbv=wbv: e.matmul(
                    out=pe[:], lhsT=xn_fm[:, k, tau * 128:(tau + 1) * 128], rhs=wbv[:, k, 0:512],
                    start=(k == 0), stop=(k == KC - 1)))
            wbuf_rd[1] = last
            wait("vector", spe, last)
            vout = v_sb[:, 4 * b + tau, 0:520].rearrange("p (h x) -> p h x", h=8)[:, :, 0:64]
            vin = pe[:].rearrange("p (h x) -> p h x", h=8)
            sdv_v = inc("vector", lambda e, vout=vout, vin=vin: e.tensor_copy(out=vout, in_=vin))
            wait("tensor", sdv, sdv_v)
            pp_rd[tau % 2] = 0
        flush_block()

    es1b.close()
    pes.close()

    es2 = ExitStack()
    wob = [sb_in(es2, f"wob{i}", [128, 16, 512], BF16) for i in range(2)]
    pbuf = [sb_in(es2, f"pbuf{i}", [128, 1024], BF16) for i in range(2)]
    afm = sb_in(es2, "afm", [128, 16, 512], BF16)
    lnv2 = sb_in(es2, "lnv2", [1, 512], F32)
    rbf = sb_in(es2, "rbf", [1, 512], BF16)
    rb64 = sb_in(es2, "rb64", [64, 512], BF16)
    tmp64 = sb_in(es2, "tmp64", [64, 512], BF16)
    woev = [sb_in(es2, f"woev{i}", [128, 512], BF16) for i in range(2)]

    pes2 = ExitStack()
    psc = [pes2.enter_context(nc.psum_tensor(f"psc{i}", [128, 1024], F32)) for i in range(2)]
    pav = [pes2.enter_context(nc.psum_tensor(f"pav{i}", [128, 512], F32)) for i in range(4)]

    norm_state = {"rbf_rd": 0, "rb64_rd": 0}

    for B in range(NTB):
        if dbg and B == 0:
            sout.v += 16; emit("gpsimd", lambda e, s=sout.h: e.dma_start(out=d_xn, in_=scr_xn).then_inc(s, 16))
            sout.v += 16; emit("gpsimd", lambda e, s=sout.h: e.dma_start(out=d_wq, in_=scr_wq).then_inc(s, 16))
            sout.v += 16; emit("gpsimd", lambda e, s=sout.h: e.dma_start(out=d_wv, in_=scr_wv).then_inc(s, 16))
            sout.v += 16; emit("gpsimd", lambda e, s=sout.h: e.dma_start(out=d_g0, in_=scr_g[0:1024, :]).then_inc(s, 16))
            sout.v += 16; emit("gpsimd", lambda e, s=sout.h: e.dma_start(out=d_q, in_=q_sb[:]).then_inc(s, 16))
            sout.v += 16; emit("gpsimd", lambda e, s=sout.h: e.dma_start(out=d_kd, in_=kd_sb[:]).then_inc(s, 16))
            sout.v += 16; emit("gpsimd", lambda e, s=sout.h: e.dma_start(out=d_v, in_=v_sb[:]).then_inc(s, 16))
        Bsl = slice(B * TB, (B + 1) * TB)
        nck = 4 * B + 4
        for s in range(2):
            for dv in range(16):
                dma("sync", wob[s][:, dv, :],
                    scr_wo[s * 512:(s + 1) * 512, dv * 128:(dv + 1) * 128], transpose=True)
        wo_ld01 = sl.v

        exp_read = {}
        av_tt = {}
        av_last = {}
        norm_q = []

        def do_norm(e_pair, pidx):
            for hh in range(2):
                pa = pav[pidx + hh]
                sac_ln = inc("scalar", lambda e, pa=pa: e.activation(
                    out=lnv2[:], in_=pa[64:65, :], func=AF.Ln))
                wait("scalar", spe, norm_state["rbf_rd"])
                wait("scalar", sac, sac_ln)
                sac_r = inc("scalar", lambda e: e.activation(
                    out=rbf[:], in_=lnv2[:], func=AF.Exp, scale=-1.0))
                wait("tensor", sac, sac_r)
                spe_b = inc("tensor", lambda e, pa=pa: e.matmul(
                    out=pa[64:128, :], lhsT=ones64[:], rhs=rbf[:], start=True, stop=True))
                norm_state["rbf_rd"] = spe_b
                wait("scalar", spe, spe_b)
                wait("scalar", sdv, norm_state["rb64_rd"])
                sac_rb = inc("scalar", lambda e, pa=pa: e.activation(
                    out=rb64[:], in_=pa[64:128, :], func=AF.Copy))
                wait("vector", sac, sac_rb)
                if hh == 0:
                    sdv_tt = inc("vector", lambda e, pa=pa, ep=e_pair: e.tensor_tensor(
                        out=afm[0:64, ep, :], in0=pa[0:64, :], in1=rb64[:], op=ALU.mult))
                else:
                    sdv_t64 = inc("vector", lambda e, pa=pa: e.tensor_tensor(
                        out=tmp64[:], in0=pa[0:64, :], in1=rb64[:], op=ALU.mult))
                    wait("vector", sdv, sdv_t64)
                    sdv_tt = inc("vector", lambda e, ep=e_pair: e.tensor_scalar_mul(
                        out=afm[64:128, ep, :], in0=tmp64[:], scalar1=1.0))
                norm_state["rb64_rd"] = sdv_tt
                av_tt[pidx + hh] = sdv_tt

        for e_pair in range(NPAIR):
            j = e_pair // 2
            pidx = (e_pair % 2) * 2
            for hh in range(2):
                if (pidx + hh) in av_tt:
                    wait("tensor", sdv, av_tt[pidx + hh])
            for c in range(nck):
                ps = psc[c % 2]
                if (c % 2) in exp_read:
                    wait("tensor", sac, exp_read[c % 2])
                inc("tensor", lambda e, ps=ps, j=j, c=c, ep=e_pair, Bsl=Bsl: e.matmul(
                    out=ps[:, 0:512], lhsT=kd_sb[0:64, j, c * 128:(c + 1) * 128],
                    rhs=q_sb[0:64, ep, Bsl], start=True, stop=True))
                spe_s = inc("tensor", lambda e, ps=ps, j=j, c=c, ep=e_pair, Bsl=Bsl: e.matmul(
                    out=ps[:, 512:1024], lhsT=kd_sb[64:128, j, c * 128:(c + 1) * 128],
                    rhs=q_sb[64:128, ep, Bsl], start=True, stop=True))
                wait("scalar", spe, spe_s)
                if (c % 2) in av_last:
                    wait("scalar", spe, av_last[c % 2])
                sac_e = inc("scalar", lambda e, ps=ps, c=c: e.activation(
                    out=pbuf[c % 2][:], in_=ps[:], func=AF.Exp, scale=0.125))
                exp_read[c % 2] = sac_e
                dep_sem, dep_v = sac, sac_e
                if c >= 4 * B:
                    wait("gpsimd", sac, sac_e)
                    spl_m = inc("gpsimd", lambda e, c=c, B=B: e.tensor_tensor(
                        out=pbuf[c % 2][:], in0=pbuf[c % 2][:], in1=MSK[c - 4 * B][:], op=ALU.mult))
                    dep_sem, dep_v = spl, spl_m
                wait("tensor", dep_sem, dep_v)
                inc("tensor", lambda e, c=c, j=j, pidx=pidx, nck=nck: e.matmul(
                    out=pav[pidx][0:65, :], lhsT=v_sb[:, c, j * 65:(j + 1) * 65],
                    rhs=pbuf[c % 2][:, 0:512], start=(c == 0), stop=(c == nck - 1)))
                spe_av = inc("tensor", lambda e, c=c, j=j, pidx=pidx, nck=nck: e.matmul(
                    out=pav[pidx + 1][0:65, :], lhsT=v_sb[:, c, j * 65:(j + 1) * 65],
                    rhs=pbuf[c % 2][:, 512:1024], start=(c == 0), stop=(c == nck - 1)))
                av_last[c % 2] = spe_av
            if norm_q:
                norm_q.pop(0)()
            norm_q.append(lambda ep=e_pair, pidx=pidx: do_norm(ep, pidx))
        while norm_q:
            norm_q.pop(0)()
        attn_done = sdv.v
        woev_rd = [0, 0]
        wob_rd = [0, 0]
        for s in range(4):
            wot = wob[s % 2]
            if s >= 2:
                wait("sync", spe, wob_rd[s % 2])
                for dv in range(16):
                    dma("sync", wot[:, dv, :],
                        scr_wo[s * 512:(s + 1) * 512, dv * 128:(dv + 1) * 128], transpose=True)
                ld_done = sl.v
            else:
                ld_done = wo_ld01
            for mj in range(4):
                mu = s * 4 + mj
                pw = psc[mu % 2]
                wait("tensor", sl, ld_done)
                wait("tensor", sdv, attn_done)
                if (mu % 2) in exp_read:
                    wait("tensor", sac, exp_read.pop(mu % 2))
                last = 0
                for dv in range(16):
                    last = inc("tensor", lambda e, dv=dv, pw=pw, wot=wot, mj=mj: e.matmul(
                        out=pw[:, 0:512], lhsT=wot[:, dv, mj * 128:(mj + 1) * 128], rhs=afm[:, dv, :],
                        start=(dv == 0), stop=(dv == 15)))
                wob_rd[s % 2] = last
                wait("scalar", spe, last)
                wait("scalar", ss, woev_rd[mu % 2])
                sac_ev = inc("scalar", lambda e, pw=pw, mu=mu: e.activation(
                    out=woev[mu % 2][:], in_=pw[:, 0:512], func=AF.Copy))
                exp_read[mu % 2] = sac_ev
                wait("scalar", sac, sac_ev)
                dma("scalar", scr_at[mu * 128:(mu + 1) * 128, Bsl], woev[mu % 2][:])
                woev_rd[mu % 2] = ss.v
        if dbg and B == NTB - 1:
            wait("gpsimd", sdv, attn_done)
            sout.v += 16; emit("gpsimd", lambda e, s=sout.h: e.dma_start(out=d_afm, in_=afm[:]).then_inc(s, 16))
        flush_block()

    pes2.close()
    es2.close()
    es1.close()

    es3 = ExitStack()
    h_sb = sb_in(es3, "h_sb", [128, 64, 512], BF16)
    xnf = sb_in(es3, "xnf", [128, KC, 512], BF16)
    gwide = [sb_in(es3, f"gwide{i}", [128, KC, 256], BF16) for i in range(2)]
    uwide = [sb_in(es3, f"uwide{i}", [128, KC, 256], BF16) for i in range(2)]
    db = [sb_in(es3, f"db{i}", [128, 512], BF16) for i in range(4)]
    sgt = [sb_in(es3, f"sgt{i}", [128, 512], BF16) for i in range(2)]
    sut = [sb_in(es3, f"sut{i}", [128, 512], BF16) for i in range(2)]
    fdev = [sb_in(es3, f"fdev{i}", [128, 512], BF16) for i in range(2)]
    xt3 = [sb_in(es3, f"xt3{i}", [128, D], F32) for i in range(2)]
    at3 = [sb_in(es3, f"at3{i}", [128, D], BF16) for i in range(2)]
    ft3 = [sb_in(es3, f"ft3{i}", [128, D], BF16) for i in range(2)]
    out3 = [sb_in(es3, f"out3{i}", [128, D], F32) for i in range(2)]

    pes3 = ExitStack()
    pg = [pes3.enter_context(nc.psum_tensor(f"pg{i}", [128, 512], F32)) for i in range(4)]
    pu = [pes3.enter_context(nc.psum_tensor(f"pu{i}", [128, 512], F32)) for i in range(4)]

    gu_mm = [0, 0]
    gu_ev_sac = [0, 0]
    gu_ev_sdv = [0, 0]
    dn_mm = {}
    dn_ev = {}
    out3_rd = [0, 0]
    xt3_rd = [0, 0]
    at3_rd = [0, 0]
    fdev_rd = [0, 0]

    for b in range(NTB):
        bsl = slice(b * TB, (b + 1) * TB)
        for c in range(KC):
            dma("sync", xnf[:, c, :], scr_xn[bsl, c * 128:(c + 1) * 128], transpose=True)
        xn_ld = sl.v
        for sp in range(32):
            set_ = sp % 2
            gw_t, uw_t = gwide[set_], uwide[set_]
            wait("sync", spe, gu_mm[set_])
            for k in range(KC):
                dma("sync", gw_t[:, k, :],
                    scr_g[sp * 256:(sp + 1) * 256, k * 128:(k + 1) * 128], transpose=True)
                dma("sync", uw_t[:, k, :],
                    scr_u[sp * 256:(sp + 1) * 256, k * 128:(k + 1) * 128], transpose=True)
            ld_done = sl.v
            pga, pgb_ = pg[2 * set_], pg[2 * set_ + 1]
            pua, pub = pu[2 * set_], pu[2 * set_ + 1]
            wait("tensor", sl, ld_done)
            wait("tensor", sl, xn_ld)
            wait("tensor", sac, gu_ev_sac[set_])
            wait("tensor", sdv, gu_ev_sdv[set_])
            last = 0
            for k in range(KC):
                inc("tensor", lambda e, k=k, gw_t=gw_t, pga=pga: e.matmul(
                    out=pga[:], lhsT=gw_t[:, k, 0:128], rhs=xnf[:, k, :],
                    start=(k == 0), stop=(k == KC - 1)))
                inc("tensor", lambda e, k=k, gw_t=gw_t, pgb_=pgb_: e.matmul(
                    out=pgb_[:], lhsT=gw_t[:, k, 128:256], rhs=xnf[:, k, :],
                    start=(k == 0), stop=(k == KC - 1)))
                inc("tensor", lambda e, k=k, uw_t=uw_t, pua=pua: e.matmul(
                    out=pua[:], lhsT=uw_t[:, k, 0:128], rhs=xnf[:, k, :],
                    start=(k == 0), stop=(k == KC - 1)))
                last = inc("tensor", lambda e, k=k, uw_t=uw_t, pub=pub: e.matmul(
                    out=pub[:], lhsT=uw_t[:, k, 128:256], rhs=xnf[:, k, :],
                    start=(k == 0), stop=(k == KC - 1)))
            gu_mm[set_] = last
            wait("scalar", spe, last)
            wait("vector", spe, last)
            e_sac = e_sdv = 0
            for d in range(2):
                ci = sp * 2 + d
                pgx = pga if d == 0 else pgb_
                pux = pua if d == 0 else pub
                e_sac = inc("scalar", lambda e, pgx=pgx, ci=ci, d=d: e.activation(
                    out=sgt[d][:], in_=pgx[:], func=AF.Silu, scale=gs_sb[:, ci:ci + 1]))
                e_sdv0 = inc("vector", lambda e, pux=pux, ci=ci, d=d: e.tensor_single_scalar(
                    out=sut[d][:], in_=pux[:], scalar=us_sb[:, ci:ci + 1], op=ALU.mult))
                wait("vector", sac, e_sac)
                wait("vector", sdv, e_sdv0)
                e_sdv = inc("vector", lambda e, ci=ci, d=d: e.tensor_tensor(
                    out=h_sb[:, ci, :], in0=sgt[d][:], in1=sut[d][:], op=ALU.mult))
            gu_ev_sac[set_] = e_sac
            gu_ev_sdv[set_] = e_sdv0 if False else e_sdv
        h_done = sdv.v
        gu_done_sac = sac.v
        for ms in range(4):
            last = 0
            for ic in range(64):
                dbuf = db[ic % 4]
                if (ic % 4) in dn_mm:
                    wait("sync", spe, dn_mm[ic % 4])
                dma("sync", dbuf[:],
                    scr_d[ms * 512:(ms + 1) * 512, ic * 128:(ic + 1) * 128], transpose=True)
                ld_done = sl.v
                wait("tensor", sl, ld_done)
                if ic == 0:
                    wait("tensor", sdv, h_done)
                    wait("tensor", sac, gu_done_sac)
                    for j in range(4):
                        if j in dn_ev:
                            wait("tensor", sac, dn_ev[j])
                for j in range(4):
                    last = inc("tensor", lambda e, j=j, dbuf=dbuf, ic=ic: e.matmul(
                        out=pg[j][:], lhsT=dbuf[:, j * 128:(j + 1) * 128], rhs=h_sb[:, ic, :],
                        start=(ic == 0), stop=(ic == 63)))
                dn_mm[ic % 4] = last
            wait("scalar", spe, last)
            for j in range(4):
                mu = ms * 4 + j
                wait("scalar", ss, fdev_rd[j % 2])
                sac_d = inc("scalar", lambda e, j=j, mu=mu: e.activation(
                    out=fdev[j % 2][:], in_=pg[j][:], func=AF.Identity, scale=ds_sb[:, mu:mu + 1]))
                dn_ev[j] = sac_d
                wait("scalar", sac, sac_d)
                dma("scalar", scr_ffn[mu * 128:(mu + 1) * 128, bsl], fdev[j % 2][:])
                fdev_rd[j % 2] = ss.v
        fst_done = ss.v
        wait("sync", ss, fst_done)
        for tau in range(4):
            t0 = (4 * b + tau) * 128
            wait("sync", sdv, xt3_rd[tau % 2])
            lx = dma("sync", xt3[tau % 2][:], x_h[t0:t0 + 128, :])
            wait("sync", sdv, at3_rd[tau % 2])
            la = dma("sync", at3[tau % 2][:], scr_at[:, t0:t0 + 128], transpose=True)
            lf = dma("sync", ft3[tau % 2][:], scr_ffn[:, t0:t0 + 128], transpose=True)
            wait("vector", sl, lf)
            wait("vector", sout, out3_rd[tau % 2])
            sdv_a = inc("vector", lambda e, tau=tau: e.tensor_tensor(
                out=out3[tau % 2][:], in0=xt3[tau % 2][:], in1=at3[tau % 2][:], op=ALU.add))
            xt3_rd[tau % 2] = sdv_a
            at3_rd[tau % 2] = sdv_a
            wait("vector", sdv, sdv_a)
            sdv_o = inc("vector", lambda e, tau=tau: e.tensor_tensor(
                out=out3[tau % 2][:], in0=out3[tau % 2][:], in1=ft3[tau % 2][:], op=ALU.add))
            wait("gpsimd", sdv, sdv_o)
            sout.v += 16
            emit("gpsimd", lambda e, t0=t0, tau=tau, s=sout.h: e.dma_start(
                out=out_h[t0:t0 + 128, :], in_=out3[tau % 2][:]).then_inc(s, 16))
            out3_rd[tau % 2] = sout.v
        flush_block()

    if dbg:
        sout.v += 16; emit("gpsimd", lambda e, s=sout.h: e.dma_start(out=d_at, in_=scr_at).then_inc(s, 16))
        sout.v += 16; emit("gpsimd", lambda e, s=sout.h: e.dma_start(out=d_h, in_=h_sb[:]).then_inc(s, 16))
    wait("gpsimd", sout, sout.v)
    flush_block()

    es3.close()
    pes3.close()
    es.close()
    return nc


_NC_CACHE = None


def kernel(L0_attn_x, L0_attn_wq, L0_attn_wk, L0_attn_wv, L0_attn_wo,
           L0_ffn_gate_w, L0_ffn_gate_scale, L0_ffn_up_w, L0_ffn_up_scale,
           L0_ffn_down_w, L0_ffn_down_scale):
    global _NC_CACHE
    x = np.ascontiguousarray(np.asarray(L0_attn_x, dtype=np.float32))
    wq = np.ascontiguousarray(np.asarray(L0_attn_wq, dtype=np.float32))
    wk = np.ascontiguousarray(np.asarray(L0_attn_wk, dtype=np.float32))
    wv = np.ascontiguousarray(np.asarray(L0_attn_wv, dtype=np.float32))
    wo = np.ascontiguousarray(np.asarray(L0_attn_wo, dtype=np.float32))
    gw = np.ascontiguousarray(np.asarray(L0_ffn_gate_w, dtype=np.int32))
    uw = np.ascontiguousarray(np.asarray(L0_ffn_up_w, dtype=np.int32))
    dw = np.ascontiguousarray(np.asarray(L0_ffn_down_w, dtype=np.int32))
    gs = np.ascontiguousarray(np.asarray(L0_ffn_gate_scale, dtype=np.float32))
    us = np.ascontiguousarray(np.asarray(L0_ffn_up_scale, dtype=np.float32))
    ds = np.ascontiguousarray(np.asarray(L0_ffn_down_scale, dtype=np.float32))
    if _NC_CACHE is None:
        _NC_CACHE = build_nc()
    nc = _NC_CACHE
    in_maps = []
    for b in range(8):
        in_maps.append(dict(x=x[b], wq=wq, wk=wk, wv=wv, wo=wo,
                            gw=gw, uw=uw, dw=dw, gs=gs, us=us, ds=ds))
    res = run_bass_kernel_spmd(nc, in_maps, core_ids=list(range(8)))
    out = np.stack([np.asarray(r["out"], dtype=np.float32) for r in res.results], axis=0)
    return out


# version 04
# speedup vs baseline: 1.2478x; 1.2478x over previous
import numpy as np
import ml_dtypes
from contextlib import ExitStack

import concourse.bass as bass
import concourse.mybir as mybir
from concourse.bass_utils import run_bass_kernel_spmd

F32 = mybir.dt.float32
BF16 = mybir.dt.bfloat16
I32 = mybir.dt.int32
AF = mybir.ActivationFunctionType
ALU = mybir.AluOpType
bfdt = ml_dtypes.bfloat16

D = 2048
T = 2048
I = 8192
KC = D // 128
NTB = 4
TB = 512
NPAIR = 16


def _consts():
    inv = 1.0 / 10000.0 ** (np.arange(0, 64, 2, dtype=np.float64) / 64.0)
    t = np.arange(T, dtype=np.float64)
    f = np.outer(t, inv)
    COS = np.tile(np.cos(f).T, (4, 1)).astype(np.float32).astype(bfdt)
    sgn = np.where((np.arange(128) // 32) % 2 == 0, -1.0, 1.0)[:, None]
    SINS = (np.tile(np.sin(f).T, (4, 1)) * sgn).astype(np.float32).astype(bfdt)
    m = np.arange(128)
    nb = (m // 64) * 64 + 2 * (m % 32) + ((m % 64) // 32)
    sigma = np.where(m % 64 < 32, m + 32, m - 32)
    PNB = np.zeros((128, 128), dtype=bfdt)
    PNB[nb, m] = 1
    PCMB = np.zeros((128, 128), dtype=bfdt)
    PCMB[nb[sigma], m] = 1
    DUPLO = np.zeros((128, 128), dtype=bfdt)
    DUPLO[m % 64, m] = 1
    DUPHI = np.zeros((128, 128), dtype=bfdt)
    DUPHI[64 + (m % 64), m] = 1
    IDN = np.zeros((128, 128), dtype=bfdt)
    IDN[np.arange(128), np.arange(128)] = 1
    MASKS = []
    tk = np.arange(128)[:, None]
    tq = np.arange(1024)[None, :] % 512
    for v in range(4):
        MASKS.append((tk <= tq - 128 * v).astype(np.float32).astype(bfdt))
    return COS, SINS, PNB, PCMB, DUPLO, DUPHI, IDN, MASKS


def build_nc(dbg=False):
    nc = bass.Bass("TRN2", target_bir_lowering=False)
    x_h = nc.dram_tensor("x", [T, D], F32, kind="ExternalInput").ap()
    wq_h = nc.dram_tensor("wq", [D, D], F32, kind="ExternalInput").ap()
    wk_h = nc.dram_tensor("wk", [512, D], F32, kind="ExternalInput").ap()
    wv_h = nc.dram_tensor("wv", [512, D], F32, kind="ExternalInput").ap()
    wo_h = nc.dram_tensor("wo", [D, D], F32, kind="ExternalInput").ap()
    gw_h = nc.dram_tensor("gw", [I, D], I32, kind="ExternalInput").ap()
    uw_h = nc.dram_tensor("uw", [I, D], I32, kind="ExternalInput").ap()
    dw_h = nc.dram_tensor("dw", [D, I], I32, kind="ExternalInput").ap()
    gs_h = nc.dram_tensor("gs", [I], F32, kind="ExternalInput").ap()
    us_h = nc.dram_tensor("us", [I], F32, kind="ExternalInput").ap()
    ds_h = nc.dram_tensor("ds", [D], F32, kind="ExternalInput").ap()
    out_h = nc.dram_tensor("out", [T, D], F32, kind="ExternalOutput").ap()
    if dbg:
        d_xn = nc.dram_tensor("d_xn", [T, D], BF16, kind="ExternalOutput").ap()
        d_hb = nc.dram_tensor("d_hb", [4, 128, 64, 512], BF16, kind="ExternalOutput").ap()
        d_ffn = nc.dram_tensor("d_ffn", [D, T], BF16, kind="ExternalOutput").ap()

        d_q = nc.dram_tensor("d_q", [128, 16, T], BF16, kind="ExternalOutput").ap()
        d_kd = nc.dram_tensor("d_kd", [128, 8, T], BF16, kind="ExternalOutput").ap()
        d_v = nc.dram_tensor("d_v", [128, 16, 520], BF16, kind="ExternalOutput").ap()
        d_afm = nc.dram_tensor("d_afm", [128, 16, 512], BF16, kind="ExternalOutput").ap()
        d_at = nc.dram_tensor("d_at", [D, T], BF16, kind="ExternalOutput").ap()
        d_h = nc.dram_tensor("d_h", [128, 64, 512], BF16, kind="ExternalOutput").ap()
    scr_wq = nc.dram_tensor("scr_wq", [D, D], BF16).ap()
    scr_wk = nc.dram_tensor("scr_wk", [D, 512], BF16).ap()
    scr_wv = nc.dram_tensor("scr_wv", [D, 512], BF16).ap()
    scr_wo = nc.dram_tensor("scr_wo", [D, D], BF16).ap()
    scr_g = nc.dram_tensor("scr_g", [D, I], BF16).ap()
    scr_u = nc.dram_tensor("scr_u", [D, I], BF16).ap()
    scr_d = nc.dram_tensor("scr_d", [I, D], BF16).ap()
    scr_xn = nc.dram_tensor("scr_xn", [T, D], BF16).ap()
    scr_at = nc.dram_tensor("scr_at", [D, T], BF16).ap()
    scr_ffn = nc.dram_tensor("scr_ffn", [D, T], BF16).ap()
    COS_c, SINS_c, PNB_c, PCMB_c, DLO_c, DHI_c, IDN_c, MASKS_c = _consts()
    cos_d = nc.inline_tensor(COS_c, name="cos_c").ap()
    sin_d = nc.inline_tensor(SINS_c, name="sin_c").ap()
    pnb_d = nc.inline_tensor(PNB_c, name="pnb_c").ap()
    pcmb_d = nc.inline_tensor(PCMB_c, name="pcmb_c").ap()
    dlo_d = nc.inline_tensor(DLO_c, name="dlo_c").ap()
    dhi_d = nc.inline_tensor(DHI_c, name="dhi_c").ap()
    idn_d = nc.inline_tensor(IDN_c, name="idn_c").ap()
    msk_d = [nc.inline_tensor(MASKS_c[v], name=f"msk{v}_c").ap() for v in range(4)]

    es = ExitStack()

    class S:
        def __init__(self, name):
            self.h = es.enter_context(nc.semaphore(name))
            self.v = 0
    sl = S("sl")
    ss = S("ss")
    sg = S("sg")
    spe = S("spe")
    sac = S("sac")
    sdv = S("sdv")
    spl = S("spl")
    sout = S("sout")

    QSEM = {"sync": sl, "scalar": ss, "gpsimd": sg}
    ESEM = {"tensor": spe, "scalar": sac, "vector": sdv, "gpsimd": spl}

    ENGS = ["sync", "scalar", "vector", "tensor", "gpsimd"]
    queues = {e: [] for e in ENGS}

    def flush_block():
        with nc.Block() as blk:
            for name in ENGS:
                fns = queues[name]
                if fns:
                    getattr(blk, name)(lambda e, fns=fns: [f(e) for f in fns])
        for name in ENGS:
            queues[name] = []

    def emit(eng, f):
        queues[eng].append(f)

    def wait(eng, sem, v=None):
        vv = sem.v if v is None else v
        if vv > 0:
            emit(eng, lambda e, s=sem.h, v=vv: e.wait_ge(s, v))

    def dma(eng, out, in_, **kw):
        sem = QSEM[eng]
        sem.v += 16
        emit(eng, lambda e, o=out, i=in_, k=kw, s=sem.h: e.dma_start(out=o, in_=i, **k).then_inc(s, 16))
        return sem.v

    def inc(eng, call):
        sem = ESEM[eng]
        sem.v += 1
        emit(eng, lambda e, c=call, s=sem.h: c(e).then_inc(s, 1))
        return sem.v

    def op(eng, call):
        emit(eng, call)

    def sb_in(stack, name, shape, dt):
        return stack.enter_context(nc.sbuf_tensor(name, shape, dt))

    COS = sb_in(es, "COS", [128, T], BF16)
    SINS = sb_in(es, "SINS", [128, T], BF16)
    PNB = sb_in(es, "PNB", [128, 128], BF16)
    PCMB = sb_in(es, "PCMB", [128, 128], BF16)
    DLO = sb_in(es, "DLO", [128, 128], BF16)
    DHI = sb_in(es, "DHI", [128, 128], BF16)
    IDN = sb_in(es, "IDN", [128, 128], BF16)
    MSK = [sb_in(es, f"MSK{v}", [128, 1024], BF16) for v in range(4)]
    gs_sb = sb_in(es, "gs_sb", [128, 64], F32)
    us_sb = sb_in(es, "us_sb", [128, 64], F32)
    ds_sb = sb_in(es, "ds_sb", [128, 16], F32)
    ones64 = sb_in(es, "ones64", [1, 64], BF16)
    epsv = sb_in(es, "epsv", [128, 1], F32)
    n128v = sb_in(es, "n128v", [128, 1], F32)

    es0 = ExitStack()
    stg32 = [sb_in(es0, f"stg32_{i}", [128, I], I32) for i in range(2)]
    stg16 = [sb_in(es0, f"stg16_{i}", [128, I], BF16) for i in range(2)]
    stgT = sb_in(es0, "stgT", [128, 64, 512], BF16)
    pes0 = ExitStack()
    pst = [pes0.enter_context(nc.psum_tensor(f"pst{i}", [128, 128], BF16)) for i in range(4)]
    st32_rd = [0, 0]
    st16_rd = [0, 0]
    st16_wr = [0, 0]
    pst_rd = [0, 0, 0, 0]
    stgT_wr = [0]

    op("gpsimd", lambda e: e.memset(ones64[:], 1.0))
    op("gpsimd", lambda e: e.memset(epsv[:], 1e-5))
    op("gpsimd", lambda e: e.memset(n128v[:], -128.0))
    dma("sync", COS[:], cos_d)
    dma("sync", SINS[:], sin_d)
    dma("sync", PNB[:], pnb_d)
    dma("sync", PCMB[:], pcmb_d)
    dma("sync", DLO[:], dlo_d)
    dma("sync", DHI[:], dhi_d)
    dma("sync", IDN[:], idn_d)
    for v in range(4):
        dma("sync", MSK[v][:], msk_d[v])
    for c in range(64):
        dma("sync", gs_sb[:, c:c + 1], gs_h[c * 128:(c + 1) * 128][:, None])
        dma("sync", us_sb[:, c:c + 1], us_h[c * 128:(c + 1) * 128][:, None])
    for c in range(16):
        dma("sync", ds_sb[:, c:c + 1], ds_h[c * 128:(c + 1) * 128][:, None])
    wait("tensor", sl, sl.v)

    def prep_weight(src_h, n_chunks, W, scr_T, sub128, src_dtype):
        kcn = W // 128
        for c in range(n_chunks):
            i = c % 2
            s32 = stg32[i].ap().bitcast(src_dtype)[:, 0:W] if src_dtype != I32 else stg32[i][:, 0:W]
            s16 = stg16[i]
            wait("gpsimd", sac, st32_rd[i])
            dma("gpsimd", s32[:, :], src_h[c * 128:(c + 1) * 128, :])
            wait("scalar", sg)
            wait("scalar", spe, st16_rd[i])
            if sub128:
                st32_rd[i] = inc("scalar", lambda e, s32=s32, s16=s16, W=W: e.activation(
                    out=s16[:, 0:W], in_=s32, func=AF.Identity, bias=n128v[:]))
            else:
                st32_rd[i] = inc("scalar", lambda e, s32=s32, s16=s16, W=W: e.activation(
                    out=s16[:, 0:W], in_=s32, func=AF.Copy))
            cast_done = st32_rd[i]
            wait("tensor", sac, cast_done)
            co = (c % 4) * 128
            last_tp = 0
            for kc in range(kcn):
                p = pst[kc % 4]
                wait("tensor", sdv, pst_rd[kc % 4])
                last_tp = inc("tensor", lambda e, p=p, s16=s16, kc=kc: e.transpose(
                    out=p[:], in_=s16[:, kc * 128:(kc + 1) * 128], identity=IDN[:]))
                wait("vector", spe, last_tp)
                wait("vector", sl, stgT_wr[0])
                pst_rd[kc % 4] = inc("vector", lambda e, p=p, kc=kc, co=co: e.tensor_copy(
                    out=stgT[:, kc, co:co + 128], in_=p[:]))
            st16_rd[i] = last_tp
            if c % 4 == 3 or c == n_chunks - 1:
                g0 = (c // 4) * 512
                gw_ = (c % 4 + 1) * 128
                wait("sync", sdv)
                for kc in range(kcn):
                    dma("sync", scr_T[kc * 128:(kc + 1) * 128, g0:g0 + gw_], stgT[:, kc, 0:gw_])
                stgT_wr[0] = sl.v

    prep_weight(wq_h, 16, D, scr_wq, False, F32)
    prep_weight(wk_h, 4, D, scr_wk, False, F32)
    prep_weight(wv_h, 4, D, scr_wv, False, F32)
    prep_weight(wo_h, 16, D, scr_wo, False, F32)
    prep_weight(gw_h, 64, D, scr_g, True, I32)
    prep_weight(uw_h, 64, D, scr_u, True, I32)
    prep_weight(dw_h, 16, I, scr_d, True, I32)
    flush_block()
    pes0.close()
    es0.close()

    es1 = ExitStack()
    q_sb = sb_in(es1, "q_sb", [128, 16, T], BF16)
    kd_sb = sb_in(es1, "kd_sb", [128, 8, T], BF16)
    v_sb = sb_in(es1, "v_sb", [128, 16, 520], BF16)

    es1b = ExitStack()
    xt = [sb_in(es1b, f"xt{i}", [128, D], F32) for i in range(2)]
    xn_tm = sb_in(es1b, "xn_tm", [128, D], BF16)
    xn_fm = sb_in(es1b, "xn_fm", [128, KC, TB], BF16)
    sumv = sb_in(es1b, "sumv", [128, 1], F32)
    sqv = sb_in(es1b, "sqv", [128, 1], F32)
    mv = sb_in(es1b, "mv", [128, 1], F32)
    ex2v = sb_in(es1b, "ex2v", [128, 1], F32)
    varv = sb_in(es1b, "varv", [128, 1], F32)
    rstdv = sb_in(es1b, "rstdv", [128, 1], F32)
    nmrv = sb_in(es1b, "nmrv", [128, 1], F32)
    wbuf = [sb_in(es1b, f"wbuf{i}", [128, 16, 512], BF16) for i in range(2)]
    qtmp = [sb_in(es1b, f"qtmp{i}", [128, 512], BF16) for i in range(2)]
    t1b = sb_in(es1b, "t1b", [128, 512], BF16)
    t2b = sb_in(es1b, "t2b", [128, 512], BF16)
    krope = sb_in(es1b, "krope", [128, 4, 512], BF16)

    pes = ExitStack()
    pp = [pes.enter_context(nc.psum_tensor(f"pp{i}", [128, 512], F32)) for i in range(2)]
    pa_ps = pes.enter_context(nc.psum_tensor("pa_ps", [128, 512], F32))
    pb_ps = pes.enter_context(nc.psum_tensor("pb_ps", [128, 512], F32))

    for tt in range(16):
        for j in range(8):
            op("vector", lambda e, tt=tt, j=j: e.memset(v_sb[:, tt, j * 65 + 64:j * 65 + 65], 1.0))

    xt_rd = [0, 0]
    xntm_rd = [0]
    sqv_rd = [0]
    qtmp_st = [(0, 0), (0, 0)]
    pp_rd = [0, 0]
    pa_rd = [0]
    pb_rd = [0]
    rope_flip = [0]

    def ln_tile(b, tau):
        t0 = (4 * b + tau) * 128
        xe = xt[tau % 2]
        wait("sync", sac, xt_rd[tau % 2])
        lv = dma("sync", xe[:], x_h[t0:t0 + 128, :])
        wait("vector", sl, lv)
        sdv_red = inc("vector", lambda e, xe=xe: e.reduce_sum(sumv[:], xe[:], axis=mybir.AxisListType.X))
        wait("scalar", sl, lv)
        wait("scalar", sdv, sqv_rd[0])
        wait("scalar", ss, xntm_rd[0])
        sac_sq = inc("scalar", lambda e, xe=xe: e.activation(
            out=xn_tm[:], in_=xe[:], func=AF.Square, accum_out=sqv[:]))
        wait("vector", sdv, sdv_red)
        sdv_mv = inc("vector", lambda e: e.tensor_scalar_mul(out=mv[:], in0=sumv[:], scalar1=1.0 / D))
        wait("vector", sac, sac_sq)
        sqv_rd[0] = inc("vector", lambda e: e.tensor_scalar_mul(out=ex2v[:], in0=sqv[:], scalar1=1.0 / D))
        wait("vector", sdv, sdv_mv)
        sdv_vm = inc("vector", lambda e: e.tensor_tensor(out=varv[:], in0=mv[:], in1=mv[:], op=ALU.mult))
        wait("vector", sdv, sdv_vm)
        sdv_var = inc("vector", lambda e: e.tensor_tensor(out=varv[:], in0=ex2v[:], in1=varv[:], op=ALU.subtract))
        wait("scalar", sdv, sdv_var)
        sac_ln1 = inc("scalar", lambda e: e.activation(out=rstdv[:], in_=varv[:], func=AF.Ln, bias=epsv[:]))
        wait("scalar", sac, sac_ln1)
        sac_r = inc("scalar", lambda e: e.activation(out=rstdv[:], in_=rstdv[:], func=AF.Exp, scale=-0.5))
        wait("vector", sac, sac_r)
        sdv_nmr = inc("vector", lambda e: e.tensor_tensor(out=nmrv[:], in0=mv[:], in1=rstdv[:], op=ALU.mult))
        wait("vector", sdv, sdv_nmr)
        sdv_nmr2 = inc("vector", lambda e: e.tensor_scalar_mul(out=nmrv[:], in0=nmrv[:], scalar1=-1.0))
        wait("scalar", sdv, sdv_nmr2)
        sac_xn = inc("scalar", lambda e, xe=xe: e.activation(
            out=xn_tm[:], in_=xe[:], func=AF.Identity, scale=rstdv[:], bias=nmrv[:]))
        xt_rd[tau % 2] = sac_xn
        wait("scalar", sac, sac_xn)
        dma("scalar", scr_xn[t0:t0 + 128, :], xn_tm[:])
        xntm_rd[0] = ss.v

    def rope_chunk(b, ps_src, ps_idx, dst_ap):
        fl = rope_flip[0]
        rope_flip[0] ^= 1
        qt = qtmp[fl]
        (g_spe, g_sdv) = qtmp_st[fl]
        wait("scalar", spe, g_spe)
        wait("scalar", sdv, g_sdv)
        sac_ev = inc("scalar", lambda e, qt=qt, p=ps_src: e.activation(out=qt[:], in_=p[:], func=AF.Copy))
        if ps_idx is not None:
            pp_rd[ps_idx] = sac_ev
        wait("tensor", sac, sac_ev)
        wait("tensor", sdv, pa_rd[0])
        spe_a = inc("tensor", lambda e, qt=qt: e.matmul(out=pa_ps[:], lhsT=PNB[:], rhs=qt[:],
                                                        start=True, stop=True))
        wait("tensor", sdv, pb_rd[0])
        spe_b = inc("tensor", lambda e, qt=qt: e.matmul(out=pb_ps[:], lhsT=PCMB[:], rhs=qt[:],
                                                        start=True, stop=True))
        bsl = slice(b * TB, (b + 1) * TB)
        wait("vector", spe, spe_a)
        sdv_t1 = inc("vector", lambda e, bsl=bsl: e.tensor_tensor(
            out=t1b[:], in0=pa_ps[:], in1=COS[:, bsl], op=ALU.mult))
        pa_rd[0] = sdv_t1
        wait("vector", spe, spe_b)
        sdv_t2 = inc("vector", lambda e, bsl=bsl: e.tensor_tensor(
            out=t2b[:], in0=pb_ps[:], in1=SINS[:, bsl], op=ALU.mult))
        pb_rd[0] = sdv_t2
        wait("vector", sdv, sdv_t2)
        sdv_o = inc("vector", lambda e, d=dst_ap: e.tensor_tensor(out=d, in0=t1b[:], in1=t2b[:], op=ALU.add))
        qtmp_st[fl] = (spe_a if False else spe_b, sdv_t1)
        return sac_ev, sdv_o

    wbuf_rd = [0, 0]

    for b in range(NTB):
        bsl = slice(b * TB, (b + 1) * TB)
        for tau in range(4):
            ln_tile(b, tau)
        wait("sync", ss, ss.v)
        for c in range(KC):
            dma("sync", xn_fm[:, c, :], scr_xn[bsl, c * 128:(c + 1) * 128], transpose=True)
        xn_ready = sl.v
        for s in range(4):
            wb = wbuf[s % 2]
            wait("sync", spe, wbuf_rd[s % 2])
            for k in range(KC):
                dma("sync", wb[:, k, :], scr_wq[k * 128:(k + 1) * 128, s * 512:(s + 1) * 512])
            ld_done = sl.v
            for mj in range(4):
                mu = s * 4 + mj
                pe = pp[mu % 2]
                wait("tensor", sl, ld_done)
                wait("tensor", sl, xn_ready)
                wait("tensor", sac, pp_rd[mu % 2])
                last = 0
                for k in range(KC):
                    last = inc("tensor", lambda e, k=k, pe=pe, wb=wb, mj=mj: e.matmul(
                        out=pe[:], lhsT=wb[:, k, mj * 128:(mj + 1) * 128], rhs=xn_fm[:, k, :],
                        start=(k == 0), stop=(k == KC - 1)))
                wbuf_rd[s % 2] = last
                wait("scalar", spe, last)
                rope_chunk(b, pe, mu % 2, q_sb[:, mu, bsl])
        wbk = wbuf[0]
        wait("sync", spe, wbuf_rd[0])
        for k in range(KC):
            dma("sync", wbk[:, k, 0:512], scr_wk[k * 128:(k + 1) * 128, :])
        ld_done = sl.v
        for mu in range(4):
            pe = pp[mu % 2]
            wait("tensor", sl, ld_done)
            wait("tensor", sl, xn_ready)
            wait("tensor", sac, pp_rd[mu % 2])
            last = 0
            for k in range(KC):
                last = inc("tensor", lambda e, k=k, pe=pe, mu=mu: e.matmul(
                    out=pe[:], lhsT=wbk[:, k, mu * 128:(mu + 1) * 128], rhs=xn_fm[:, k, :],
                    start=(k == 0), stop=(k == KC - 1)))
            wbuf_rd[0] = last
            wait("scalar", spe, last)
            _, sdv_o = rope_chunk(b, pe, mu % 2, krope[:, mu, :])
            wait("tensor", sdv, sdv_o)
            for half, PM in ((0, DLO), (1, DHI)):
                wait("tensor", sdv, pa_rd[0])
                spe_d = inc("tensor", lambda e, PM=PM, mu=mu: e.matmul(
                    out=pa_ps[:], lhsT=PM[:], rhs=krope[:, mu, :], start=True, stop=True))
                wait("scalar", spe, spe_d)
                sac_e = inc("scalar", lambda e, mu=mu, half=half, bsl=bsl: e.activation(
                    out=kd_sb[:, 2 * mu + half, bsl], in_=pa_ps[:], func=AF.Copy))
                pa_rd[0] = 0
                wait("tensor", sac, sac_e)
        wbv = wbuf[1]
        wait("sync", spe, wbuf_rd[1])
        for k in range(KC):
            dma("sync", wbv[:, k, 0:512], scr_wv[k * 128:(k + 1) * 128, :])
        ld_done = sl.v
        for tau in range(4):
            pe = pp[tau % 2]
            wait("tensor", sl, ld_done)
            wait("tensor", sl, xn_ready)
            wait("tensor", sac, pp_rd[tau % 2])
            last = 0
            for k in range(KC):
                last = inc("tensor", lambda e, k=k, pe=pe, tau=tau, wbv=wbv: e.matmul(
                    out=pe[:], lhsT=xn_fm[:, k, tau * 128:(tau + 1) * 128], rhs=wbv[:, k, 0:512],
                    start=(k == 0), stop=(k == KC - 1)))
            wbuf_rd[1] = last
            wait("vector", spe, last)
            vout = v_sb[:, 4 * b + tau, 0:520].rearrange("p (h x) -> p h x", h=8)[:, :, 0:64]
            vin = pe[:].rearrange("p (h x) -> p h x", h=8)
            sdv_v = inc("vector", lambda e, vout=vout, vin=vin: e.tensor_copy(out=vout, in_=vin))
            wait("tensor", sdv, sdv_v)
            pp_rd[tau % 2] = 0
        flush_block()

    es1b.close()
    pes.close()

    es2 = ExitStack()
    wob = [sb_in(es2, f"wob{i}", [128, 16, 512], BF16) for i in range(2)]
    pbuf = [sb_in(es2, f"pbuf{i}", [128, 1024], BF16) for i in range(2)]
    afm = sb_in(es2, "afm", [128, 16, 512], BF16)
    lnv2 = sb_in(es2, "lnv2", [1, 512], F32)
    rbf = sb_in(es2, "rbf", [1, 512], BF16)
    rb64 = sb_in(es2, "rb64", [64, 512], BF16)
    tmp64 = sb_in(es2, "tmp64", [64, 512], BF16)
    woev = [sb_in(es2, f"woev{i}", [128, 512], BF16) for i in range(2)]

    pes2 = ExitStack()
    psc = [pes2.enter_context(nc.psum_tensor(f"psc{i}", [128, 1024], F32)) for i in range(2)]
    pav = [pes2.enter_context(nc.psum_tensor(f"pav{i}", [128, 512], F32)) for i in range(4)]

    norm_state = {"rbf_rd": 0, "rb64_rd": 0}

    for B in range(NTB):
        if dbg and B == 0:
            sout.v += 16; emit("gpsimd", lambda e, s=sout.h: e.dma_start(out=d_xn, in_=scr_xn).then_inc(s, 16))


            sout.v += 16; emit("gpsimd", lambda e, s=sout.h: e.dma_start(out=d_q, in_=q_sb[:]).then_inc(s, 16))
            sout.v += 16; emit("gpsimd", lambda e, s=sout.h: e.dma_start(out=d_kd, in_=kd_sb[:]).then_inc(s, 16))
            sout.v += 16; emit("gpsimd", lambda e, s=sout.h: e.dma_start(out=d_v, in_=v_sb[:]).then_inc(s, 16))
        Bsl = slice(B * TB, (B + 1) * TB)
        nck = 4 * B + 4
        for s in range(2):
            for dv in range(16):
                dma("sync", wob[s][:, dv, :],
                    scr_wo[dv * 128:(dv + 1) * 128, s * 512:(s + 1) * 512])
        wo_ld01 = sl.v

        exp_read = {}
        av_tt = {}
        av_last = {}
        norm_q = []

        def do_norm(e_pair, pidx):
            for hh in range(2):
                pa = pav[pidx + hh]
                sac_ln = inc("scalar", lambda e, pa=pa: e.activation(
                    out=lnv2[:], in_=pa[64:65, :], func=AF.Ln))
                wait("scalar", spe, norm_state["rbf_rd"])
                wait("scalar", sac, sac_ln)
                sac_r = inc("scalar", lambda e: e.activation(
                    out=rbf[:], in_=lnv2[:], func=AF.Exp, scale=-1.0))
                wait("tensor", sac, sac_r)
                spe_b = inc("tensor", lambda e, pa=pa: e.matmul(
                    out=pa[64:128, :], lhsT=ones64[:], rhs=rbf[:], start=True, stop=True))
                norm_state["rbf_rd"] = spe_b
                wait("scalar", spe, spe_b)
                wait("scalar", sdv, norm_state["rb64_rd"])
                sac_rb = inc("scalar", lambda e, pa=pa: e.activation(
                    out=rb64[:], in_=pa[64:128, :], func=AF.Copy))
                wait("vector", sac, sac_rb)
                if hh == 0:
                    sdv_tt = inc("vector", lambda e, pa=pa, ep=e_pair: e.tensor_tensor(
                        out=afm[0:64, ep, :], in0=pa[0:64, :], in1=rb64[:], op=ALU.mult))
                else:
                    sdv_t64 = inc("vector", lambda e, pa=pa: e.tensor_tensor(
                        out=tmp64[:], in0=pa[0:64, :], in1=rb64[:], op=ALU.mult))
                    wait("vector", sdv, sdv_t64)
                    sdv_tt = inc("vector", lambda e, ep=e_pair: e.tensor_scalar_mul(
                        out=afm[64:128, ep, :], in0=tmp64[:], scalar1=1.0))
                norm_state["rb64_rd"] = sdv_tt
                av_tt[pidx + hh] = sdv_tt

        for e_pair in range(NPAIR):
            j = e_pair // 2
            pidx = (e_pair % 2) * 2
            for hh in range(2):
                if (pidx + hh) in av_tt:
                    wait("tensor", sdv, av_tt[pidx + hh])
            for c in range(nck):
                ps = psc[c % 2]
                if (c % 2) in exp_read:
                    wait("tensor", sac, exp_read[c % 2])
                inc("tensor", lambda e, ps=ps, j=j, c=c, ep=e_pair, Bsl=Bsl: e.matmul(
                    out=ps[:, 0:512], lhsT=kd_sb[0:64, j, c * 128:(c + 1) * 128],
                    rhs=q_sb[0:64, ep, Bsl], start=True, stop=True))
                spe_s = inc("tensor", lambda e, ps=ps, j=j, c=c, ep=e_pair, Bsl=Bsl: e.matmul(
                    out=ps[:, 512:1024], lhsT=kd_sb[64:128, j, c * 128:(c + 1) * 128],
                    rhs=q_sb[64:128, ep, Bsl], start=True, stop=True))
                wait("scalar", spe, spe_s)
                if (c % 2) in av_last:
                    wait("scalar", spe, av_last[c % 2])
                sac_e = inc("scalar", lambda e, ps=ps, c=c: e.activation(
                    out=pbuf[c % 2][:], in_=ps[:], func=AF.Exp, scale=0.125))
                exp_read[c % 2] = sac_e
                dep_sem, dep_v = sac, sac_e
                if c >= 4 * B:
                    wait("gpsimd", sac, sac_e)
                    spl_m = inc("gpsimd", lambda e, c=c, B=B: e.tensor_tensor(
                        out=pbuf[c % 2][:], in0=pbuf[c % 2][:], in1=MSK[c - 4 * B][:], op=ALU.mult))
                    dep_sem, dep_v = spl, spl_m
                wait("tensor", dep_sem, dep_v)
                inc("tensor", lambda e, c=c, j=j, pidx=pidx, nck=nck: e.matmul(
                    out=pav[pidx][0:65, :], lhsT=v_sb[:, c, j * 65:(j + 1) * 65],
                    rhs=pbuf[c % 2][:, 0:512], start=(c == 0), stop=(c == nck - 1)))
                spe_av = inc("tensor", lambda e, c=c, j=j, pidx=pidx, nck=nck: e.matmul(
                    out=pav[pidx + 1][0:65, :], lhsT=v_sb[:, c, j * 65:(j + 1) * 65],
                    rhs=pbuf[c % 2][:, 512:1024], start=(c == 0), stop=(c == nck - 1)))
                av_last[c % 2] = spe_av
            if norm_q:
                norm_q.pop(0)()
            norm_q.append(lambda ep=e_pair, pidx=pidx: do_norm(ep, pidx))
        while norm_q:
            norm_q.pop(0)()
        attn_done = sdv.v
        woev_rd = [0, 0]
        wob_rd = [0, 0]
        for s in range(4):
            wot = wob[s % 2]
            if s >= 2:
                wait("sync", spe, wob_rd[s % 2])
                for dv in range(16):
                    dma("sync", wot[:, dv, :],
                        scr_wo[dv * 128:(dv + 1) * 128, s * 512:(s + 1) * 512])
                ld_done = sl.v
            else:
                ld_done = wo_ld01
            for mj in range(4):
                mu = s * 4 + mj
                pw = psc[mu % 2]
                wait("tensor", sl, ld_done)
                wait("tensor", sdv, attn_done)
                if (mu % 2) in exp_read:
                    wait("tensor", sac, exp_read.pop(mu % 2))
                last = 0
                for dv in range(16):
                    last = inc("tensor", lambda e, dv=dv, pw=pw, wot=wot, mj=mj: e.matmul(
                        out=pw[:, 0:512], lhsT=wot[:, dv, mj * 128:(mj + 1) * 128], rhs=afm[:, dv, :],
                        start=(dv == 0), stop=(dv == 15)))
                wob_rd[s % 2] = last
                wait("scalar", spe, last)
                wait("scalar", ss, woev_rd[mu % 2])
                sac_ev = inc("scalar", lambda e, pw=pw, mu=mu: e.activation(
                    out=woev[mu % 2][:], in_=pw[:, 0:512], func=AF.Copy))
                exp_read[mu % 2] = sac_ev
                wait("scalar", sac, sac_ev)
                dma("scalar", scr_at[mu * 128:(mu + 1) * 128, Bsl], woev[mu % 2][:])
                woev_rd[mu % 2] = ss.v
        if dbg and B == NTB - 1:
            wait("gpsimd", sdv, attn_done)
            sout.v += 16; emit("gpsimd", lambda e, s=sout.h: e.dma_start(out=d_afm, in_=afm[:]).then_inc(s, 16))
        flush_block()

    pes2.close()
    es2.close()
    es1.close()

    es3 = ExitStack()
    h_sb = sb_in(es3, "h_sb", [128, 64, 512], BF16)
    xnf = sb_in(es3, "xnf", [128, KC, 512], BF16)
    gwide = [sb_in(es3, f"gwide{i}", [128, KC, 256], BF16) for i in range(2)]
    uwide = [sb_in(es3, f"uwide{i}", [128, KC, 256], BF16) for i in range(2)]
    dbw = [sb_in(es3, f"dbw{i}", [128, 8, 512], BF16) for i in range(2)]
    sgt = [sb_in(es3, f"sgt{i}", [128, 512], BF16) for i in range(2)]
    sut = [sb_in(es3, f"sut{i}", [128, 512], BF16) for i in range(2)]
    fdev = [sb_in(es3, f"fdev{i}", [128, 512], BF16) for i in range(2)]
    xt3 = [sb_in(es3, f"xt3{i}", [128, D], F32) for i in range(2)]
    at3 = [sb_in(es3, f"at3{i}", [128, D], BF16) for i in range(2)]
    ft3 = [sb_in(es3, f"ft3{i}", [128, D], BF16) for i in range(2)]
    out3 = [sb_in(es3, f"out3{i}", [128, D], F32) for i in range(2)]

    pes3 = ExitStack()
    pg = [pes3.enter_context(nc.psum_tensor(f"pg{i}", [128, 512], F32)) for i in range(4)]
    pu = [pes3.enter_context(nc.psum_tensor(f"pu{i}", [128, 512], F32)) for i in range(4)]

    gu_mm = [0, 0]
    sgt_rd = [0, 0]
    gu_ev_sac = [0, 0]
    gu_ev_sdv = [0, 0]
    dn_mm = {}
    out3_rd = [0, 0]
    xt3_rd = [0, 0]
    at3_rd = [0, 0]
    fdev_rd = [0, 0]

    for b in range(NTB):
        bsl = slice(b * TB, (b + 1) * TB)
        for c in range(KC):
            dma("sync", xnf[:, c, :], scr_xn[bsl, c * 128:(c + 1) * 128], transpose=True)
        xn_ld = sl.v
        for sp in range(32):
            set_ = sp % 2
            gw_t, uw_t = gwide[set_], uwide[set_]
            wait("sync", spe, gu_mm[set_])
            for k in range(KC):
                dma("sync", gw_t[:, k, :],
                    scr_g[k * 128:(k + 1) * 128, sp * 256:(sp + 1) * 256])
                dma("sync", uw_t[:, k, :],
                    scr_u[k * 128:(k + 1) * 128, sp * 256:(sp + 1) * 256])
            ld_done = sl.v
            pga, pgb_ = pg[2 * set_], pg[2 * set_ + 1]
            pua, pub = pu[2 * set_], pu[2 * set_ + 1]
            wait("tensor", sl, ld_done)
            wait("tensor", sl, xn_ld)
            wait("tensor", sac, gu_ev_sac[set_])
            wait("tensor", sdv, gu_ev_sdv[set_])
            last = 0
            for k in range(KC):
                inc("tensor", lambda e, k=k, gw_t=gw_t, pga=pga: e.matmul(
                    out=pga[:], lhsT=gw_t[:, k, 0:128], rhs=xnf[:, k, :],
                    start=(k == 0), stop=(k == KC - 1)))
                inc("tensor", lambda e, k=k, gw_t=gw_t, pgb_=pgb_: e.matmul(
                    out=pgb_[:], lhsT=gw_t[:, k, 128:256], rhs=xnf[:, k, :],
                    start=(k == 0), stop=(k == KC - 1)))
                inc("tensor", lambda e, k=k, uw_t=uw_t, pua=pua: e.matmul(
                    out=pua[:], lhsT=uw_t[:, k, 0:128], rhs=xnf[:, k, :],
                    start=(k == 0), stop=(k == KC - 1)))
                last = inc("tensor", lambda e, k=k, uw_t=uw_t, pub=pub: e.matmul(
                    out=pub[:], lhsT=uw_t[:, k, 128:256], rhs=xnf[:, k, :],
                    start=(k == 0), stop=(k == KC - 1)))
            gu_mm[set_] = last
            wait("scalar", spe, last)
            wait("vector", spe, last)
            e_sac = e_sdv = 0
            for d in range(2):
                ci = sp * 2 + d
                pgx = pga if d == 0 else pgb_
                pux = pua if d == 0 else pub
                wait("scalar", sdv, sgt_rd[d])
                e_sac = inc("scalar", lambda e, pgx=pgx, ci=ci, d=d: e.activation(
                    out=sgt[d][:], in_=pgx[:], func=AF.Silu, scale=gs_sb[:, ci:ci + 1]))
                e_sdv0 = inc("vector", lambda e, pux=pux, ci=ci, d=d: e.tensor_single_scalar(
                    out=sut[d][:], in_=pux[:], scalar=us_sb[:, ci:ci + 1], op=ALU.mult))
                wait("vector", sac, e_sac)
                wait("vector", sdv, e_sdv0)
                e_sdv = inc("vector", lambda e, ci=ci, d=d: e.tensor_tensor(
                    out=h_sb[:, ci, :], in0=sgt[d][:], in1=sut[d][:], op=ALU.mult))
                sgt_rd[d] = e_sdv
            gu_ev_sac[set_] = e_sac
            gu_ev_sdv[set_] = e_sdv
        h_done = sdv.v
        gu_done_sac = sac.v
        dn_ev = {}
        for ms in range(4):
            last = 0
            for grp in range(8):
                gbuf = dbw[grp % 2]
                wait("sync", spe, dn_mm.get(grp % 2, 0))
                for q in range(8):
                    ic = grp * 8 + q
                    dma("sync", gbuf[:, q, :],
                        scr_d[ic * 128:(ic + 1) * 128, ms * 512:(ms + 1) * 512])
                ld_done = sl.v
                wait("tensor", sl, ld_done)
                if grp == 0:
                    wait("tensor", sdv, h_done)
                    wait("tensor", sac, gu_done_sac)
                    for j in range(4):
                        if j in dn_ev:
                            wait("tensor", sac, dn_ev[j])
                for q in range(8):
                    ic = grp * 8 + q
                    for j in range(4):
                        last = inc("tensor", lambda e, j=j, gbuf=gbuf, q=q, ic=ic: e.matmul(
                            out=pg[j][:], lhsT=gbuf[:, q, j * 128:(j + 1) * 128], rhs=h_sb[:, ic, :],
                            start=(ic == 0), stop=(ic == 63)))
                dn_mm[grp % 2] = last
            wait("scalar", spe, last)
            for j in range(4):
                mu = ms * 4 + j
                wait("scalar", ss, fdev_rd[j % 2])
                sac_d = inc("scalar", lambda e, j=j, mu=mu: e.activation(
                    out=fdev[j % 2][:], in_=pg[j][:], func=AF.Identity, scale=ds_sb[:, mu:mu + 1]))
                dn_ev[j] = sac_d
                wait("scalar", sac, sac_d)
                dma("scalar", scr_ffn[mu * 128:(mu + 1) * 128, bsl], fdev[j % 2][:])
                fdev_rd[j % 2] = ss.v
        fst_done = ss.v
        if dbg:
            wait("gpsimd", sdv, h_done)
            sout.v += 16
            emit("gpsimd", lambda e, b=b, s=sout.h: e.dma_start(out=d_hb[b], in_=h_sb[:]).then_inc(s, 16))
        wait("sync", ss, fst_done)
        for tau in range(4):
            t0 = (4 * b + tau) * 128
            wait("sync", sdv, xt3_rd[tau % 2])
            lx = dma("sync", xt3[tau % 2][:], x_h[t0:t0 + 128, :])
            wait("sync", sdv, at3_rd[tau % 2])
            la = dma("sync", at3[tau % 2][:], scr_at[:, t0:t0 + 128], transpose=True)
            lf = dma("sync", ft3[tau % 2][:], scr_ffn[:, t0:t0 + 128], transpose=True)
            wait("vector", sl, lf)
            wait("vector", sout, out3_rd[tau % 2])
            sdv_a = inc("vector", lambda e, tau=tau: e.tensor_tensor(
                out=out3[tau % 2][:], in0=xt3[tau % 2][:], in1=at3[tau % 2][:], op=ALU.add))
            xt3_rd[tau % 2] = sdv_a
            at3_rd[tau % 2] = sdv_a
            wait("vector", sdv, sdv_a)
            sdv_o = inc("vector", lambda e, tau=tau: e.tensor_tensor(
                out=out3[tau % 2][:], in0=out3[tau % 2][:], in1=ft3[tau % 2][:], op=ALU.add))
            wait("gpsimd", sdv, sdv_o)
            sout.v += 16
            emit("gpsimd", lambda e, t0=t0, tau=tau, s=sout.h: e.dma_start(
                out=out_h[t0:t0 + 128, :], in_=out3[tau % 2][:]).then_inc(s, 16))
            out3_rd[tau % 2] = sout.v
        flush_block()

    if dbg:
        sout.v += 16; emit("gpsimd", lambda e, s=sout.h: e.dma_start(out=d_at, in_=scr_at).then_inc(s, 16))
        sout.v += 16; emit("gpsimd", lambda e, s=sout.h: e.dma_start(out=d_ffn, in_=scr_ffn).then_inc(s, 16))
        sout.v += 16; emit("gpsimd", lambda e, s=sout.h: e.dma_start(out=d_h, in_=h_sb[:]).then_inc(s, 16))
    wait("gpsimd", sout, sout.v)
    flush_block()

    es3.close()
    pes3.close()
    es.close()
    return nc


_NC_CACHE = None


def kernel(L0_attn_x, L0_attn_wq, L0_attn_wk, L0_attn_wv, L0_attn_wo,
           L0_ffn_gate_w, L0_ffn_gate_scale, L0_ffn_up_w, L0_ffn_up_scale,
           L0_ffn_down_w, L0_ffn_down_scale):
    global _NC_CACHE
    x = np.ascontiguousarray(np.asarray(L0_attn_x, dtype=np.float32))
    wq = np.ascontiguousarray(np.asarray(L0_attn_wq, dtype=np.float32))
    wk = np.ascontiguousarray(np.asarray(L0_attn_wk, dtype=np.float32))
    wv = np.ascontiguousarray(np.asarray(L0_attn_wv, dtype=np.float32))
    wo = np.ascontiguousarray(np.asarray(L0_attn_wo, dtype=np.float32))
    gw = np.ascontiguousarray(np.asarray(L0_ffn_gate_w, dtype=np.int32))
    uw = np.ascontiguousarray(np.asarray(L0_ffn_up_w, dtype=np.int32))
    dw = np.ascontiguousarray(np.asarray(L0_ffn_down_w, dtype=np.int32))
    gs = np.ascontiguousarray(np.asarray(L0_ffn_gate_scale, dtype=np.float32))
    us = np.ascontiguousarray(np.asarray(L0_ffn_up_scale, dtype=np.float32))
    ds = np.ascontiguousarray(np.asarray(L0_ffn_down_scale, dtype=np.float32))
    if _NC_CACHE is None:
        _NC_CACHE = build_nc()
    nc = _NC_CACHE
    in_maps = []
    for b in range(8):
        in_maps.append(dict(x=x[b], wq=wq, wk=wk, wv=wv, wo=wo,
                            gw=gw, uw=uw, dw=dw, gs=gs, us=us, ds=ds))
    res = run_bass_kernel_spmd(nc, in_maps, core_ids=list(range(8)))
    out = np.stack([np.asarray(r["out"], dtype=np.float32) for r in res.results], axis=0)
    return out


# version 07
# speedup vs baseline: 1.4154x; 1.1343x over previous
import numpy as np
import ml_dtypes
from contextlib import ExitStack

import concourse.bass as bass
import concourse.mybir as mybir
from concourse.bass_utils import run_bass_kernel_spmd

F32 = mybir.dt.float32
BF16 = mybir.dt.bfloat16
I32 = mybir.dt.int32
AF = mybir.ActivationFunctionType
ALU = mybir.AluOpType
bfdt = ml_dtypes.bfloat16

D = 2048
T = 2048
I = 8192
KC = D // 128
NTB = 4
TB = 512
NPAIR = 16


def _consts():
    inv = 1.0 / 10000.0 ** (np.arange(0, 64, 2, dtype=np.float64) / 64.0)
    t = np.arange(T, dtype=np.float64)
    f = np.outer(t, inv)
    COS = np.tile(np.cos(f).T, (4, 1)).astype(np.float32).astype(bfdt)
    sgn = np.where((np.arange(128) // 32) % 2 == 0, -1.0, 1.0)[:, None]
    SINS = (np.tile(np.sin(f).T, (4, 1)) * sgn).astype(np.float32).astype(bfdt)
    m = np.arange(128)
    nb = (m // 64) * 64 + 2 * (m % 32) + ((m % 64) // 32)
    sigma = np.where(m % 64 < 32, m + 32, m - 32)
    PNB = np.zeros((128, 128), dtype=bfdt)
    PNB[nb, m] = 1
    PCMB = np.zeros((128, 128), dtype=bfdt)
    PCMB[nb[sigma], m] = 1
    DUPLO = np.zeros((128, 128), dtype=bfdt)
    DUPLO[m % 64, m] = 1
    DUPHI = np.zeros((128, 128), dtype=bfdt)
    DUPHI[64 + (m % 64), m] = 1
    IDN = np.zeros((128, 128), dtype=bfdt)
    IDN[np.arange(128), np.arange(128)] = 1
    MASKS = []
    tk = np.arange(128)[:, None]
    tq = np.arange(1024)[None, :] % 512
    for v in range(4):
        MASKS.append((tk <= tq - 128 * v).astype(np.float32).astype(bfdt))
    return COS, SINS, PNB, PCMB, DUPLO, DUPHI, IDN, MASKS


def build_nc(dbg=False):
    nc = bass.Bass("TRN2", target_bir_lowering=False)
    x_h = nc.dram_tensor("x", [T, D], F32, kind="ExternalInput").ap()
    wq_h = nc.dram_tensor("wq", [D, D], F32, kind="ExternalInput").ap()
    wk_h = nc.dram_tensor("wk", [512, D], F32, kind="ExternalInput").ap()
    wv_h = nc.dram_tensor("wv", [512, D], F32, kind="ExternalInput").ap()
    wo_h = nc.dram_tensor("wo", [D, D], F32, kind="ExternalInput").ap()
    gw_h = nc.dram_tensor("gw", [I, D], I32, kind="ExternalInput").ap()
    uw_h = nc.dram_tensor("uw", [I, D], I32, kind="ExternalInput").ap()
    dw_h = nc.dram_tensor("dw", [D, I], I32, kind="ExternalInput").ap()
    gs_h = nc.dram_tensor("gs", [I], F32, kind="ExternalInput").ap()
    us_h = nc.dram_tensor("us", [I], F32, kind="ExternalInput").ap()
    ds_h = nc.dram_tensor("ds", [D], F32, kind="ExternalInput").ap()
    out_h = nc.dram_tensor("out", [T, D], F32, kind="ExternalOutput").ap()
    if dbg:
        d_xn = nc.dram_tensor("d_xn", [T, D], BF16, kind="ExternalOutput").ap()
        d_hb = nc.dram_tensor("d_hb", [4, 128, 64, 512], BF16, kind="ExternalOutput").ap()
        d_sd = nc.dram_tensor("d_sd", [4, 128, 64 * 512], BF16, kind="ExternalOutput").ap()
        d_sg2 = nc.dram_tensor("d_sg2", [32, 128, 16 * 256], BF16, kind="ExternalOutput").ap()
        d_ffn = nc.dram_tensor("d_ffn", [D, T], BF16, kind="ExternalOutput").ap()

        d_q = nc.dram_tensor("d_q", [128, 16, T], BF16, kind="ExternalOutput").ap()
        d_kd = nc.dram_tensor("d_kd", [128, 8, T], BF16, kind="ExternalOutput").ap()
        d_v = nc.dram_tensor("d_v", [128, 16, 520], BF16, kind="ExternalOutput").ap()
        d_afm = nc.dram_tensor("d_afm", [128, 16, 512], BF16, kind="ExternalOutput").ap()
        d_at = nc.dram_tensor("d_at", [D, T], BF16, kind="ExternalOutput").ap()
        d_h = nc.dram_tensor("d_h", [128, 64, 512], BF16, kind="ExternalOutput").ap()
    scr_wq = nc.dram_tensor("scr_wq", [4, 128, 16 * 512], BF16).ap()
    scr_wk = nc.dram_tensor("scr_wk", [1, 128, 16 * 512], BF16).ap()
    scr_wv = nc.dram_tensor("scr_wv", [1, 128, 16 * 512], BF16).ap()
    scr_wo = nc.dram_tensor("scr_wo", [4, 128, 16 * 512], BF16).ap()
    scr_g = nc.dram_tensor("scr_g", [32, 128, 16 * 256], BF16).ap()
    scr_u = nc.dram_tensor("scr_u", [32, 128, 16 * 256], BF16).ap()
    scr_d = nc.dram_tensor("scr_d", [4, 128, 64 * 512], BF16).ap()
    scr_xn = nc.dram_tensor("scr_xn", [T, D], BF16).ap()
    scr_at = nc.dram_tensor("scr_at", [D, T], BF16).ap()
    scr_ffn = nc.dram_tensor("scr_ffn", [D, T], BF16).ap()
    COS_c, SINS_c, PNB_c, PCMB_c, DLO_c, DHI_c, IDN_c, MASKS_c = _consts()
    cos_d = nc.inline_tensor(COS_c, name="cos_c").ap()
    sin_d = nc.inline_tensor(SINS_c, name="sin_c").ap()
    pnb_d = nc.inline_tensor(PNB_c, name="pnb_c").ap()
    pcmb_d = nc.inline_tensor(PCMB_c, name="pcmb_c").ap()
    dlo_d = nc.inline_tensor(DLO_c, name="dlo_c").ap()
    dhi_d = nc.inline_tensor(DHI_c, name="dhi_c").ap()
    idn_d = nc.inline_tensor(IDN_c, name="idn_c").ap()
    msk_d = [nc.inline_tensor(MASKS_c[v], name=f"msk{v}_c").ap() for v in range(4)]

    es = ExitStack()

    class S:
        def __init__(self, name):
            self.h = es.enter_context(nc.semaphore(name))
            self.v = 0
    sl = S("sl")
    ss = S("ss")
    sg = S("sg")
    spe = S("spe")
    sac = S("sac")
    sdv = S("sdv")
    spl = S("spl")
    sout = S("sout")

    QSEM = {"sync": sl, "scalar": ss, "gpsimd": sg}
    ESEM = {"tensor": spe, "scalar": sac, "vector": sdv, "gpsimd": spl}

    ENGS = ["sync", "scalar", "vector", "tensor", "gpsimd"]
    queues = {e: [] for e in ENGS}

    def flush_block():
        with nc.Block() as blk:
            for name in ENGS:
                fns = queues[name]
                if fns:
                    getattr(blk, name)(lambda e, fns=fns: [f(e) for f in fns])
        for name in ENGS:
            queues[name] = []

    def emit(eng, f):
        queues[eng].append(f)

    def wait(eng, sem, v=None):
        vv = sem.v if v is None else v
        if vv > 0:
            emit(eng, lambda e, s=sem.h, v=vv: e.wait_ge(s, v))

    def dma(eng, out, in_, **kw):
        sem = QSEM[eng]
        sem.v += 16
        emit(eng, lambda e, o=out, i=in_, k=kw, s=sem.h: e.dma_start(out=o, in_=i, **k).then_inc(s, 16))
        return sem.v

    def inc(eng, call):
        sem = ESEM[eng]
        sem.v += 1
        emit(eng, lambda e, c=call, s=sem.h: c(e).then_inc(s, 1))
        return sem.v

    def op(eng, call):
        emit(eng, call)

    def drain_dmas():
        wait("sync", sl, sl.v)
        wait("scalar", ss, ss.v)
        wait("gpsimd", sg, sg.v)

    def sb_in(stack, name, shape, dt):
        return stack.enter_context(nc.sbuf_tensor(name, shape, dt))

    COS = sb_in(es, "COS", [128, T], BF16)
    SINS = sb_in(es, "SINS", [128, T], BF16)
    PNB = sb_in(es, "PNB", [128, 128], BF16)
    PCMB = sb_in(es, "PCMB", [128, 128], BF16)
    DLO = sb_in(es, "DLO", [128, 128], BF16)
    DHI = sb_in(es, "DHI", [128, 128], BF16)
    IDN = sb_in(es, "IDN", [128, 128], BF16)
    MSK = [sb_in(es, f"MSK{v}", [128, 1024], BF16) for v in range(4)]
    gs_sb = sb_in(es, "gs_sb", [128, 64], F32)
    us_sb = sb_in(es, "us_sb", [128, 64], F32)
    ds_sb = sb_in(es, "ds_sb", [128, 16], F32)
    ones64 = sb_in(es, "ones64", [1, 64], BF16)
    epsv = sb_in(es, "epsv", [128, 1], F32)
    n128v = sb_in(es, "n128v", [128, 1], F32)

    es0 = ExitStack()
    stg32 = [sb_in(es0, f"stg32_{i}", [128, I], I32) for i in range(2)]
    stg16 = [sb_in(es0, f"stg16_{i}", [128, I], BF16) for i in range(2)]
    stgT = sb_in(es0, "stgT", [128, 64, 512], BF16)
    pes0 = ExitStack()
    pst = [pes0.enter_context(nc.psum_tensor(f"pst{i}", [128, 128], BF16)) for i in range(4)]
    st32_rd = [0, 0]
    st16_rd = [0, 0]
    st16_wr = [0, 0]
    pst_rd = [0, 0, 0, 0]
    stgT_wr = [0]

    op("gpsimd", lambda e: e.memset(ones64[:], 1.0))
    op("gpsimd", lambda e: e.memset(epsv[:], 1e-5))
    op("gpsimd", lambda e: e.memset(n128v[:], -128.0))
    dma("sync", COS[:], cos_d)
    dma("sync", SINS[:], sin_d)
    dma("sync", PNB[:], pnb_d)
    dma("sync", PCMB[:], pcmb_d)
    dma("sync", DLO[:], dlo_d)
    dma("sync", DHI[:], dhi_d)
    dma("sync", IDN[:], idn_d)
    for v in range(4):
        dma("sync", MSK[v][:], msk_d[v])
    for c in range(64):
        dma("sync", gs_sb[:, c:c + 1], gs_h[c * 128:(c + 1) * 128][:, None])
        dma("sync", us_sb[:, c:c + 1], us_h[c * 128:(c + 1) * 128][:, None])
    for c in range(16):
        dma("sync", ds_sb[:, c:c + 1], ds_h[c * 128:(c + 1) * 128][:, None])
    wait("tensor", sl, sl.v)

    def prep_weight(src_h, n_chunks, W, scr_T, sub128, src_dtype, span=512):
        kcn = W // 128
        stgT2 = stgT[:, 0:kcn, :].rearrange("p x y -> p (x y)").rearrange("p (a b c) -> p a b c", a=2, b=kcn, c=256) if span == 256 else None
        for c in range(n_chunks):
            i = c % 2
            s32 = stg32[i].ap().bitcast(src_dtype)[:, 0:W] if src_dtype != I32 else stg32[i][:, 0:W]
            s16 = stg16[i]
            wait("gpsimd", sac, st32_rd[i])
            dma("gpsimd", s32[:, :], src_h[c * 128:(c + 1) * 128, :])
            wait("scalar", sg)
            wait("scalar", spe, st16_rd[i])
            if sub128:
                st32_rd[i] = inc("scalar", lambda e, s32=s32, s16=s16, W=W: e.activation(
                    out=s16[:, 0:W], in_=s32, func=AF.Identity, bias=n128v[:]))
            else:
                st32_rd[i] = inc("scalar", lambda e, s32=s32, s16=s16, W=W: e.activation(
                    out=s16[:, 0:W], in_=s32, func=AF.Copy))
            cast_done = st32_rd[i]
            wait("tensor", sac, cast_done)
            co = (c % 4) * 128
            last_tp = 0
            for kc in range(kcn):
                p = pst[kc % 4]
                wait("tensor", sdv, pst_rd[kc % 4])
                last_tp = inc("tensor", lambda e, p=p, s16=s16, kc=kc: e.transpose(
                    out=p[:], in_=s16[:, kc * 128:(kc + 1) * 128], identity=IDN[:]))
                wait("vector", spe, last_tp)
                wait("vector", sl, stgT_wr[0])
                if span == 512:
                    dst = stgT[:, kc, co:co + 128]
                else:
                    dst = stgT2[:, co // 256, kc, (co % 256):(co % 256) + 128]
                pst_rd[kc % 4] = inc("vector", lambda e, p=p, dst=dst: e.tensor_copy(
                    out=dst, in_=p[:]))
            st16_rd[i] = last_tp
            if c % 4 == 3:
                g = c // 4
                wait("sync", sdv)
                if span == 512:
                    dma("sync", scr_T[g], stgT[:, 0:kcn, 0:512])
                else:
                    dma("sync", scr_T[2 * g], stgT2[:, 0])
                    dma("sync", scr_T[2 * g + 1], stgT2[:, 1])
                stgT_wr[0] = sl.v

    prep_weight(wq_h, 16, D, scr_wq, False, F32)
    prep_weight(wk_h, 4, D, scr_wk, False, F32)
    prep_weight(wv_h, 4, D, scr_wv, False, F32)
    prep_weight(wo_h, 16, D, scr_wo, False, F32)
    prep_weight(gw_h, 64, D, scr_g, True, I32, span=256)
    prep_weight(uw_h, 64, D, scr_u, True, I32, span=256)
    prep_weight(dw_h, 16, I, scr_d, True, I32)
    drain_dmas()
    flush_block()
    pes0.close()
    es0.close()

    es1 = ExitStack()
    q_sb = sb_in(es1, "q_sb", [128, 16, T], BF16)
    kd_sb = sb_in(es1, "kd_sb", [128, 8, T], BF16)
    v_sb = sb_in(es1, "v_sb", [128, 16, 520], BF16)

    es1b = ExitStack()
    xt = [sb_in(es1b, f"xt{i}", [128, D], F32) for i in range(2)]
    xn_tm = sb_in(es1b, "xn_tm", [128, D], BF16)
    xn_fm = sb_in(es1b, "xn_fm", [128, KC, TB], BF16)
    sumv = sb_in(es1b, "sumv", [128, 1], F32)
    sqv = sb_in(es1b, "sqv", [128, 1], F32)
    mv = sb_in(es1b, "mv", [128, 1], F32)
    ex2v = sb_in(es1b, "ex2v", [128, 1], F32)
    varv = sb_in(es1b, "varv", [128, 1], F32)
    rstdv = sb_in(es1b, "rstdv", [128, 1], F32)
    nmrv = sb_in(es1b, "nmrv", [128, 1], F32)
    wbuf = [sb_in(es1b, f"wbuf{i}", [128, 16, 512], BF16) for i in range(2)]
    qtmp = [sb_in(es1b, f"qtmp{i}", [128, 512], BF16) for i in range(2)]
    t1b = sb_in(es1b, "t1b", [128, 512], BF16)
    t2b = sb_in(es1b, "t2b", [128, 512], BF16)
    krope = sb_in(es1b, "krope", [128, 4, 512], BF16)

    pes = ExitStack()
    pp = [pes.enter_context(nc.psum_tensor(f"pp{i}", [128, 512], F32)) for i in range(2)]
    pa_ps = pes.enter_context(nc.psum_tensor("pa_ps", [128, 512], F32))
    pb_ps = pes.enter_context(nc.psum_tensor("pb_ps", [128, 512], F32))

    for tt in range(16):
        for j in range(8):
            op("vector", lambda e, tt=tt, j=j: e.memset(v_sb[:, tt, j * 65 + 64:j * 65 + 65], 1.0))

    xt_rd = [0, 0]
    xntm_rd = [0]
    sqv_rd = [0]
    qtmp_st = [(0, 0), (0, 0)]
    pp_rd = [0, 0]
    pa_rd = [0]
    pb_rd = [0]
    rope_flip = [0]

    def ln_tile(b, tau):
        t0 = (4 * b + tau) * 128
        xe = xt[tau % 2]
        wait("sync", sac, xt_rd[tau % 2])
        lv = dma("sync", xe[:], x_h[t0:t0 + 128, :])
        wait("vector", sl, lv)
        sdv_red = inc("vector", lambda e, xe=xe: e.reduce_sum(sumv[:], xe[:], axis=mybir.AxisListType.X))
        wait("scalar", sl, lv)
        wait("scalar", sdv, sqv_rd[0])
        wait("scalar", ss, xntm_rd[0])
        sac_sq = inc("scalar", lambda e, xe=xe: e.activation(
            out=xn_tm[:], in_=xe[:], func=AF.Square, accum_out=sqv[:]))
        wait("vector", sdv, sdv_red)
        sdv_mv = inc("vector", lambda e: e.tensor_scalar_mul(out=mv[:], in0=sumv[:], scalar1=1.0 / D))
        wait("vector", sac, sac_sq)
        sqv_rd[0] = inc("vector", lambda e: e.tensor_scalar_mul(out=ex2v[:], in0=sqv[:], scalar1=1.0 / D))
        wait("vector", sdv, sdv_mv)
        sdv_vm = inc("vector", lambda e: e.tensor_tensor(out=varv[:], in0=mv[:], in1=mv[:], op=ALU.mult))
        wait("vector", sdv, sdv_vm)
        sdv_var = inc("vector", lambda e: e.tensor_tensor(out=varv[:], in0=ex2v[:], in1=varv[:], op=ALU.subtract))
        wait("scalar", sdv, sdv_var)
        sac_ln1 = inc("scalar", lambda e: e.activation(out=rstdv[:], in_=varv[:], func=AF.Ln, bias=epsv[:]))
        wait("scalar", sac, sac_ln1)
        sac_r = inc("scalar", lambda e: e.activation(out=rstdv[:], in_=rstdv[:], func=AF.Exp, scale=-0.5))
        wait("vector", sac, sac_r)
        sdv_nmr = inc("vector", lambda e: e.tensor_tensor(out=nmrv[:], in0=mv[:], in1=rstdv[:], op=ALU.mult))
        wait("vector", sdv, sdv_nmr)
        sdv_nmr2 = inc("vector", lambda e: e.tensor_scalar_mul(out=nmrv[:], in0=nmrv[:], scalar1=-1.0))
        wait("scalar", sdv, sdv_nmr2)
        sac_xn = inc("scalar", lambda e, xe=xe: e.activation(
            out=xn_tm[:], in_=xe[:], func=AF.Identity, scale=rstdv[:], bias=nmrv[:]))
        xt_rd[tau % 2] = sac_xn
        wait("scalar", sac, sac_xn)
        dma("scalar", scr_xn[t0:t0 + 128, :], xn_tm[:])
        xntm_rd[0] = ss.v

    def rope_chunk(b, ps_src, ps_idx, dst_ap):
        fl = rope_flip[0]
        rope_flip[0] ^= 1
        qt = qtmp[fl]
        (g_spe, g_sdv) = qtmp_st[fl]
        wait("scalar", spe, g_spe)
        wait("scalar", sdv, g_sdv)
        sac_ev = inc("scalar", lambda e, qt=qt, p=ps_src: e.activation(out=qt[:], in_=p[:], func=AF.Copy))
        if ps_idx is not None:
            pp_rd[ps_idx] = sac_ev
        wait("tensor", sac, sac_ev)
        wait("tensor", sdv, pa_rd[0])
        spe_a = inc("tensor", lambda e, qt=qt: e.matmul(out=pa_ps[:], lhsT=PNB[:], rhs=qt[:],
                                                        start=True, stop=True))
        wait("tensor", sdv, pb_rd[0])
        spe_b = inc("tensor", lambda e, qt=qt: e.matmul(out=pb_ps[:], lhsT=PCMB[:], rhs=qt[:],
                                                        start=True, stop=True))
        bsl = slice(b * TB, (b + 1) * TB)
        wait("vector", spe, spe_a)
        sdv_t1 = inc("vector", lambda e, bsl=bsl: e.tensor_tensor(
            out=t1b[:], in0=pa_ps[:], in1=COS[:, bsl], op=ALU.mult))
        pa_rd[0] = sdv_t1
        wait("vector", spe, spe_b)
        sdv_t2 = inc("vector", lambda e, bsl=bsl: e.tensor_tensor(
            out=t2b[:], in0=pb_ps[:], in1=SINS[:, bsl], op=ALU.mult))
        pb_rd[0] = sdv_t2
        wait("vector", sdv, sdv_t2)
        sdv_o = inc("vector", lambda e, d=dst_ap: e.tensor_tensor(out=d, in0=t1b[:], in1=t2b[:], op=ALU.add))
        qtmp_st[fl] = (spe_a if False else spe_b, sdv_t1)
        return sac_ev, sdv_o

    wbuf_rd = [0, 0]

    for b in range(NTB):
        bsl = slice(b * TB, (b + 1) * TB)
        for tau in range(4):
            ln_tile(b, tau)
        wait("sync", ss, ss.v)
        for c in range(KC):
            dma("sync", xn_fm[:, c, :], scr_xn[bsl, c * 128:(c + 1) * 128], transpose=True)
        xn_ready = sl.v
        for s in range(4):
            wb = wbuf[s % 2]
            wait("sync", spe, wbuf_rd[s % 2])
            dma("sync", wb[:, :, :], scr_wq[s])
            ld_done = sl.v
            for mj in range(4):
                mu = s * 4 + mj
                pe = pp[mu % 2]
                wait("tensor", sl, ld_done)
                wait("tensor", sl, xn_ready)
                wait("tensor", sac, pp_rd[mu % 2])
                last = 0
                for k in range(KC):
                    last = inc("tensor", lambda e, k=k, pe=pe, wb=wb, mj=mj: e.matmul(
                        out=pe[:], lhsT=wb[:, k, mj * 128:(mj + 1) * 128], rhs=xn_fm[:, k, :],
                        start=(k == 0), stop=(k == KC - 1)))
                wbuf_rd[s % 2] = last
                wait("scalar", spe, last)
                rope_chunk(b, pe, mu % 2, q_sb[:, mu, bsl])
        wbk = wbuf[0]
        wait("sync", spe, wbuf_rd[0])
        dma("sync", wbk[:, :, :], scr_wk[0])
        ld_done = sl.v
        for mu in range(4):
            pe = pp[mu % 2]
            wait("tensor", sl, ld_done)
            wait("tensor", sl, xn_ready)
            wait("tensor", sac, pp_rd[mu % 2])
            last = 0
            for k in range(KC):
                last = inc("tensor", lambda e, k=k, pe=pe, mu=mu: e.matmul(
                    out=pe[:], lhsT=wbk[:, k, mu * 128:(mu + 1) * 128], rhs=xn_fm[:, k, :],
                    start=(k == 0), stop=(k == KC - 1)))
            wbuf_rd[0] = last
            wait("scalar", spe, last)
            _, sdv_o = rope_chunk(b, pe, mu % 2, krope[:, mu, :])
            wait("tensor", sdv, sdv_o)
            for half, PM in ((0, DLO), (1, DHI)):
                wait("tensor", sdv, pa_rd[0])
                spe_d = inc("tensor", lambda e, PM=PM, mu=mu: e.matmul(
                    out=pa_ps[:], lhsT=PM[:], rhs=krope[:, mu, :], start=True, stop=True))
                wait("scalar", spe, spe_d)
                sac_e = inc("scalar", lambda e, mu=mu, half=half, bsl=bsl: e.activation(
                    out=kd_sb[:, 2 * mu + half, bsl], in_=pa_ps[:], func=AF.Copy))
                pa_rd[0] = 0
                wait("tensor", sac, sac_e)
        wbv = wbuf[1]
        wait("sync", spe, wbuf_rd[1])
        dma("sync", wbv[:, :, :], scr_wv[0])
        ld_done = sl.v
        for tau in range(4):
            pe = pp[tau % 2]
            wait("tensor", sl, ld_done)
            wait("tensor", sl, xn_ready)
            wait("tensor", sac, pp_rd[tau % 2])
            last = 0
            for k in range(KC):
                last = inc("tensor", lambda e, k=k, pe=pe, tau=tau, wbv=wbv: e.matmul(
                    out=pe[:], lhsT=xn_fm[:, k, tau * 128:(tau + 1) * 128], rhs=wbv[:, k, 0:512],
                    start=(k == 0), stop=(k == KC - 1)))
            wbuf_rd[1] = last
            wait("vector", spe, last)
            vout = v_sb[:, 4 * b + tau, 0:520].rearrange("p (h x) -> p h x", h=8)[:, :, 0:64]
            vin = pe[:].rearrange("p (h x) -> p h x", h=8)
            sdv_v = inc("vector", lambda e, vout=vout, vin=vin: e.tensor_copy(out=vout, in_=vin))
            wait("tensor", sdv, sdv_v)
            pp_rd[tau % 2] = 0
        if b == NTB - 1:
            drain_dmas()
        flush_block()

    es1b.close()
    pes.close()

    es2 = ExitStack()
    wob = [sb_in(es2, f"wob{i}", [128, 16, 512], BF16) for i in range(2)]
    pbuf = [sb_in(es2, f"pbuf{i}", [128, 1024], BF16) for i in range(2)]
    afm = sb_in(es2, "afm", [128, 16, 512], BF16)
    lnv2 = sb_in(es2, "lnv2", [1, 512], F32)
    rbf = sb_in(es2, "rbf", [1, 512], BF16)
    rb64 = sb_in(es2, "rb64", [64, 512], BF16)
    tmp64 = sb_in(es2, "tmp64", [64, 512], BF16)
    woev = [sb_in(es2, f"woev{i}", [128, 512], BF16) for i in range(2)]

    pes2 = ExitStack()
    psc = [pes2.enter_context(nc.psum_tensor(f"psc{i}", [128, 1024], F32)) for i in range(2)]
    pav = [pes2.enter_context(nc.psum_tensor(f"pav{i}", [128, 512], F32)) for i in range(4)]

    norm_state = {"rbf_rd": 0, "rb64_rd": 0}

    for B in range(NTB):
        if dbg and B == 0:
            sout.v += 16; emit("gpsimd", lambda e, s=sout.h: e.dma_start(out=d_xn, in_=scr_xn).then_inc(s, 16))
            sout.v += 16; emit("gpsimd", lambda e, s=sout.h: e.dma_start(out=d_sd, in_=scr_d).then_inc(s, 16))
            sout.v += 16; emit("gpsimd", lambda e, s=sout.h: e.dma_start(out=d_sg2, in_=scr_g).then_inc(s, 16))


            sout.v += 16; emit("gpsimd", lambda e, s=sout.h: e.dma_start(out=d_q, in_=q_sb[:]).then_inc(s, 16))
            sout.v += 16; emit("gpsimd", lambda e, s=sout.h: e.dma_start(out=d_kd, in_=kd_sb[:]).then_inc(s, 16))
            sout.v += 16; emit("gpsimd", lambda e, s=sout.h: e.dma_start(out=d_v, in_=v_sb[:]).then_inc(s, 16))
        Bsl = slice(B * TB, (B + 1) * TB)
        nck = 4 * B + 4
        for s in range(2):
            dma("sync", wob[s][:, :, :], scr_wo[s])
        wo_ld01 = sl.v

        exp_read = {}
        av_tt = {}
        av_last = {}
        norm_q = []

        def do_norm(e_pair, pidx):
            for hh in range(2):
                pa = pav[pidx + hh]
                sac_ln = inc("scalar", lambda e, pa=pa: e.activation(
                    out=lnv2[:], in_=pa[64:65, :], func=AF.Ln))
                wait("scalar", spe, norm_state["rbf_rd"])
                wait("scalar", sac, sac_ln)
                sac_r = inc("scalar", lambda e: e.activation(
                    out=rbf[:], in_=lnv2[:], func=AF.Exp, scale=-1.0))
                wait("tensor", sac, sac_r)
                spe_b = inc("tensor", lambda e, pa=pa: e.matmul(
                    out=pa[64:128, :], lhsT=ones64[:], rhs=rbf[:], start=True, stop=True))
                norm_state["rbf_rd"] = spe_b
                wait("scalar", spe, spe_b)
                wait("scalar", sdv, norm_state["rb64_rd"])
                sac_rb = inc("scalar", lambda e, pa=pa: e.activation(
                    out=rb64[:], in_=pa[64:128, :], func=AF.Copy))
                wait("vector", sac, sac_rb)
                if hh == 0:
                    sdv_tt = inc("vector", lambda e, pa=pa, ep=e_pair: e.tensor_tensor(
                        out=afm[0:64, ep, :], in0=pa[0:64, :], in1=rb64[:], op=ALU.mult))
                else:
                    sdv_t64 = inc("vector", lambda e, pa=pa: e.tensor_tensor(
                        out=tmp64[:], in0=pa[0:64, :], in1=rb64[:], op=ALU.mult))
                    wait("vector", sdv, sdv_t64)
                    sdv_tt = inc("vector", lambda e, ep=e_pair: e.tensor_scalar_mul(
                        out=afm[64:128, ep, :], in0=tmp64[:], scalar1=1.0))
                norm_state["rb64_rd"] = sdv_tt
                av_tt[pidx + hh] = sdv_tt

        for e_pair in range(NPAIR):
            j = e_pair // 2
            pidx = (e_pair % 2) * 2
            for hh in range(2):
                if (pidx + hh) in av_tt:
                    wait("tensor", sdv, av_tt[pidx + hh])
            for c in range(nck):
                ps = psc[c % 2]
                if (c % 2) in exp_read:
                    wait("tensor", sac, exp_read[c % 2])
                inc("tensor", lambda e, ps=ps, j=j, c=c, ep=e_pair, Bsl=Bsl: e.matmul(
                    out=ps[:, 0:512], lhsT=kd_sb[0:64, j, c * 128:(c + 1) * 128],
                    rhs=q_sb[0:64, ep, Bsl], start=True, stop=True))
                spe_s = inc("tensor", lambda e, ps=ps, j=j, c=c, ep=e_pair, Bsl=Bsl: e.matmul(
                    out=ps[:, 512:1024], lhsT=kd_sb[64:128, j, c * 128:(c + 1) * 128],
                    rhs=q_sb[64:128, ep, Bsl], start=True, stop=True))
                wait("scalar", spe, spe_s)
                if (c % 2) in av_last:
                    wait("scalar", spe, av_last[c % 2])
                sac_e = inc("scalar", lambda e, ps=ps, c=c: e.activation(
                    out=pbuf[c % 2][:], in_=ps[:], func=AF.Exp, scale=0.125))
                exp_read[c % 2] = sac_e
                dep_sem, dep_v = sac, sac_e
                if c >= 4 * B:
                    wait("gpsimd", sac, sac_e)
                    spl_m = inc("gpsimd", lambda e, c=c, B=B: e.tensor_tensor(
                        out=pbuf[c % 2][:], in0=pbuf[c % 2][:], in1=MSK[c - 4 * B][:], op=ALU.mult))
                    dep_sem, dep_v = spl, spl_m
                wait("tensor", dep_sem, dep_v)
                inc("tensor", lambda e, c=c, j=j, pidx=pidx, nck=nck: e.matmul(
                    out=pav[pidx][0:65, :], lhsT=v_sb[:, c, j * 65:(j + 1) * 65],
                    rhs=pbuf[c % 2][:, 0:512], start=(c == 0), stop=(c == nck - 1)))
                spe_av = inc("tensor", lambda e, c=c, j=j, pidx=pidx, nck=nck: e.matmul(
                    out=pav[pidx + 1][0:65, :], lhsT=v_sb[:, c, j * 65:(j + 1) * 65],
                    rhs=pbuf[c % 2][:, 512:1024], start=(c == 0), stop=(c == nck - 1)))
                av_last[c % 2] = spe_av
            if norm_q:
                norm_q.pop(0)()
            norm_q.append(lambda ep=e_pair, pidx=pidx: do_norm(ep, pidx))
        while norm_q:
            norm_q.pop(0)()
        attn_done = sdv.v
        woev_rd = [0, 0]
        wob_rd = [0, 0]
        for s in range(4):
            wot = wob[s % 2]
            if s >= 2:
                wait("sync", spe, wob_rd[s % 2])
                dma("sync", wot[:, :, :], scr_wo[s])
                ld_done = sl.v
            else:
                ld_done = wo_ld01
            for mj in range(4):
                mu = s * 4 + mj
                pw = psc[mu % 2]
                wait("tensor", sl, ld_done)
                wait("tensor", sdv, attn_done)
                if (mu % 2) in exp_read:
                    wait("tensor", sac, exp_read.pop(mu % 2))
                last = 0
                for dv in range(16):
                    last = inc("tensor", lambda e, dv=dv, pw=pw, wot=wot, mj=mj: e.matmul(
                        out=pw[:, 0:512], lhsT=wot[:, dv, mj * 128:(mj + 1) * 128], rhs=afm[:, dv, :],
                        start=(dv == 0), stop=(dv == 15)))
                wob_rd[s % 2] = last
                wait("scalar", spe, last)
                wait("scalar", ss, woev_rd[mu % 2])
                sac_ev = inc("scalar", lambda e, pw=pw, mu=mu: e.activation(
                    out=woev[mu % 2][:], in_=pw[:, 0:512], func=AF.Copy))
                exp_read[mu % 2] = sac_ev
                wait("scalar", sac, sac_ev)
                dma("scalar", scr_at[mu * 128:(mu + 1) * 128, Bsl], woev[mu % 2][:])
                woev_rd[mu % 2] = ss.v
        if dbg and B == NTB - 1:
            wait("gpsimd", sdv, attn_done)
            sout.v += 16; emit("gpsimd", lambda e, s=sout.h: e.dma_start(out=d_afm, in_=afm[:]).then_inc(s, 16))
        if B == NTB - 1:
            drain_dmas()
        flush_block()

    pes2.close()
    es2.close()
    es1.close()

    es3 = ExitStack()
    h_sb = sb_in(es3, "h_sb", [128, 64, 512], BF16)
    xnf = sb_in(es3, "xnf", [128, KC, 512], BF16)
    gwide = [sb_in(es3, f"gwide{i}", [128, KC, 256], BF16) for i in range(2)]
    uwide = [sb_in(es3, f"uwide{i}", [128, KC, 256], BF16) for i in range(2)]
    dbw = [sb_in(es3, f"dbw{i}", [128, 8, 512], BF16) for i in range(2)]
    sgt = [sb_in(es3, f"sgt{i}", [128, 512], BF16) for i in range(2)]
    sut = [sb_in(es3, f"sut{i}", [128, 512], BF16) for i in range(2)]
    fdev = [sb_in(es3, f"fdev{i}", [128, 512], BF16) for i in range(2)]
    xt3 = [sb_in(es3, f"xt3{i}", [128, D], F32) for i in range(2)]
    at3 = [sb_in(es3, f"at3{i}", [128, D], BF16) for i in range(2)]
    ft3 = [sb_in(es3, f"ft3{i}", [128, D], BF16) for i in range(2)]
    out3 = [sb_in(es3, f"out3{i}", [128, D], F32) for i in range(2)]

    pes3 = ExitStack()
    pg = [pes3.enter_context(nc.psum_tensor(f"pg{i}", [128, 512], F32)) for i in range(4)]
    pu = [pes3.enter_context(nc.psum_tensor(f"pu{i}", [128, 512], F32)) for i in range(4)]

    gu_mm = [0, 0]
    sgt_rd = [0, 0]
    gu_ev_sac = [0, 0]
    gu_ev_sdv = [0, 0]
    dn_mm = {}
    out3_rd = [0, 0]
    xt3_rd = [0, 0]
    at3_rd = [0, 0]
    fdev_rd = [0, 0]

    for b in range(NTB):
        bsl = slice(b * TB, (b + 1) * TB)
        for c in range(KC):
            dma("sync", xnf[:, c, :], scr_xn[bsl, c * 128:(c + 1) * 128], transpose=True)
        xn_ld = sl.v
        for sp in range(32):
            set_ = sp % 2
            gw_t, uw_t = gwide[set_], uwide[set_]
            wait("sync", spe, gu_mm[set_])
            dma("sync", gw_t[:, :, :], scr_g[sp])
            dma("sync", uw_t[:, :, :], scr_u[sp])
            ld_done = sl.v
            pga, pgb_ = pg[2 * set_], pg[2 * set_ + 1]
            pua, pub = pu[2 * set_], pu[2 * set_ + 1]
            wait("tensor", sl, ld_done)
            wait("tensor", sl, xn_ld)
            wait("tensor", sac, gu_ev_sac[set_])
            wait("tensor", sdv, gu_ev_sdv[set_])
            last = 0
            for k in range(KC):
                inc("tensor", lambda e, k=k, gw_t=gw_t, pga=pga: e.matmul(
                    out=pga[:], lhsT=gw_t[:, k, 0:128], rhs=xnf[:, k, :],
                    start=(k == 0), stop=(k == KC - 1)))
                inc("tensor", lambda e, k=k, gw_t=gw_t, pgb_=pgb_: e.matmul(
                    out=pgb_[:], lhsT=gw_t[:, k, 128:256], rhs=xnf[:, k, :],
                    start=(k == 0), stop=(k == KC - 1)))
                inc("tensor", lambda e, k=k, uw_t=uw_t, pua=pua: e.matmul(
                    out=pua[:], lhsT=uw_t[:, k, 0:128], rhs=xnf[:, k, :],
                    start=(k == 0), stop=(k == KC - 1)))
                last = inc("tensor", lambda e, k=k, uw_t=uw_t, pub=pub: e.matmul(
                    out=pub[:], lhsT=uw_t[:, k, 128:256], rhs=xnf[:, k, :],
                    start=(k == 0), stop=(k == KC - 1)))
            gu_mm[set_] = last
            wait("scalar", spe, last)
            wait("vector", spe, last)
            e_sac = e_sdv = 0
            for d in range(2):
                ci = sp * 2 + d
                pgx = pga if d == 0 else pgb_
                pux = pua if d == 0 else pub
                wait("scalar", sdv, sgt_rd[d])
                e_sac = inc("scalar", lambda e, pgx=pgx, ci=ci, d=d: e.activation(
                    out=sgt[d][:], in_=pgx[:], func=AF.Silu, scale=gs_sb[:, ci:ci + 1]))
                e_sdv0 = inc("vector", lambda e, pux=pux, ci=ci, d=d: e.tensor_single_scalar(
                    out=sut[d][:], in_=pux[:], scalar=us_sb[:, ci:ci + 1], op=ALU.mult))
                wait("vector", sac, e_sac)
                wait("vector", sdv, e_sdv0)
                e_sdv = inc("vector", lambda e, ci=ci, d=d: e.tensor_tensor(
                    out=h_sb[:, ci, :], in0=sgt[d][:], in1=sut[d][:], op=ALU.mult))
                sgt_rd[d] = e_sdv
            gu_ev_sac[set_] = e_sac
            gu_ev_sdv[set_] = e_sdv
        h_done = sdv.v
        gu_done_sac = sac.v
        dn_ev = {}
        for ms in range(4):
            last = 0
            for grp in range(8):
                gbuf = dbw[grp % 2]
                wait("sync", spe, dn_mm.get(grp % 2, 0))
                dma("sync", gbuf[:, :, :],
                    scr_d[ms][:, grp * 8 * 512:(grp + 1) * 8 * 512])
                ld_done = sl.v
                wait("tensor", sl, ld_done)
                if grp == 0:
                    wait("tensor", sdv, h_done)
                    wait("tensor", sac, gu_done_sac)
                    for j in range(4):
                        if j in dn_ev:
                            wait("tensor", sac, dn_ev[j])
                for q in range(8):
                    ic = grp * 8 + q
                    for j in range(4):
                        last = inc("tensor", lambda e, j=j, gbuf=gbuf, q=q, ic=ic: e.matmul(
                            out=pg[j][:], lhsT=gbuf[:, q, j * 128:(j + 1) * 128], rhs=h_sb[:, ic, :],
                            start=(ic == 0), stop=(ic == 63)))
                dn_mm[grp % 2] = last
            wait("scalar", spe, last)
            for j in range(4):
                mu = ms * 4 + j
                wait("scalar", ss, fdev_rd[j % 2])
                sac_d = inc("scalar", lambda e, j=j, mu=mu: e.activation(
                    out=fdev[j % 2][:], in_=pg[j][:], func=AF.Identity, scale=ds_sb[:, mu:mu + 1]))
                dn_ev[j] = sac_d
                wait("scalar", sac, sac_d)
                dma("scalar", scr_ffn[mu * 128:(mu + 1) * 128, bsl], fdev[j % 2][:])
                fdev_rd[j % 2] = ss.v
        fst_done = ss.v
        if dbg:
            wait("gpsimd", sdv, h_done)
            sout.v += 16
            emit("gpsimd", lambda e, b=b, s=sout.h: e.dma_start(out=d_hb[b], in_=h_sb[:]).then_inc(s, 16))
        wait("sync", ss, fst_done)
        for tau in range(4):
            t0 = (4 * b + tau) * 128
            wait("sync", sdv, xt3_rd[tau % 2])
            lx = dma("sync", xt3[tau % 2][:], x_h[t0:t0 + 128, :])
            wait("sync", sdv, at3_rd[tau % 2])
            la = dma("sync", at3[tau % 2][:], scr_at[:, t0:t0 + 128], transpose=True)
            lf = dma("sync", ft3[tau % 2][:], scr_ffn[:, t0:t0 + 128], transpose=True)
            wait("vector", sl, lf)
            wait("vector", sout, out3_rd[tau % 2])
            sdv_a = inc("vector", lambda e, tau=tau: e.tensor_tensor(
                out=out3[tau % 2][:], in0=xt3[tau % 2][:], in1=at3[tau % 2][:], op=ALU.add))
            xt3_rd[tau % 2] = sdv_a
            at3_rd[tau % 2] = sdv_a
            wait("vector", sdv, sdv_a)
            sdv_o = inc("vector", lambda e, tau=tau: e.tensor_tensor(
                out=out3[tau % 2][:], in0=out3[tau % 2][:], in1=ft3[tau % 2][:], op=ALU.add))
            wait("gpsimd", sdv, sdv_o)
            sout.v += 16
            emit("gpsimd", lambda e, t0=t0, tau=tau, s=sout.h: e.dma_start(
                out=out_h[t0:t0 + 128, :], in_=out3[tau % 2][:]).then_inc(s, 16))
            out3_rd[tau % 2] = sout.v
        flush_block()

    if dbg:
        sout.v += 16; emit("gpsimd", lambda e, s=sout.h: e.dma_start(out=d_at, in_=scr_at).then_inc(s, 16))
        sout.v += 16; emit("gpsimd", lambda e, s=sout.h: e.dma_start(out=d_ffn, in_=scr_ffn).then_inc(s, 16))
        sout.v += 16; emit("gpsimd", lambda e, s=sout.h: e.dma_start(out=d_h, in_=h_sb[:]).then_inc(s, 16))
    wait("gpsimd", sout, sout.v)
    flush_block()

    es3.close()
    pes3.close()
    es.close()
    return nc


_NC_CACHE = None


def kernel(L0_attn_x, L0_attn_wq, L0_attn_wk, L0_attn_wv, L0_attn_wo,
           L0_ffn_gate_w, L0_ffn_gate_scale, L0_ffn_up_w, L0_ffn_up_scale,
           L0_ffn_down_w, L0_ffn_down_scale):
    global _NC_CACHE
    x = np.ascontiguousarray(np.asarray(L0_attn_x, dtype=np.float32))
    wq = np.ascontiguousarray(np.asarray(L0_attn_wq, dtype=np.float32))
    wk = np.ascontiguousarray(np.asarray(L0_attn_wk, dtype=np.float32))
    wv = np.ascontiguousarray(np.asarray(L0_attn_wv, dtype=np.float32))
    wo = np.ascontiguousarray(np.asarray(L0_attn_wo, dtype=np.float32))
    gw = np.ascontiguousarray(np.asarray(L0_ffn_gate_w, dtype=np.int32))
    uw = np.ascontiguousarray(np.asarray(L0_ffn_up_w, dtype=np.int32))
    dw = np.ascontiguousarray(np.asarray(L0_ffn_down_w, dtype=np.int32))
    gs = np.ascontiguousarray(np.asarray(L0_ffn_gate_scale, dtype=np.float32))
    us = np.ascontiguousarray(np.asarray(L0_ffn_up_scale, dtype=np.float32))
    ds = np.ascontiguousarray(np.asarray(L0_ffn_down_scale, dtype=np.float32))
    if _NC_CACHE is None:
        _NC_CACHE = build_nc()
    nc = _NC_CACHE
    in_maps = []
    for b in range(8):
        in_maps.append(dict(x=x[b], wq=wq, wk=wk, wv=wv, wo=wo,
                            gw=gw, uw=uw, dw=dw, gs=gs, us=us, ds=ds))
    res = run_bass_kernel_spmd(nc, in_maps, core_ids=list(range(8)))
    out = np.stack([np.asarray(r["out"], dtype=np.float32) for r in res.results], axis=0)
    return out


# version 08
# speedup vs baseline: 1.6832x; 1.1892x over previous
import numpy as np
import ml_dtypes
from contextlib import ExitStack

import concourse.bass as bass
import concourse.mybir as mybir
from concourse.bass_utils import run_bass_kernel_spmd

F32 = mybir.dt.float32
BF16 = mybir.dt.bfloat16
I32 = mybir.dt.int32
AF = mybir.ActivationFunctionType
ALU = mybir.AluOpType
bfdt = ml_dtypes.bfloat16

D = 2048
T = 2048
I = 8192
KC = D // 128
NTB = 4
TB = 512
NPAIR = 16


def _consts():
    inv = 1.0 / 10000.0 ** (np.arange(0, 64, 2, dtype=np.float64) / 64.0)
    t = np.arange(T, dtype=np.float64)
    f = np.outer(t, inv)
    COS = np.tile(np.cos(f).T, (4, 1)).astype(np.float32).astype(bfdt)
    sgn = np.where((np.arange(128) // 32) % 2 == 0, -1.0, 1.0)[:, None]
    SINS = (np.tile(np.sin(f).T, (4, 1)) * sgn).astype(np.float32).astype(bfdt)
    m = np.arange(128)
    nb = (m // 64) * 64 + 2 * (m % 32) + ((m % 64) // 32)
    sigma = np.where(m % 64 < 32, m + 32, m - 32)
    PNB = np.zeros((128, 128), dtype=bfdt)
    PNB[nb, m] = 1
    PCMB = np.zeros((128, 128), dtype=bfdt)
    PCMB[nb[sigma], m] = 1
    DUPLO = np.zeros((128, 128), dtype=bfdt)
    DUPLO[m % 64, m] = 1
    DUPHI = np.zeros((128, 128), dtype=bfdt)
    DUPHI[64 + (m % 64), m] = 1
    IDN = np.zeros((128, 128), dtype=bfdt)
    IDN[np.arange(128), np.arange(128)] = 1
    NIDN = np.zeros((128, 128), dtype=bfdt)
    NIDN[np.arange(128), np.arange(128)] = -1.0e5
    MASKS = []
    tk = np.arange(128)[:, None]
    tq = np.arange(512)[None, :]
    for v in range(4):
        MASKS.append((tk > tq - 128 * v).astype(np.float32).astype(bfdt))
    return COS, SINS, PNB, PCMB, DUPLO, DUPHI, IDN, NIDN, MASKS


def build_nc(dbg=False):
    nc = bass.Bass("TRN2", target_bir_lowering=False)
    x_h = nc.dram_tensor("x", [T, D], F32, kind="ExternalInput").ap()
    wq_h = nc.dram_tensor("wq", [D, D], F32, kind="ExternalInput").ap()
    wk_h = nc.dram_tensor("wk", [512, D], F32, kind="ExternalInput").ap()
    wv_h = nc.dram_tensor("wv", [512, D], F32, kind="ExternalInput").ap()
    wo_h = nc.dram_tensor("wo", [D, D], F32, kind="ExternalInput").ap()
    gw_h = nc.dram_tensor("gw", [I, D], I32, kind="ExternalInput").ap()
    uw_h = nc.dram_tensor("uw", [I, D], I32, kind="ExternalInput").ap()
    dw_h = nc.dram_tensor("dw", [D, I], I32, kind="ExternalInput").ap()
    gs_h = nc.dram_tensor("gs", [I], F32, kind="ExternalInput").ap()
    us_h = nc.dram_tensor("us", [I], F32, kind="ExternalInput").ap()
    ds_h = nc.dram_tensor("ds", [D], F32, kind="ExternalInput").ap()
    out_h = nc.dram_tensor("out", [T, D], F32, kind="ExternalOutput").ap()
    if dbg:
        d_xn = nc.dram_tensor("d_xn", [T, D], BF16, kind="ExternalOutput").ap()
        d_hb = nc.dram_tensor("d_hb", [4, 128, 64, 512], BF16, kind="ExternalOutput").ap()
        d_sd = nc.dram_tensor("d_sd", [4, 128, 64 * 512], BF16, kind="ExternalOutput").ap()
        d_sg2 = nc.dram_tensor("d_sg2", [32, 128, 16 * 256], BF16, kind="ExternalOutput").ap()
        d_ffn = nc.dram_tensor("d_ffn", [D, T], BF16, kind="ExternalOutput").ap()

        d_q = nc.dram_tensor("d_q", [128, 16, T], BF16, kind="ExternalOutput").ap()
        d_kd = nc.dram_tensor("d_kd", [128, 8, T], BF16, kind="ExternalOutput").ap()
        d_v = nc.dram_tensor("d_v", [128, 16, 520], BF16, kind="ExternalOutput").ap()
        d_afm = nc.dram_tensor("d_afm", [128, 16, 512], BF16, kind="ExternalOutput").ap()
        d_at = nc.dram_tensor("d_at", [D, T], BF16, kind="ExternalOutput").ap()
        d_h = nc.dram_tensor("d_h", [128, 64, 512], BF16, kind="ExternalOutput").ap()
    scr_wq = nc.dram_tensor("scr_wq", [4, 128, 16 * 512], BF16).ap()
    scr_wk = nc.dram_tensor("scr_wk", [1, 128, 16 * 512], BF16).ap()
    scr_wv = nc.dram_tensor("scr_wv", [1, 128, 16 * 512], BF16).ap()
    scr_wo = nc.dram_tensor("scr_wo", [4, 128, 16 * 512], BF16).ap()
    scr_g = nc.dram_tensor("scr_g", [32, 128, 16 * 256], BF16).ap()
    scr_u = nc.dram_tensor("scr_u", [32, 128, 16 * 256], BF16).ap()
    scr_d = nc.dram_tensor("scr_d", [4, 128, 64 * 512], BF16).ap()
    scr_xn = nc.dram_tensor("scr_xn", [T, D], BF16).ap()
    scr_at = nc.dram_tensor("scr_at", [D, T], BF16).ap()
    scr_ffn = nc.dram_tensor("scr_ffn", [D, T], BF16).ap()
    COS_c, SINS_c, PNB_c, PCMB_c, DLO_c, DHI_c, IDN_c, NIDN_c, MASKS_c = _consts()
    cos_d = nc.inline_tensor(COS_c, name="cos_c").ap()
    sin_d = nc.inline_tensor(SINS_c, name="sin_c").ap()
    pnb_d = nc.inline_tensor(PNB_c, name="pnb_c").ap()
    pcmb_d = nc.inline_tensor(PCMB_c, name="pcmb_c").ap()
    dlo_d = nc.inline_tensor(DLO_c, name="dlo_c").ap()
    dhi_d = nc.inline_tensor(DHI_c, name="dhi_c").ap()
    idn_d = nc.inline_tensor(IDN_c, name="idn_c").ap()
    nidn_d = nc.inline_tensor(NIDN_c, name="nidn_c").ap()
    msk_d = [nc.inline_tensor(MASKS_c[v], name=f"msk{v}_c").ap() for v in range(4)]

    es = ExitStack()

    class S:
        def __init__(self, name):
            self.h = es.enter_context(nc.semaphore(name))
            self.v = 0
    sl = S("sl")
    ss = S("ss")
    sg = S("sg")
    spe = S("spe")
    sac = S("sac")
    sdv = S("sdv")
    spl = S("spl")
    sout = S("sout")

    QSEM = {"sync": sl, "scalar": ss, "gpsimd": sg}
    ESEM = {"tensor": spe, "scalar": sac, "vector": sdv, "gpsimd": spl}

    ENGS = ["sync", "scalar", "vector", "tensor", "gpsimd"]
    queues = {e: [] for e in ENGS}

    def flush_block():
        with nc.Block() as blk:
            for name in ENGS:
                fns = queues[name]
                if fns:
                    getattr(blk, name)(lambda e, fns=fns: [f(e) for f in fns])
        for name in ENGS:
            queues[name] = []

    def emit(eng, f):
        queues[eng].append(f)

    def wait(eng, sem, v=None):
        vv = sem.v if v is None else v
        if vv > 0:
            emit(eng, lambda e, s=sem.h, v=vv: e.wait_ge(s, v))

    def dma(eng, out, in_, **kw):
        sem = QSEM[eng]
        sem.v += 16
        emit(eng, lambda e, o=out, i=in_, k=kw, s=sem.h: e.dma_start(out=o, in_=i, **k).then_inc(s, 16))
        return sem.v

    def inc(eng, call):
        sem = ESEM[eng]
        sem.v += 1
        emit(eng, lambda e, c=call, s=sem.h: c(e).then_inc(s, 1))
        return sem.v

    def op(eng, call):
        emit(eng, call)

    def drain_dmas():
        wait("sync", sl, sl.v)
        wait("scalar", ss, ss.v)
        wait("gpsimd", sg, sg.v)

    def sb_in(stack, name, shape, dt):
        return stack.enter_context(nc.sbuf_tensor(name, shape, dt))

    COS = sb_in(es, "COS", [128, T], BF16)
    SINS = sb_in(es, "SINS", [128, T], BF16)
    PNB = sb_in(es, "PNB", [128, 128], BF16)
    PCMB = sb_in(es, "PCMB", [128, 128], BF16)
    DLO = sb_in(es, "DLO", [128, 128], BF16)
    DHI = sb_in(es, "DHI", [128, 128], BF16)
    IDN = sb_in(es, "IDN", [128, 128], BF16)
    MSK = [sb_in(es, f"MSK{v}", [128, 512], BF16) for v in range(4)]
    NIDN = sb_in(es, "NIDN", [128, 128], BF16)
    gs_sb = sb_in(es, "gs_sb", [128, 64], F32)
    us_sb = sb_in(es, "us_sb", [128, 64], F32)
    ds_sb = sb_in(es, "ds_sb", [128, 16], F32)
    ones64 = sb_in(es, "ones64", [1, 64], BF16)
    epsv = sb_in(es, "epsv", [128, 1], F32)
    n128v = sb_in(es, "n128v", [128, 1], F32)

    es0 = ExitStack()
    stg32 = [sb_in(es0, f"stg32_{i}", [128, I], I32) for i in range(2)]
    stg16 = [sb_in(es0, f"stg16_{i}", [128, I], BF16) for i in range(2)]
    stgT = sb_in(es0, "stgT", [128, 64, 512], BF16)
    pes0 = ExitStack()
    pst = [pes0.enter_context(nc.psum_tensor(f"pst{i}", [128, 128], BF16)) for i in range(4)]
    st32_rd = [0, 0]
    st16_rd = [0, 0]
    st16_wr = [0, 0]
    pst_rd = [0, 0, 0, 0]
    stgT_wr = [0]

    op("gpsimd", lambda e: e.memset(ones64[:], 1.0))
    op("gpsimd", lambda e: e.memset(epsv[:], 1e-5))
    op("gpsimd", lambda e: e.memset(n128v[:], -128.0))
    dma("sync", COS[:], cos_d)
    dma("sync", SINS[:], sin_d)
    dma("sync", PNB[:], pnb_d)
    dma("sync", PCMB[:], pcmb_d)
    dma("sync", DLO[:], dlo_d)
    dma("sync", DHI[:], dhi_d)
    dma("sync", IDN[:], idn_d)
    dma("sync", NIDN[:], nidn_d)
    for v in range(4):
        dma("sync", MSK[v][:], msk_d[v])
    for c in range(64):
        dma("sync", gs_sb[:, c:c + 1], gs_h[c * 128:(c + 1) * 128][:, None])
        dma("sync", us_sb[:, c:c + 1], us_h[c * 128:(c + 1) * 128][:, None])
    for c in range(16):
        dma("sync", ds_sb[:, c:c + 1], ds_h[c * 128:(c + 1) * 128][:, None])
    wait("tensor", sl, sl.v)

    def prep_weight(src_h, n_chunks, W, scr_T, sub128, src_dtype, span=512):
        kcn = W // 128
        stgT2 = stgT[:, 0:kcn, :].rearrange("p x y -> p (x y)").rearrange("p (a b c) -> p a b c", a=2, b=kcn, c=256) if span == 256 else None
        for c in range(n_chunks):
            i = c % 2
            s32 = stg32[i].ap().bitcast(src_dtype)[:, 0:W] if src_dtype != I32 else stg32[i][:, 0:W]
            s16 = stg16[i]
            wait("gpsimd", sac, st32_rd[i])
            dma("gpsimd", s32[:, :], src_h[c * 128:(c + 1) * 128, :])
            wait("scalar", sg)
            wait("scalar", spe, st16_rd[i])
            if sub128:
                st32_rd[i] = inc("scalar", lambda e, s32=s32, s16=s16, W=W: e.activation(
                    out=s16[:, 0:W], in_=s32, func=AF.Identity, bias=n128v[:]))
            else:
                st32_rd[i] = inc("scalar", lambda e, s32=s32, s16=s16, W=W: e.activation(
                    out=s16[:, 0:W], in_=s32, func=AF.Copy))
            cast_done = st32_rd[i]
            wait("tensor", sac, cast_done)
            co = (c % 4) * 128
            last_tp = 0
            for kc in range(kcn):
                p = pst[kc % 4]
                wait("tensor", sdv, pst_rd[kc % 4])
                last_tp = inc("tensor", lambda e, p=p, s16=s16, kc=kc: e.transpose(
                    out=p[:], in_=s16[:, kc * 128:(kc + 1) * 128], identity=IDN[:]))
                wait("vector", spe, last_tp)
                wait("vector", sl, stgT_wr[0])
                if span == 512:
                    dst = stgT[:, kc, co:co + 128]
                else:
                    dst = stgT2[:, co // 256, kc, (co % 256):(co % 256) + 128]
                pst_rd[kc % 4] = inc("vector", lambda e, p=p, dst=dst: e.tensor_copy(
                    out=dst, in_=p[:]))
            st16_rd[i] = last_tp
            if c % 4 == 3:
                g = c // 4
                wait("sync", sdv)
                if span == 512:
                    dma("sync", scr_T[g], stgT[:, 0:kcn, 0:512])
                else:
                    dma("sync", scr_T[2 * g], stgT2[:, 0])
                    dma("sync", scr_T[2 * g + 1], stgT2[:, 1])
                stgT_wr[0] = sl.v

    prep_weight(wq_h, 16, D, scr_wq, False, F32)
    prep_weight(wk_h, 4, D, scr_wk, False, F32)
    prep_weight(wv_h, 4, D, scr_wv, False, F32)
    prep_weight(wo_h, 16, D, scr_wo, False, F32)
    prep_weight(gw_h, 64, D, scr_g, True, I32, span=256)
    prep_weight(uw_h, 64, D, scr_u, True, I32, span=256)
    prep_weight(dw_h, 16, I, scr_d, True, I32)
    drain_dmas()
    flush_block()
    pes0.close()
    es0.close()

    es1 = ExitStack()
    q_sb = sb_in(es1, "q_sb", [128, 16, T], BF16)
    kd_sb = sb_in(es1, "kd_sb", [128, 8, T], BF16)
    v_sb = sb_in(es1, "v_sb", [128, 16, 520], BF16)

    es1b = ExitStack()
    xt = [sb_in(es1b, f"xt{i}", [128, D], F32) for i in range(2)]
    xn_tm = sb_in(es1b, "xn_tm", [128, D], BF16)
    xn_fm = sb_in(es1b, "xn_fm", [128, KC, TB], BF16)
    sumv = sb_in(es1b, "sumv", [128, 1], F32)
    sqv = sb_in(es1b, "sqv", [128, 1], F32)
    mv = sb_in(es1b, "mv", [128, 1], F32)
    ex2v = sb_in(es1b, "ex2v", [128, 1], F32)
    varv = sb_in(es1b, "varv", [128, 1], F32)
    rstdv = sb_in(es1b, "rstdv", [128, 1], F32)
    nmrv = sb_in(es1b, "nmrv", [128, 1], F32)
    wbuf = [sb_in(es1b, f"wbuf{i}", [128, 16, 512], BF16) for i in range(2)]
    qtmp = [sb_in(es1b, f"qtmp{i}", [128, 512], BF16) for i in range(2)]
    t1b = sb_in(es1b, "t1b", [128, 512], BF16)
    t2b = sb_in(es1b, "t2b", [128, 512], BF16)
    krope = sb_in(es1b, "krope", [128, 4, 512], BF16)

    pes = ExitStack()
    pp = [pes.enter_context(nc.psum_tensor(f"pp{i}", [128, 512], F32)) for i in range(2)]
    pa_ps = pes.enter_context(nc.psum_tensor("pa_ps", [128, 512], F32))
    pb_ps = pes.enter_context(nc.psum_tensor("pb_ps", [128, 512], F32))

    for tt in range(16):
        for j in range(8):
            op("vector", lambda e, tt=tt, j=j: e.memset(v_sb[:, tt, j * 65 + 64:j * 65 + 65], 1.0))

    xt_rd = [0, 0]
    xntm_rd = [0]
    sqv_rd = [0]
    qtmp_st = [(0, 0), (0, 0)]
    pp_rd = [0, 0]
    pa_rd = [0]
    pb_rd = [0]
    rope_flip = [0]

    def ln_tile(b, tau):
        t0 = (4 * b + tau) * 128
        xe = xt[tau % 2]
        wait("sync", sac, xt_rd[tau % 2])
        lv = dma("sync", xe[:], x_h[t0:t0 + 128, :])
        wait("vector", sl, lv)
        sdv_red = inc("vector", lambda e, xe=xe: e.reduce_sum(sumv[:], xe[:], axis=mybir.AxisListType.X))
        wait("scalar", sl, lv)
        wait("scalar", sdv, sqv_rd[0])
        wait("scalar", ss, xntm_rd[0])
        sac_sq = inc("scalar", lambda e, xe=xe: e.activation(
            out=xn_tm[:], in_=xe[:], func=AF.Square, accum_out=sqv[:]))
        wait("vector", sdv, sdv_red)
        sdv_mv = inc("vector", lambda e: e.tensor_scalar_mul(out=mv[:], in0=sumv[:], scalar1=1.0 / D))
        wait("vector", sac, sac_sq)
        sqv_rd[0] = inc("vector", lambda e: e.tensor_scalar_mul(out=ex2v[:], in0=sqv[:], scalar1=1.0 / D))
        wait("vector", sdv, sdv_mv)
        sdv_vm = inc("vector", lambda e: e.tensor_tensor(out=varv[:], in0=mv[:], in1=mv[:], op=ALU.mult))
        wait("vector", sdv, sdv_vm)
        sdv_var = inc("vector", lambda e: e.tensor_tensor(out=varv[:], in0=ex2v[:], in1=varv[:], op=ALU.subtract))
        wait("scalar", sdv, sdv_var)
        sac_ln1 = inc("scalar", lambda e: e.activation(out=rstdv[:], in_=varv[:], func=AF.Ln, bias=epsv[:]))
        wait("scalar", sac, sac_ln1)
        sac_r = inc("scalar", lambda e: e.activation(out=rstdv[:], in_=rstdv[:], func=AF.Exp, scale=-0.5))
        wait("vector", sac, sac_r)
        sdv_nmr = inc("vector", lambda e: e.tensor_tensor(out=nmrv[:], in0=mv[:], in1=rstdv[:], op=ALU.mult))
        wait("vector", sdv, sdv_nmr)
        sdv_nmr2 = inc("vector", lambda e: e.tensor_scalar_mul(out=nmrv[:], in0=nmrv[:], scalar1=-1.0))
        wait("scalar", sdv, sdv_nmr2)
        sac_xn = inc("scalar", lambda e, xe=xe: e.activation(
            out=xn_tm[:], in_=xe[:], func=AF.Identity, scale=rstdv[:], bias=nmrv[:]))
        xt_rd[tau % 2] = sac_xn
        wait("scalar", sac, sac_xn)
        dma("scalar", scr_xn[t0:t0 + 128, :], xn_tm[:])
        xntm_rd[0] = ss.v

    def rope_chunk(b, ps_src, ps_idx, dst_ap):
        fl = rope_flip[0]
        rope_flip[0] ^= 1
        qt = qtmp[fl]
        (g_spe, g_sdv) = qtmp_st[fl]
        wait("scalar", spe, g_spe)
        wait("scalar", sdv, g_sdv)
        sac_ev = inc("scalar", lambda e, qt=qt, p=ps_src: e.activation(out=qt[:], in_=p[:], func=AF.Copy))
        if ps_idx is not None:
            pp_rd[ps_idx] = sac_ev
        wait("tensor", sac, sac_ev)
        wait("tensor", sdv, pa_rd[0])
        spe_a = inc("tensor", lambda e, qt=qt: e.matmul(out=pa_ps[:], lhsT=PNB[:], rhs=qt[:],
                                                        start=True, stop=True))
        wait("tensor", sdv, pb_rd[0])
        spe_b = inc("tensor", lambda e, qt=qt: e.matmul(out=pb_ps[:], lhsT=PCMB[:], rhs=qt[:],
                                                        start=True, stop=True))
        bsl = slice(b * TB, (b + 1) * TB)
        wait("vector", spe, spe_a)
        sdv_t1 = inc("vector", lambda e, bsl=bsl: e.tensor_tensor(
            out=t1b[:], in0=pa_ps[:], in1=COS[:, bsl], op=ALU.mult))
        pa_rd[0] = sdv_t1
        wait("vector", spe, spe_b)
        sdv_t2 = inc("vector", lambda e, bsl=bsl: e.tensor_tensor(
            out=t2b[:], in0=pb_ps[:], in1=SINS[:, bsl], op=ALU.mult))
        pb_rd[0] = sdv_t2
        wait("vector", sdv, sdv_t2)
        sdv_o = inc("vector", lambda e, d=dst_ap: e.tensor_tensor(out=d, in0=t1b[:], in1=t2b[:], op=ALU.add))
        qtmp_st[fl] = (spe_a if False else spe_b, sdv_t1)
        return sac_ev, sdv_o

    wbuf_rd = [0, 0]

    for b in range(NTB):
        bsl = slice(b * TB, (b + 1) * TB)
        for tau in range(4):
            ln_tile(b, tau)
        wait("sync", ss, ss.v)
        for c in range(KC):
            dma("sync", xn_fm[:, c, :], scr_xn[bsl, c * 128:(c + 1) * 128], transpose=True)
        xn_ready = sl.v
        for s in range(4):
            wb = wbuf[s % 2]
            wait("sync", spe, wbuf_rd[s % 2])
            dma("sync", wb[:, :, :], scr_wq[s])
            ld_done = sl.v
            for mj in range(4):
                mu = s * 4 + mj
                pe = pp[mu % 2]
                wait("tensor", sl, ld_done)
                wait("tensor", sl, xn_ready)
                wait("tensor", sac, pp_rd[mu % 2])
                last = 0
                for k in range(KC):
                    last = inc("tensor", lambda e, k=k, pe=pe, wb=wb, mj=mj: e.matmul(
                        out=pe[:], lhsT=wb[:, k, mj * 128:(mj + 1) * 128], rhs=xn_fm[:, k, :],
                        start=(k == 0), stop=(k == KC - 1)))
                wbuf_rd[s % 2] = last
                wait("scalar", spe, last)
                rope_chunk(b, pe, mu % 2, q_sb[:, mu, bsl])
        wbk = wbuf[0]
        wait("sync", spe, wbuf_rd[0])
        dma("sync", wbk[:, :, :], scr_wk[0])
        ld_done = sl.v
        for mu in range(4):
            pe = pp[mu % 2]
            wait("tensor", sl, ld_done)
            wait("tensor", sl, xn_ready)
            wait("tensor", sac, pp_rd[mu % 2])
            last = 0
            for k in range(KC):
                last = inc("tensor", lambda e, k=k, pe=pe, mu=mu: e.matmul(
                    out=pe[:], lhsT=wbk[:, k, mu * 128:(mu + 1) * 128], rhs=xn_fm[:, k, :],
                    start=(k == 0), stop=(k == KC - 1)))
            wbuf_rd[0] = last
            wait("scalar", spe, last)
            _, sdv_o = rope_chunk(b, pe, mu % 2, krope[:, mu, :])
            wait("tensor", sdv, sdv_o)
            for half, PM in ((0, DLO), (1, DHI)):
                wait("tensor", sdv, pa_rd[0])
                spe_d = inc("tensor", lambda e, PM=PM, mu=mu: e.matmul(
                    out=pa_ps[:], lhsT=PM[:], rhs=krope[:, mu, :], start=True, stop=True))
                wait("scalar", spe, spe_d)
                sac_e = inc("scalar", lambda e, mu=mu, half=half, bsl=bsl: e.activation(
                    out=kd_sb[:, 2 * mu + half, bsl], in_=pa_ps[:], func=AF.Copy))
                pa_rd[0] = 0
                wait("tensor", sac, sac_e)
        wbv = wbuf[1]
        wait("sync", spe, wbuf_rd[1])
        dma("sync", wbv[:, :, :], scr_wv[0])
        ld_done = sl.v
        for tau in range(4):
            pe = pp[tau % 2]
            wait("tensor", sl, ld_done)
            wait("tensor", sl, xn_ready)
            wait("tensor", sac, pp_rd[tau % 2])
            last = 0
            for k in range(KC):
                last = inc("tensor", lambda e, k=k, pe=pe, tau=tau, wbv=wbv: e.matmul(
                    out=pe[:], lhsT=xn_fm[:, k, tau * 128:(tau + 1) * 128], rhs=wbv[:, k, 0:512],
                    start=(k == 0), stop=(k == KC - 1)))
            wbuf_rd[1] = last
            wait("vector", spe, last)
            vout = v_sb[:, 4 * b + tau, 0:520].rearrange("p (h x) -> p h x", h=8)[:, :, 0:64]
            vin = pe[:].rearrange("p (h x) -> p h x", h=8)
            sdv_v = inc("vector", lambda e, vout=vout, vin=vin: e.tensor_copy(out=vout, in_=vin))
            wait("tensor", sdv, sdv_v)
            pp_rd[tau % 2] = 0
        if b == NTB - 1:
            drain_dmas()
        flush_block()

    es1b.close()
    pes.close()

    es2 = ExitStack()
    wob = [sb_in(es2, f"wob{i}", [128, 16, 512], BF16) for i in range(2)]
    pbuf = [sb_in(es2, f"pbuf{i}", [128, 1024], BF16) for i in range(2)]
    afm = sb_in(es2, "afm", [128, 16, 512], BF16)
    lnv2 = sb_in(es2, "lnv2", [1, 512], F32)
    rbf = sb_in(es2, "rbf", [1, 512], BF16)
    rb64 = sb_in(es2, "rb64", [64, 512], BF16)
    tmp64 = sb_in(es2, "tmp64", [64, 512], BF16)
    woev = [sb_in(es2, f"woev{i}", [128, 512], BF16) for i in range(2)]

    pes2 = ExitStack()
    psc = [pes2.enter_context(nc.psum_tensor(f"psc{i}", [128, 1024], F32)) for i in range(2)]
    pav = [pes2.enter_context(nc.psum_tensor(f"pav{i}", [128, 512], F32)) for i in range(4)]

    norm_state = {"rbf_rd": 0, "rb64_rd": 0}

    for B in range(NTB):
        if dbg and B == 0:
            sout.v += 16; emit("gpsimd", lambda e, s=sout.h: e.dma_start(out=d_xn, in_=scr_xn).then_inc(s, 16))
            sout.v += 16; emit("gpsimd", lambda e, s=sout.h: e.dma_start(out=d_sd, in_=scr_d).then_inc(s, 16))
            sout.v += 16; emit("gpsimd", lambda e, s=sout.h: e.dma_start(out=d_sg2, in_=scr_g).then_inc(s, 16))


            sout.v += 16; emit("gpsimd", lambda e, s=sout.h: e.dma_start(out=d_q, in_=q_sb[:]).then_inc(s, 16))
            sout.v += 16; emit("gpsimd", lambda e, s=sout.h: e.dma_start(out=d_kd, in_=kd_sb[:]).then_inc(s, 16))
            sout.v += 16; emit("gpsimd", lambda e, s=sout.h: e.dma_start(out=d_v, in_=v_sb[:]).then_inc(s, 16))
        Bsl = slice(B * TB, (B + 1) * TB)
        nck = 4 * B + 4
        for s in range(2):
            dma("sync", wob[s][:, :, :], scr_wo[s])
        wo_ld01 = sl.v

        exp_read = {}
        av_tt = {}
        av_last = {}
        norm_q = []

        def do_norm(e_pair, pidx):
            for hh in range(2):
                pa = pav[pidx + hh]
                sac_ln = inc("scalar", lambda e, pa=pa: e.activation(
                    out=lnv2[:], in_=pa[64:65, :], func=AF.Ln))
                wait("scalar", spe, norm_state["rbf_rd"])
                wait("scalar", sac, sac_ln)
                sac_r = inc("scalar", lambda e: e.activation(
                    out=rbf[:], in_=lnv2[:], func=AF.Exp, scale=-1.0))
                wait("tensor", sac, sac_r)
                spe_b = inc("tensor", lambda e, pa=pa: e.matmul(
                    out=pa[64:128, :], lhsT=ones64[:], rhs=rbf[:], start=True, stop=True))
                norm_state["rbf_rd"] = spe_b
                wait("scalar", spe, spe_b)
                wait("scalar", sdv, norm_state["rb64_rd"])
                sac_rb = inc("scalar", lambda e, pa=pa: e.activation(
                    out=rb64[:], in_=pa[64:128, :], func=AF.Copy))
                wait("vector", sac, sac_rb)
                if hh == 0:
                    sdv_tt = inc("vector", lambda e, pa=pa, ep=e_pair: e.tensor_tensor(
                        out=afm[0:64, ep, :], in0=pa[0:64, :], in1=rb64[:], op=ALU.mult))
                else:
                    sdv_t64 = inc("vector", lambda e, pa=pa: e.tensor_tensor(
                        out=tmp64[:], in0=pa[0:64, :], in1=rb64[:], op=ALU.mult))
                    wait("vector", sdv, sdv_t64)
                    sdv_tt = inc("vector", lambda e, ep=e_pair: e.tensor_scalar_mul(
                        out=afm[64:128, ep, :], in0=tmp64[:], scalar1=1.0))
                norm_state["rb64_rd"] = sdv_tt
                av_tt[pidx + hh] = sdv_tt

        for e_pair in range(NPAIR):
            j = e_pair // 2
            pidx = (e_pair % 2) * 2
            for hh in range(2):
                if (pidx + hh) in av_tt:
                    wait("tensor", sdv, av_tt[pidx + hh])
            av_q = []
            for c in range(nck):
                ps = psc[c % 2]
                if (c % 2) in exp_read:
                    wait("tensor", sac, exp_read[c % 2])
                inc("tensor", lambda e, ps=ps, j=j, c=c, ep=e_pair, Bsl=Bsl: e.matmul(
                    out=ps[:, 0:512], lhsT=kd_sb[0:64, j, c * 128:(c + 1) * 128],
                    rhs=q_sb[0:64, ep, Bsl], start=True, stop=(c < 4 * B)))
                spe_s = inc("tensor", lambda e, ps=ps, j=j, c=c, ep=e_pair, Bsl=Bsl: e.matmul(
                    out=ps[:, 512:1024], lhsT=kd_sb[64:128, j, c * 128:(c + 1) * 128],
                    rhs=q_sb[64:128, ep, Bsl], start=True, stop=(c < 4 * B)))
                if c >= 4 * B:
                    vmk = MSK[c - 4 * B]
                    inc("tensor", lambda e, ps=ps, vmk=vmk: e.matmul(
                        out=ps[:, 0:512], lhsT=NIDN[:], rhs=vmk[:], start=False, stop=True))
                    spe_s = inc("tensor", lambda e, ps=ps, vmk=vmk: e.matmul(
                        out=ps[:, 512:1024], lhsT=NIDN[:], rhs=vmk[:], start=False, stop=True))
                if av_q:
                    av_q.pop(0)()
                wait("scalar", spe, spe_s)
                if (c % 2) in av_last:
                    wait("scalar", spe, av_last[c % 2])
                sac_e = inc("scalar", lambda e, ps=ps, c=c: e.activation(
                    out=pbuf[c % 2][:], in_=ps[:], func=AF.Exp, scale=0.125))
                exp_read[c % 2] = sac_e

                def do_av(c=c, j=j, pidx=pidx, nck=nck, sac_e=sac_e):
                    wait("tensor", sac, sac_e)
                    inc("tensor", lambda e, c=c, j=j, pidx=pidx, nck=nck: e.matmul(
                        out=pav[pidx][0:65, :], lhsT=v_sb[:, c, j * 65:(j + 1) * 65],
                        rhs=pbuf[c % 2][:, 0:512], start=(c == 0), stop=(c == nck - 1)))
                    spe_av = inc("tensor", lambda e, c=c, j=j, pidx=pidx, nck=nck: e.matmul(
                        out=pav[pidx + 1][0:65, :], lhsT=v_sb[:, c, j * 65:(j + 1) * 65],
                        rhs=pbuf[c % 2][:, 512:1024], start=(c == 0), stop=(c == nck - 1)))
                    av_last[c % 2] = spe_av
                av_q.append(do_av)
            while av_q:
                av_q.pop(0)()
            if norm_q:
                norm_q.pop(0)()
            norm_q.append(lambda ep=e_pair, pidx=pidx: do_norm(ep, pidx))
        while norm_q:
            norm_q.pop(0)()
        attn_done = sdv.v
        woev_rd = [0, 0]
        wob_rd = [0, 0]
        for s in range(4):
            wot = wob[s % 2]
            if s >= 2:
                wait("sync", spe, wob_rd[s % 2])
                dma("sync", wot[:, :, :], scr_wo[s])
                ld_done = sl.v
            else:
                ld_done = wo_ld01
            for mj in range(4):
                mu = s * 4 + mj
                pw = psc[mu % 2]
                wait("tensor", sl, ld_done)
                wait("tensor", sdv, attn_done)
                if (mu % 2) in exp_read:
                    wait("tensor", sac, exp_read.pop(mu % 2))
                last = 0
                for dv in range(16):
                    last = inc("tensor", lambda e, dv=dv, pw=pw, wot=wot, mj=mj: e.matmul(
                        out=pw[:, 0:512], lhsT=wot[:, dv, mj * 128:(mj + 1) * 128], rhs=afm[:, dv, :],
                        start=(dv == 0), stop=(dv == 15)))
                wob_rd[s % 2] = last
                wait("scalar", spe, last)
                wait("scalar", ss, woev_rd[mu % 2])
                sac_ev = inc("scalar", lambda e, pw=pw, mu=mu: e.activation(
                    out=woev[mu % 2][:], in_=pw[:, 0:512], func=AF.Copy))
                exp_read[mu % 2] = sac_ev
                wait("scalar", sac, sac_ev)
                dma("scalar", scr_at[mu * 128:(mu + 1) * 128, Bsl], woev[mu % 2][:])
                woev_rd[mu % 2] = ss.v
        if dbg and B == NTB - 1:
            wait("gpsimd", sdv, attn_done)
            sout.v += 16; emit("gpsimd", lambda e, s=sout.h: e.dma_start(out=d_afm, in_=afm[:]).then_inc(s, 16))
        if B == NTB - 1:
            drain_dmas()
        flush_block()

    pes2.close()
    es2.close()
    es1.close()

    es3 = ExitStack()
    h_sb = sb_in(es3, "h_sb", [128, 64, 512], BF16)
    xnf = sb_in(es3, "xnf", [128, KC, 512], BF16)
    gwide = [sb_in(es3, f"gwide{i}", [128, KC, 256], BF16) for i in range(2)]
    uwide = [sb_in(es3, f"uwide{i}", [128, KC, 256], BF16) for i in range(2)]
    dbw = [sb_in(es3, f"dbw{i}", [128, 8, 512], BF16) for i in range(2)]
    sgt = [sb_in(es3, f"sgt{i}", [128, 512], BF16) for i in range(2)]
    sut = [sb_in(es3, f"sut{i}", [128, 512], BF16) for i in range(2)]
    fdev = [sb_in(es3, f"fdev{i}", [128, 512], BF16) for i in range(2)]
    xt3 = [sb_in(es3, f"xt3{i}", [128, D], F32) for i in range(2)]
    at3 = [sb_in(es3, f"at3{i}", [128, D], BF16) for i in range(2)]
    ft3 = [sb_in(es3, f"ft3{i}", [128, D], BF16) for i in range(2)]
    out3 = [sb_in(es3, f"out3{i}", [128, D], F32) for i in range(2)]

    pes3 = ExitStack()
    pg = [pes3.enter_context(nc.psum_tensor(f"pg{i}", [128, 512], F32)) for i in range(4)]
    pu = [pes3.enter_context(nc.psum_tensor(f"pu{i}", [128, 512], F32)) for i in range(4)]

    gu_mm = [0, 0]
    sgt_rd = [0, 0]
    gu_ev_sac = [0, 0]
    gu_ev_sdv = [0, 0]
    dn_mm = {}
    out3_rd = [0, 0]
    xt3_rd = [0, 0]
    at3_rd = [0, 0]
    fdev_rd = [0, 0]

    for b in range(NTB):
        bsl = slice(b * TB, (b + 1) * TB)
        for c in range(KC):
            dma("sync", xnf[:, c, :], scr_xn[bsl, c * 128:(c + 1) * 128], transpose=True)
        xn_ld = sl.v
        for sp in range(32):
            set_ = sp % 2
            gw_t, uw_t = gwide[set_], uwide[set_]
            wait("sync", spe, gu_mm[set_])
            dma("sync", gw_t[:, :, :], scr_g[sp])
            dma("sync", uw_t[:, :, :], scr_u[sp])
            ld_done = sl.v
            pga, pgb_ = pg[2 * set_], pg[2 * set_ + 1]
            pua, pub = pu[2 * set_], pu[2 * set_ + 1]
            wait("tensor", sl, ld_done)
            wait("tensor", sl, xn_ld)
            wait("tensor", sac, gu_ev_sac[set_])
            wait("tensor", sdv, gu_ev_sdv[set_])
            last = 0
            for k in range(KC):
                inc("tensor", lambda e, k=k, gw_t=gw_t, pga=pga: e.matmul(
                    out=pga[:], lhsT=gw_t[:, k, 0:128], rhs=xnf[:, k, :],
                    start=(k == 0), stop=(k == KC - 1)))
                inc("tensor", lambda e, k=k, gw_t=gw_t, pgb_=pgb_: e.matmul(
                    out=pgb_[:], lhsT=gw_t[:, k, 128:256], rhs=xnf[:, k, :],
                    start=(k == 0), stop=(k == KC - 1)))
                inc("tensor", lambda e, k=k, uw_t=uw_t, pua=pua: e.matmul(
                    out=pua[:], lhsT=uw_t[:, k, 0:128], rhs=xnf[:, k, :],
                    start=(k == 0), stop=(k == KC - 1)))
                last = inc("tensor", lambda e, k=k, uw_t=uw_t, pub=pub: e.matmul(
                    out=pub[:], lhsT=uw_t[:, k, 128:256], rhs=xnf[:, k, :],
                    start=(k == 0), stop=(k == KC - 1)))
            gu_mm[set_] = last
            wait("scalar", spe, last)
            wait("vector", spe, last)
            e_sac = e_sdv = 0
            for d in range(2):
                ci = sp * 2 + d
                pgx = pga if d == 0 else pgb_
                pux = pua if d == 0 else pub
                wait("scalar", sdv, sgt_rd[d])
                e_sac = inc("scalar", lambda e, pgx=pgx, ci=ci, d=d: e.activation(
                    out=sgt[d][:], in_=pgx[:], func=AF.Silu, scale=gs_sb[:, ci:ci + 1]))
                e_sdv0 = inc("vector", lambda e, pux=pux, ci=ci, d=d: e.tensor_single_scalar(
                    out=sut[d][:], in_=pux[:], scalar=us_sb[:, ci:ci + 1], op=ALU.mult))
                wait("vector", sac, e_sac)
                wait("vector", sdv, e_sdv0)
                e_sdv = inc("vector", lambda e, ci=ci, d=d: e.tensor_tensor(
                    out=h_sb[:, ci, :], in0=sgt[d][:], in1=sut[d][:], op=ALU.mult))
                sgt_rd[d] = e_sdv
            gu_ev_sac[set_] = e_sac
            gu_ev_sdv[set_] = e_sdv
        h_done = sdv.v
        gu_done_sac = sac.v
        dn_ev = {}
        for ms in range(4):
            last = 0
            for grp in range(8):
                gbuf = dbw[grp % 2]
                wait("sync", spe, dn_mm.get(grp % 2, 0))
                dma("sync", gbuf[:, :, :],
                    scr_d[ms][:, grp * 8 * 512:(grp + 1) * 8 * 512])
                ld_done = sl.v
                wait("tensor", sl, ld_done)
                if grp == 0:
                    wait("tensor", sdv, h_done)
                    wait("tensor", sac, gu_done_sac)
                    for j in range(4):
                        if j in dn_ev:
                            wait("tensor", sac, dn_ev[j])
                for q in range(8):
                    ic = grp * 8 + q
                    for j in range(4):
                        last = inc("tensor", lambda e, j=j, gbuf=gbuf, q=q, ic=ic: e.matmul(
                            out=pg[j][:], lhsT=gbuf[:, q, j * 128:(j + 1) * 128], rhs=h_sb[:, ic, :],
                            start=(ic == 0), stop=(ic == 63)))
                dn_mm[grp % 2] = last
            wait("scalar", spe, last)
            for j in range(4):
                mu = ms * 4 + j
                wait("scalar", ss, fdev_rd[j % 2])
                sac_d = inc("scalar", lambda e, j=j, mu=mu: e.activation(
                    out=fdev[j % 2][:], in_=pg[j][:], func=AF.Identity, scale=ds_sb[:, mu:mu + 1]))
                dn_ev[j] = sac_d
                wait("scalar", sac, sac_d)
                dma("scalar", scr_ffn[mu * 128:(mu + 1) * 128, bsl], fdev[j % 2][:])
                fdev_rd[j % 2] = ss.v
        fst_done = ss.v
        if dbg:
            wait("gpsimd", sdv, h_done)
            sout.v += 16
            emit("gpsimd", lambda e, b=b, s=sout.h: e.dma_start(out=d_hb[b], in_=h_sb[:]).then_inc(s, 16))
        wait("sync", ss, fst_done)
        for tau in range(4):
            t0 = (4 * b + tau) * 128
            wait("sync", sdv, xt3_rd[tau % 2])
            lx = dma("sync", xt3[tau % 2][:], x_h[t0:t0 + 128, :])
            wait("sync", sdv, at3_rd[tau % 2])
            la = dma("sync", at3[tau % 2][:], scr_at[:, t0:t0 + 128], transpose=True)
            lf = dma("sync", ft3[tau % 2][:], scr_ffn[:, t0:t0 + 128], transpose=True)
            wait("vector", sl, lf)
            wait("vector", sout, out3_rd[tau % 2])
            sdv_a = inc("vector", lambda e, tau=tau: e.tensor_tensor(
                out=out3[tau % 2][:], in0=xt3[tau % 2][:], in1=at3[tau % 2][:], op=ALU.add))
            xt3_rd[tau % 2] = sdv_a
            at3_rd[tau % 2] = sdv_a
            wait("vector", sdv, sdv_a)
            sdv_o = inc("vector", lambda e, tau=tau: e.tensor_tensor(
                out=out3[tau % 2][:], in0=out3[tau % 2][:], in1=ft3[tau % 2][:], op=ALU.add))
            wait("gpsimd", sdv, sdv_o)
            sout.v += 16
            emit("gpsimd", lambda e, t0=t0, tau=tau, s=sout.h: e.dma_start(
                out=out_h[t0:t0 + 128, :], in_=out3[tau % 2][:]).then_inc(s, 16))
            out3_rd[tau % 2] = sout.v
        flush_block()

    if dbg:
        sout.v += 16; emit("gpsimd", lambda e, s=sout.h: e.dma_start(out=d_at, in_=scr_at).then_inc(s, 16))
        sout.v += 16; emit("gpsimd", lambda e, s=sout.h: e.dma_start(out=d_ffn, in_=scr_ffn).then_inc(s, 16))
        sout.v += 16; emit("gpsimd", lambda e, s=sout.h: e.dma_start(out=d_h, in_=h_sb[:]).then_inc(s, 16))
    wait("gpsimd", sout, sout.v)
    flush_block()

    es3.close()
    pes3.close()
    es.close()
    return nc


_NC_CACHE = None


def kernel(L0_attn_x, L0_attn_wq, L0_attn_wk, L0_attn_wv, L0_attn_wo,
           L0_ffn_gate_w, L0_ffn_gate_scale, L0_ffn_up_w, L0_ffn_up_scale,
           L0_ffn_down_w, L0_ffn_down_scale):
    global _NC_CACHE
    x = np.ascontiguousarray(np.asarray(L0_attn_x, dtype=np.float32))
    wq = np.ascontiguousarray(np.asarray(L0_attn_wq, dtype=np.float32))
    wk = np.ascontiguousarray(np.asarray(L0_attn_wk, dtype=np.float32))
    wv = np.ascontiguousarray(np.asarray(L0_attn_wv, dtype=np.float32))
    wo = np.ascontiguousarray(np.asarray(L0_attn_wo, dtype=np.float32))
    gw = np.ascontiguousarray(np.asarray(L0_ffn_gate_w, dtype=np.int32))
    uw = np.ascontiguousarray(np.asarray(L0_ffn_up_w, dtype=np.int32))
    dw = np.ascontiguousarray(np.asarray(L0_ffn_down_w, dtype=np.int32))
    gs = np.ascontiguousarray(np.asarray(L0_ffn_gate_scale, dtype=np.float32))
    us = np.ascontiguousarray(np.asarray(L0_ffn_up_scale, dtype=np.float32))
    ds = np.ascontiguousarray(np.asarray(L0_ffn_down_scale, dtype=np.float32))
    if _NC_CACHE is None:
        _NC_CACHE = build_nc()
    nc = _NC_CACHE
    in_maps = []
    for b in range(8):
        in_maps.append(dict(x=x[b], wq=wq, wk=wk, wv=wv, wo=wo,
                            gw=gw, uw=uw, dw=dw, gs=gs, us=us, ds=ds))
    res = run_bass_kernel_spmd(nc, in_maps, core_ids=list(range(8)))
    out = np.stack([np.asarray(r["out"], dtype=np.float32) for r in res.results], axis=0)
    return out


# version 09
# speedup vs baseline: 1.7463x; 1.0375x over previous
import numpy as np
import ml_dtypes
from contextlib import ExitStack

import concourse.bass as bass
import concourse.mybir as mybir
from concourse.bass_utils import run_bass_kernel_spmd

F32 = mybir.dt.float32
BF16 = mybir.dt.bfloat16
I32 = mybir.dt.int32
AF = mybir.ActivationFunctionType
ALU = mybir.AluOpType
bfdt = ml_dtypes.bfloat16

D = 2048
T = 2048
I = 8192
KC = D // 128
NTB = 4
TB = 512
NPAIR = 16


def _consts():
    inv = 1.0 / 10000.0 ** (np.arange(0, 64, 2, dtype=np.float64) / 64.0)
    t = np.arange(T, dtype=np.float64)
    f = np.outer(t, inv)
    COS = np.tile(np.cos(f).T, (4, 1)).astype(np.float32).astype(bfdt)
    sgn = np.where((np.arange(128) // 32) % 2 == 0, -1.0, 1.0)[:, None]
    SINS = (np.tile(np.sin(f).T, (4, 1)) * sgn).astype(np.float32).astype(bfdt)
    m = np.arange(128)
    nb = (m // 64) * 64 + 2 * (m % 32) + ((m % 64) // 32)
    sigma = np.where(m % 64 < 32, m + 32, m - 32)
    PNB = np.zeros((128, 128), dtype=bfdt)
    PNB[nb, m] = 1
    PCMB = np.zeros((128, 128), dtype=bfdt)
    PCMB[nb[sigma], m] = 1
    DUPLO = np.zeros((128, 128), dtype=bfdt)
    DUPLO[m % 64, m] = 1
    DUPHI = np.zeros((128, 128), dtype=bfdt)
    DUPHI[64 + (m % 64), m] = 1
    IDN = np.zeros((128, 128), dtype=bfdt)
    IDN[np.arange(128), np.arange(128)] = 1
    NIDN = np.zeros((128, 128), dtype=bfdt)
    NIDN[np.arange(128), np.arange(128)] = -1.0e5
    MASKS = []
    tk = np.arange(128)[:, None]
    tq = np.arange(512)[None, :]
    for v in range(4):
        MASKS.append((tk > tq - 128 * v).astype(np.float32).astype(bfdt))
    return COS, SINS, PNB, PCMB, DUPLO, DUPHI, IDN, NIDN, MASKS


def build_nc(dbg=False):
    nc = bass.Bass("TRN2", target_bir_lowering=False)
    x_h = nc.dram_tensor("x", [T, D], F32, kind="ExternalInput").ap()
    wq_h = nc.dram_tensor("wq", [D, D], F32, kind="ExternalInput").ap()
    wk_h = nc.dram_tensor("wk", [512, D], F32, kind="ExternalInput").ap()
    wv_h = nc.dram_tensor("wv", [512, D], F32, kind="ExternalInput").ap()
    wo_h = nc.dram_tensor("wo", [D, D], F32, kind="ExternalInput").ap()
    gw_h = nc.dram_tensor("gw", [I, D], I32, kind="ExternalInput").ap()
    uw_h = nc.dram_tensor("uw", [I, D], I32, kind="ExternalInput").ap()
    dw_h = nc.dram_tensor("dw", [D, I], I32, kind="ExternalInput").ap()
    gs_h = nc.dram_tensor("gs", [I], F32, kind="ExternalInput").ap()
    us_h = nc.dram_tensor("us", [I], F32, kind="ExternalInput").ap()
    ds_h = nc.dram_tensor("ds", [D], F32, kind="ExternalInput").ap()
    out_h = nc.dram_tensor("out", [T, D], F32, kind="ExternalOutput").ap()
    if dbg:
        d_xn = nc.dram_tensor("d_xn", [T, D], BF16, kind="ExternalOutput").ap()
        d_hb = nc.dram_tensor("d_hb", [4, 128, 64, 512], BF16, kind="ExternalOutput").ap()
        d_sd = nc.dram_tensor("d_sd", [4, 128, 64 * 512], BF16, kind="ExternalOutput").ap()
        d_sg2 = nc.dram_tensor("d_sg2", [32, 128, 16 * 256], BF16, kind="ExternalOutput").ap()
        d_ffn = nc.dram_tensor("d_ffn", [D, T], BF16, kind="ExternalOutput").ap()

        d_q = nc.dram_tensor("d_q", [128, 16, T], BF16, kind="ExternalOutput").ap()
        d_kd = nc.dram_tensor("d_kd", [128, 8, T], BF16, kind="ExternalOutput").ap()
        d_v = nc.dram_tensor("d_v", [128, 16, 520], BF16, kind="ExternalOutput").ap()
        d_afm = nc.dram_tensor("d_afm", [128, 16, 512], BF16, kind="ExternalOutput").ap()
        d_at = nc.dram_tensor("d_at", [D, T], BF16, kind="ExternalOutput").ap()
        d_h = nc.dram_tensor("d_h", [128, 64, 512], BF16, kind="ExternalOutput").ap()
    scr_wq = nc.dram_tensor("scr_wq", [8, 128, 16 * 256], BF16).ap()
    scr_wk = nc.dram_tensor("scr_wk", [2, 128, 16 * 256], BF16).ap()
    scr_wv = nc.dram_tensor("scr_wv", [2, 128, 16 * 256], BF16).ap()
    scr_wo = nc.dram_tensor("scr_wo", [8, 128, 16 * 256], BF16).ap()
    scr_g = nc.dram_tensor("scr_g", [32, 128, 16 * 256], BF16).ap()
    scr_u = nc.dram_tensor("scr_u", [32, 128, 16 * 256], BF16).ap()
    scr_d = nc.dram_tensor("scr_d", [8, 128, 64 * 256], BF16).ap()
    scr_xn = nc.dram_tensor("scr_xn", [T, D], BF16).ap()
    scr_at = nc.dram_tensor("scr_at", [D, T], BF16).ap()
    scr_ffn = nc.dram_tensor("scr_ffn", [D, T], BF16).ap()
    COS_c, SINS_c, PNB_c, PCMB_c, DLO_c, DHI_c, IDN_c, NIDN_c, MASKS_c = _consts()
    cos_d = nc.inline_tensor(COS_c, name="cos_c").ap()
    sin_d = nc.inline_tensor(SINS_c, name="sin_c").ap()
    pnb_d = nc.inline_tensor(PNB_c, name="pnb_c").ap()
    pcmb_d = nc.inline_tensor(PCMB_c, name="pcmb_c").ap()
    dlo_d = nc.inline_tensor(DLO_c, name="dlo_c").ap()
    dhi_d = nc.inline_tensor(DHI_c, name="dhi_c").ap()
    idn_d = nc.inline_tensor(IDN_c, name="idn_c").ap()
    nidn_d = nc.inline_tensor(NIDN_c, name="nidn_c").ap()
    msk_d = [nc.inline_tensor(MASKS_c[v], name=f"msk{v}_c").ap() for v in range(4)]

    es = ExitStack()

    class S:
        def __init__(self, name):
            self.h = es.enter_context(nc.semaphore(name))
            self.v = 0
    sl = S("sl")
    ss = S("ss")
    sg = S("sg")
    spe = S("spe")
    sac = S("sac")
    sdv = S("sdv")
    spl = S("spl")
    sout = S("sout")

    QSEM = {"sync": sl, "scalar": ss, "gpsimd": sg}
    ESEM = {"tensor": spe, "scalar": sac, "vector": sdv, "gpsimd": spl}

    ENGS = ["sync", "scalar", "vector", "tensor", "gpsimd"]
    queues = {e: [] for e in ENGS}

    def flush_block():
        with nc.Block() as blk:
            for name in ENGS:
                fns = queues[name]
                if fns:
                    getattr(blk, name)(lambda e, fns=fns: [f(e) for f in fns])
        for name in ENGS:
            queues[name] = []

    def emit(eng, f):
        queues[eng].append(f)

    def wait(eng, sem, v=None):
        vv = sem.v if v is None else v
        if vv > 0:
            emit(eng, lambda e, s=sem.h, v=vv: e.wait_ge(s, v))

    def dma(eng, out, in_, **kw):
        sem = QSEM[eng]
        sem.v += 16
        emit(eng, lambda e, o=out, i=in_, k=kw, s=sem.h: e.dma_start(out=o, in_=i, **k).then_inc(s, 16))
        return sem.v

    def inc(eng, call):
        sem = ESEM[eng]
        sem.v += 1
        emit(eng, lambda e, c=call, s=sem.h: c(e).then_inc(s, 1))
        return sem.v

    def op(eng, call):
        emit(eng, call)

    def drain_dmas():
        wait("sync", sl, sl.v)
        wait("scalar", ss, ss.v)
        wait("gpsimd", sg, sg.v)

    def sb_in(stack, name, shape, dt):
        return stack.enter_context(nc.sbuf_tensor(name, shape, dt))

    COS = sb_in(es, "COS", [128, T], BF16)
    SINS = sb_in(es, "SINS", [128, T], BF16)
    PNB = sb_in(es, "PNB", [128, 128], BF16)
    PCMB = sb_in(es, "PCMB", [128, 128], BF16)
    DLO = sb_in(es, "DLO", [128, 128], BF16)
    DHI = sb_in(es, "DHI", [128, 128], BF16)
    IDN = sb_in(es, "IDN", [128, 128], BF16)
    MSK = [sb_in(es, f"MSK{v}", [128, 512], BF16) for v in range(4)]
    NIDN = sb_in(es, "NIDN", [128, 128], BF16)
    gs_sb = sb_in(es, "gs_sb", [128, 64], F32)
    us_sb = sb_in(es, "us_sb", [128, 64], F32)
    ds_sb = sb_in(es, "ds_sb", [128, 16], F32)
    ones64 = sb_in(es, "ones64", [1, 64], BF16)
    epsv = sb_in(es, "epsv", [128, 1], F32)
    n128v = sb_in(es, "n128v", [128, 1], F32)

    es0 = ExitStack()
    stg32 = [sb_in(es0, f"stg32_{i}", [128, I], I32) for i in range(2)]
    stg16 = [sb_in(es0, f"stg16_{i}", [128, I], BF16) for i in range(2)]
    stgT = [sb_in(es0, f"stgT{i}", [128, 64, 256], BF16) for i in range(2)]
    pes0 = ExitStack()
    pst = [pes0.enter_context(nc.psum_tensor(f"pst{i}", [128, 128], BF16)) for i in range(4)]
    st32_rd = [0, 0]
    st16_rd = [0, 0]
    st16_wr = [0, 0]
    pst_rd = [0, 0, 0, 0]
    stgT_wr = [0, 0]

    op("gpsimd", lambda e: e.memset(ones64[:], 1.0))
    op("gpsimd", lambda e: e.memset(epsv[:], 1e-5))
    op("gpsimd", lambda e: e.memset(n128v[:], -128.0))
    dma("sync", COS[:], cos_d)
    dma("sync", SINS[:], sin_d)
    dma("sync", PNB[:], pnb_d)
    dma("sync", PCMB[:], pcmb_d)
    dma("sync", DLO[:], dlo_d)
    dma("sync", DHI[:], dhi_d)
    dma("sync", IDN[:], idn_d)
    dma("sync", NIDN[:], nidn_d)
    for v in range(4):
        dma("sync", MSK[v][:], msk_d[v])
    for c in range(64):
        dma("sync", gs_sb[:, c:c + 1], gs_h[c * 128:(c + 1) * 128][:, None])
        dma("sync", us_sb[:, c:c + 1], us_h[c * 128:(c + 1) * 128][:, None])
    for c in range(16):
        dma("sync", ds_sb[:, c:c + 1], ds_h[c * 128:(c + 1) * 128][:, None])
    wait("tensor", sl, sl.v)

    def prep_weight(src_h, n_chunks, W, scr_T, sub128, src_dtype):
        kcn = W // 128
        for c in range(n_chunks):
            i = c % 2
            s32 = stg32[i].ap().bitcast(src_dtype)[:, 0:W] if src_dtype != I32 else stg32[i][:, 0:W]
            s16 = stg16[i]
            wait("gpsimd", sac, st32_rd[i])
            dma("gpsimd", s32[:, :], src_h[c * 128:(c + 1) * 128, :])
            wait("scalar", sg)
            wait("scalar", spe, st16_rd[i])
            if sub128:
                st32_rd[i] = inc("scalar", lambda e, s32=s32, s16=s16, W=W: e.activation(
                    out=s16[:, 0:W], in_=s32, func=AF.Identity, bias=n128v[:]))
            else:
                st32_rd[i] = inc("scalar", lambda e, s32=s32, s16=s16, W=W: e.activation(
                    out=s16[:, 0:W], in_=s32, func=AF.Copy))
            cast_done = st32_rd[i]
            wait("tensor", sac, cast_done)
            co = (c % 2) * 128
            stg_i = (c // 2) % 2
            stA = stgT[stg_i]
            last_tp = 0
            for kc in range(kcn):
                p = pst[kc % 4]
                wait("tensor", sdv, pst_rd[kc % 4])
                last_tp = inc("tensor", lambda e, p=p, s16=s16, kc=kc: e.transpose(
                    out=p[:], in_=s16[:, kc * 128:(kc + 1) * 128], identity=IDN[:]))
                wait("vector", spe, last_tp)
                wait("vector", sl, stgT_wr[stg_i])
                pst_rd[kc % 4] = inc("vector", lambda e, p=p, stA=stA, kc=kc, co=co: e.tensor_copy(
                    out=stA[:, kc, co:co + 128], in_=p[:]))
            st16_rd[i] = last_tp
            if c % 2 == 1:
                g = c // 2
                wait("sync", sdv)
                dma("sync", scr_T[g], stA[:, 0:kcn, 0:256])
                stgT_wr[stg_i] = sl.v

    prep_weight(wq_h, 16, D, scr_wq, False, F32)
    prep_weight(wk_h, 4, D, scr_wk, False, F32)
    prep_weight(wv_h, 4, D, scr_wv, False, F32)
    prep_weight(wo_h, 16, D, scr_wo, False, F32)
    prep_weight(gw_h, 64, D, scr_g, True, I32)
    prep_weight(uw_h, 64, D, scr_u, True, I32)
    prep_weight(dw_h, 16, I, scr_d, True, I32)
    drain_dmas()
    flush_block()
    pes0.close()
    es0.close()

    es1 = ExitStack()
    q_sb = sb_in(es1, "q_sb", [128, 16, T], BF16)
    kd_sb = sb_in(es1, "kd_sb", [128, 8, T], BF16)
    v_sb = sb_in(es1, "v_sb", [128, 16, 520], BF16)

    es1b = ExitStack()
    xt = [sb_in(es1b, f"xt{i}", [128, D], F32) for i in range(2)]
    xn_tm = sb_in(es1b, "xn_tm", [128, D], BF16)
    xn_fm = sb_in(es1b, "xn_fm", [128, KC, TB], BF16)
    sumv = sb_in(es1b, "sumv", [128, 1], F32)
    sqv = sb_in(es1b, "sqv", [128, 1], F32)
    mv = sb_in(es1b, "mv", [128, 1], F32)
    ex2v = sb_in(es1b, "ex2v", [128, 1], F32)
    varv = sb_in(es1b, "varv", [128, 1], F32)
    rstdv = sb_in(es1b, "rstdv", [128, 1], F32)
    nmrv = sb_in(es1b, "nmrv", [128, 1], F32)
    wbuf = [sb_in(es1b, f"wbuf{i}", [128, 16, 256], BF16) for i in range(2)]
    wbv2 = [sb_in(es1b, f"wbv2_{i}", [128, 16, 256], BF16) for i in range(2)]
    qtmp = [sb_in(es1b, f"qtmp{i}", [128, 512], BF16) for i in range(2)]
    t1b = sb_in(es1b, "t1b", [128, 512], BF16)
    t2b = sb_in(es1b, "t2b", [128, 512], BF16)
    krope = sb_in(es1b, "krope", [128, 4, 512], BF16)

    pes = ExitStack()
    pp = [pes.enter_context(nc.psum_tensor(f"pp{i}", [128, 512], F32)) for i in range(2)]
    pa_ps = pes.enter_context(nc.psum_tensor("pa_ps", [128, 512], F32))
    pb_ps = pes.enter_context(nc.psum_tensor("pb_ps", [128, 512], F32))

    for tt in range(16):
        for j in range(8):
            op("vector", lambda e, tt=tt, j=j: e.memset(v_sb[:, tt, j * 65 + 64:j * 65 + 65], 1.0))

    xt_rd = [0, 0]
    xntm_rd = [0]
    sqv_rd = [0]
    qtmp_st = [(0, 0), (0, 0)]
    pp_rd = [0, 0]
    pa_rd = [0]
    pb_rd = [0]
    rope_flip = [0]

    def ln_tile(b, tau):
        t0 = (4 * b + tau) * 128
        xe = xt[tau % 2]
        wait("sync", sac, xt_rd[tau % 2])
        lv = dma("sync", xe[:], x_h[t0:t0 + 128, :])
        wait("vector", sl, lv)
        sdv_red = inc("vector", lambda e, xe=xe: e.reduce_sum(sumv[:], xe[:], axis=mybir.AxisListType.X))
        wait("scalar", sl, lv)
        wait("scalar", sdv, sqv_rd[0])
        wait("scalar", ss, xntm_rd[0])
        sac_sq = inc("scalar", lambda e, xe=xe: e.activation(
            out=xn_tm[:], in_=xe[:], func=AF.Square, accum_out=sqv[:]))
        wait("vector", sdv, sdv_red)
        sdv_mv = inc("vector", lambda e: e.tensor_scalar_mul(out=mv[:], in0=sumv[:], scalar1=1.0 / D))
        wait("vector", sac, sac_sq)
        sqv_rd[0] = inc("vector", lambda e: e.tensor_scalar_mul(out=ex2v[:], in0=sqv[:], scalar1=1.0 / D))
        wait("vector", sdv, sdv_mv)
        sdv_vm = inc("vector", lambda e: e.tensor_tensor(out=varv[:], in0=mv[:], in1=mv[:], op=ALU.mult))
        wait("vector", sdv, sdv_vm)
        sdv_var = inc("vector", lambda e: e.tensor_tensor(out=varv[:], in0=ex2v[:], in1=varv[:], op=ALU.subtract))
        wait("scalar", sdv, sdv_var)
        sac_ln1 = inc("scalar", lambda e: e.activation(out=rstdv[:], in_=varv[:], func=AF.Ln, bias=epsv[:]))
        wait("scalar", sac, sac_ln1)
        sac_r = inc("scalar", lambda e: e.activation(out=rstdv[:], in_=rstdv[:], func=AF.Exp, scale=-0.5))
        wait("vector", sac, sac_r)
        sdv_nmr = inc("vector", lambda e: e.tensor_tensor(out=nmrv[:], in0=mv[:], in1=rstdv[:], op=ALU.mult))
        wait("vector", sdv, sdv_nmr)
        sdv_nmr2 = inc("vector", lambda e: e.tensor_scalar_mul(out=nmrv[:], in0=nmrv[:], scalar1=-1.0))
        wait("scalar", sdv, sdv_nmr2)
        sac_xn = inc("scalar", lambda e, xe=xe: e.activation(
            out=xn_tm[:], in_=xe[:], func=AF.Identity, scale=rstdv[:], bias=nmrv[:]))
        xt_rd[tau % 2] = sac_xn
        wait("scalar", sac, sac_xn)
        dma("scalar", scr_xn[t0:t0 + 128, :], xn_tm[:])
        xntm_rd[0] = ss.v

    def rope_chunk(b, ps_src, ps_idx, dst_ap):
        fl = rope_flip[0]
        rope_flip[0] ^= 1
        qt = qtmp[fl]
        (g_spe, g_sdv) = qtmp_st[fl]
        wait("scalar", spe, g_spe)
        wait("scalar", sdv, g_sdv)
        sac_ev = inc("scalar", lambda e, qt=qt, p=ps_src: e.activation(out=qt[:], in_=p[:], func=AF.Copy))
        if ps_idx is not None:
            pp_rd[ps_idx] = sac_ev
        wait("tensor", sac, sac_ev)
        wait("tensor", sdv, pa_rd[0])
        spe_a = inc("tensor", lambda e, qt=qt: e.matmul(out=pa_ps[:], lhsT=PNB[:], rhs=qt[:],
                                                        start=True, stop=True))
        wait("tensor", sdv, pb_rd[0])
        spe_b = inc("tensor", lambda e, qt=qt: e.matmul(out=pb_ps[:], lhsT=PCMB[:], rhs=qt[:],
                                                        start=True, stop=True))
        bsl = slice(b * TB, (b + 1) * TB)
        wait("vector", spe, spe_a)
        sdv_t1 = inc("vector", lambda e, bsl=bsl: e.tensor_tensor(
            out=t1b[:], in0=pa_ps[:], in1=COS[:, bsl], op=ALU.mult))
        pa_rd[0] = sdv_t1
        wait("vector", spe, spe_b)
        sdv_t2 = inc("vector", lambda e, bsl=bsl: e.tensor_tensor(
            out=t2b[:], in0=pb_ps[:], in1=SINS[:, bsl], op=ALU.mult))
        pb_rd[0] = sdv_t2
        wait("vector", sdv, sdv_t2)
        sdv_o = inc("vector", lambda e, d=dst_ap: e.tensor_tensor(out=d, in0=t1b[:], in1=t2b[:], op=ALU.add))
        qtmp_st[fl] = (spe_a if False else spe_b, sdv_t1)
        return sac_ev, sdv_o

    wbuf_rd = [0, 0]
    wbv_rd = [0, 0]
    v_ev = [0]

    for b in range(NTB):
        bsl = slice(b * TB, (b + 1) * TB)
        for tau in range(4):
            ln_tile(b, tau)
        wait("sync", ss, ss.v)
        for c in range(KC):
            dma("sync", xn_fm[:, c, :], scr_xn[bsl, c * 128:(c + 1) * 128], transpose=True)
        xn_ready = sl.v
        for s in range(8):
            wb = wbuf[s % 2]
            wait("sync", spe, wbuf_rd[s % 2])
            dma("sync", wb[:, :, :], scr_wq[s])
            ld_done = sl.v
            for mj in range(2):
                mu = s * 2 + mj
                pe = pp[mu % 2]
                wait("tensor", sl, ld_done)
                wait("tensor", sl, xn_ready)
                wait("tensor", sac, pp_rd[mu % 2])
                last = 0
                for k in range(KC):
                    last = inc("tensor", lambda e, k=k, pe=pe, wb=wb, mj=mj: e.matmul(
                        out=pe[:], lhsT=wb[:, k, mj * 128:(mj + 1) * 128], rhs=xn_fm[:, k, :],
                        start=(k == 0), stop=(k == KC - 1)))
                wbuf_rd[s % 2] = last
                wait("scalar", spe, last)
                rope_chunk(b, pe, mu % 2, q_sb[:, mu, bsl])
        for s in range(2):
            wbk = wbuf[s % 2]
            wait("sync", spe, wbuf_rd[s % 2])
            dma("sync", wbk[:, :, :], scr_wk[s])
            ld_done = sl.v
            for mj in range(2):
                mu = s * 2 + mj
                pe = pp[mu % 2]
                wait("tensor", sl, ld_done)
                wait("tensor", sl, xn_ready)
                wait("tensor", sac, pp_rd[mu % 2])
                last = 0
                for k in range(KC):
                    last = inc("tensor", lambda e, k=k, pe=pe, wbk=wbk, mj=mj: e.matmul(
                        out=pe[:], lhsT=wbk[:, k, mj * 128:(mj + 1) * 128], rhs=xn_fm[:, k, :],
                        start=(k == 0), stop=(k == KC - 1)))
                wbuf_rd[s % 2] = last
                wait("scalar", spe, last)
                _, sdv_o = rope_chunk(b, pe, mu % 2, krope[:, mu, :])
                wait("tensor", sdv, sdv_o)
                for half, PM in ((0, DLO), (1, DHI)):
                    wait("tensor", sdv, pa_rd[0])
                    spe_d = inc("tensor", lambda e, PM=PM, mu=mu: e.matmul(
                        out=pa_ps[:], lhsT=PM[:], rhs=krope[:, mu, :], start=True, stop=True))
                    wait("scalar", spe, spe_d)
                    sac_e = inc("scalar", lambda e, mu=mu, half=half, bsl=bsl: e.activation(
                        out=kd_sb[:, 2 * mu + half, bsl], in_=pa_ps[:], func=AF.Copy))
                    pa_rd[0] = 0
                    wait("tensor", sac, sac_e)
        for hv in range(2):
            wait("sync", spe, wbv_rd[hv])
            dma("sync", wbv2[hv][:, :, :], scr_wv[hv])
        ld_done = sl.v
        for tau in range(4):
            pe0, pe1 = pp[0], pp[1]
            wait("tensor", sl, ld_done)
            wait("tensor", sl, xn_ready)
            wait("tensor", sac, pp_rd[0])
            wait("tensor", sac, pp_rd[1])
            wait("tensor", sdv, v_ev[0])
            last = 0
            for k in range(KC):
                inc("tensor", lambda e, k=k, pe0=pe0, tau=tau: e.matmul(
                    out=pe0[:, 0:256], lhsT=xn_fm[:, k, tau * 128:(tau + 1) * 128],
                    rhs=wbv2[0][:, k, :], start=(k == 0), stop=(k == KC - 1)))
                last = inc("tensor", lambda e, k=k, pe1=pe1, tau=tau: e.matmul(
                    out=pe1[:, 0:256], lhsT=xn_fm[:, k, tau * 128:(tau + 1) * 128],
                    rhs=wbv2[1][:, k, :], start=(k == 0), stop=(k == KC - 1)))
            wbv_rd[0] = wbv_rd[1] = last
            wait("vector", spe, last)
            vout0 = v_sb[:, 4 * b + tau, 0:260].rearrange("p (h x) -> p h x", h=4)[:, :, 0:64]
            vout1 = v_sb[:, 4 * b + tau, 260:520].rearrange("p (h x) -> p h x", h=4)[:, :, 0:64]
            vin0 = pe0[:, 0:256].rearrange("p (h x) -> p h x", h=4)
            vin1 = pe1[:, 0:256].rearrange("p (h x) -> p h x", h=4)
            inc("vector", lambda e, vout0=vout0, vin0=vin0: e.tensor_copy(out=vout0, in_=vin0))
            sdv_v = inc("vector", lambda e, vout1=vout1, vin1=vin1: e.tensor_copy(out=vout1, in_=vin1))
            v_ev[0] = sdv_v
            wait("tensor", sdv, sdv_v)
            pp_rd[0] = pp_rd[1] = 0
        if b == NTB - 1:
            drain_dmas()
        flush_block()

    es1b.close()
    pes.close()

    es2 = ExitStack()
    wob = [sb_in(es2, f"wob{i}", [128, 16, 256], BF16) for i in range(2)]
    pbuf = [sb_in(es2, f"pbuf{i}", [128, 1024], BF16) for i in range(2)]
    afm = sb_in(es2, "afm", [128, 16, 512], BF16)
    lnv2 = sb_in(es2, "lnv2", [1, 512], F32)
    rbf = sb_in(es2, "rbf", [1, 512], BF16)
    rb64 = sb_in(es2, "rb64", [64, 512], BF16)
    tmp64 = sb_in(es2, "tmp64", [64, 512], BF16)
    woev = [sb_in(es2, f"woev{i}", [128, 512], BF16) for i in range(2)]

    pes2 = ExitStack()
    psc = [pes2.enter_context(nc.psum_tensor(f"psc{i}", [128, 1024], F32)) for i in range(2)]
    pav = [pes2.enter_context(nc.psum_tensor(f"pav{i}", [128, 512], F32)) for i in range(4)]

    norm_state = {"rbf_rd": 0, "rb64_rd": 0}

    for B in range(NTB):
        if dbg and B == 0:
            sout.v += 16; emit("gpsimd", lambda e, s=sout.h: e.dma_start(out=d_xn, in_=scr_xn).then_inc(s, 16))
            sout.v += 16; emit("gpsimd", lambda e, s=sout.h: e.dma_start(out=d_sd, in_=scr_d).then_inc(s, 16))
            sout.v += 16; emit("gpsimd", lambda e, s=sout.h: e.dma_start(out=d_sg2, in_=scr_g).then_inc(s, 16))


            sout.v += 16; emit("gpsimd", lambda e, s=sout.h: e.dma_start(out=d_q, in_=q_sb[:]).then_inc(s, 16))
            sout.v += 16; emit("gpsimd", lambda e, s=sout.h: e.dma_start(out=d_kd, in_=kd_sb[:]).then_inc(s, 16))
            sout.v += 16; emit("gpsimd", lambda e, s=sout.h: e.dma_start(out=d_v, in_=v_sb[:]).then_inc(s, 16))
        Bsl = slice(B * TB, (B + 1) * TB)
        nck = 4 * B + 4
        for s in range(2):
            dma("sync", wob[s][:, :, :], scr_wo[s])
        wo_ld01 = sl.v

        exp_read = {}
        av_tt = {}
        av_last = {}
        norm_q = []

        def do_norm(e_pair, pidx):
            for hh in range(2):
                pa = pav[pidx + hh]
                sac_ln = inc("scalar", lambda e, pa=pa: e.activation(
                    out=lnv2[:], in_=pa[64:65, :], func=AF.Ln))
                wait("scalar", spe, norm_state["rbf_rd"])
                wait("scalar", sac, sac_ln)
                sac_r = inc("scalar", lambda e: e.activation(
                    out=rbf[:], in_=lnv2[:], func=AF.Exp, scale=-1.0))
                wait("tensor", sac, sac_r)
                spe_b = inc("tensor", lambda e, pa=pa: e.matmul(
                    out=pa[64:128, :], lhsT=ones64[:], rhs=rbf[:], start=True, stop=True))
                norm_state["rbf_rd"] = spe_b
                wait("scalar", spe, spe_b)
                wait("scalar", sdv, norm_state["rb64_rd"])
                sac_rb = inc("scalar", lambda e, pa=pa: e.activation(
                    out=rb64[:], in_=pa[64:128, :], func=AF.Copy))
                wait("vector", sac, sac_rb)
                if hh == 0:
                    sdv_tt = inc("vector", lambda e, pa=pa, ep=e_pair: e.tensor_tensor(
                        out=afm[0:64, ep, :], in0=pa[0:64, :], in1=rb64[:], op=ALU.mult))
                else:
                    sdv_t64 = inc("vector", lambda e, pa=pa: e.tensor_tensor(
                        out=tmp64[:], in0=pa[0:64, :], in1=rb64[:], op=ALU.mult))
                    wait("vector", sdv, sdv_t64)
                    sdv_tt = inc("vector", lambda e, ep=e_pair: e.tensor_scalar_mul(
                        out=afm[64:128, ep, :], in0=tmp64[:], scalar1=1.0))
                norm_state["rb64_rd"] = sdv_tt
                av_tt[pidx + hh] = sdv_tt

        for e_pair in range(NPAIR):
            j = e_pair // 2
            pidx = (e_pair % 2) * 2
            for hh in range(2):
                if (pidx + hh) in av_tt:
                    wait("tensor", sdv, av_tt[pidx + hh])
            av_q = []
            for c in range(nck):
                ps = psc[c % 2]
                if (c % 2) in exp_read:
                    wait("tensor", sac, exp_read[c % 2])
                inc("tensor", lambda e, ps=ps, j=j, c=c, ep=e_pair, Bsl=Bsl: e.matmul(
                    out=ps[:, 0:512], lhsT=kd_sb[0:64, j, c * 128:(c + 1) * 128],
                    rhs=q_sb[0:64, ep, Bsl], start=True, stop=(c < 4 * B)))
                spe_s = inc("tensor", lambda e, ps=ps, j=j, c=c, ep=e_pair, Bsl=Bsl: e.matmul(
                    out=ps[:, 512:1024], lhsT=kd_sb[64:128, j, c * 128:(c + 1) * 128],
                    rhs=q_sb[64:128, ep, Bsl], start=True, stop=(c < 4 * B)))
                if c >= 4 * B:
                    vmk = MSK[c - 4 * B]
                    inc("tensor", lambda e, ps=ps, vmk=vmk: e.matmul(
                        out=ps[:, 0:512], lhsT=NIDN[:], rhs=vmk[:], start=False, stop=True))
                    spe_s = inc("tensor", lambda e, ps=ps, vmk=vmk: e.matmul(
                        out=ps[:, 512:1024], lhsT=NIDN[:], rhs=vmk[:], start=False, stop=True))
                if av_q:
                    av_q.pop(0)()
                wait("scalar", spe, spe_s)
                if (c % 2) in av_last:
                    wait("scalar", spe, av_last[c % 2])
                sac_e = inc("scalar", lambda e, ps=ps, c=c: e.activation(
                    out=pbuf[c % 2][:], in_=ps[:], func=AF.Exp, scale=0.125))
                exp_read[c % 2] = sac_e

                def do_av(c=c, j=j, pidx=pidx, nck=nck, sac_e=sac_e):
                    wait("tensor", sac, sac_e)
                    inc("tensor", lambda e, c=c, j=j, pidx=pidx, nck=nck: e.matmul(
                        out=pav[pidx][0:65, :], lhsT=v_sb[:, c, j * 65:(j + 1) * 65],
                        rhs=pbuf[c % 2][:, 0:512], start=(c == 0), stop=(c == nck - 1)))
                    spe_av = inc("tensor", lambda e, c=c, j=j, pidx=pidx, nck=nck: e.matmul(
                        out=pav[pidx + 1][0:65, :], lhsT=v_sb[:, c, j * 65:(j + 1) * 65],
                        rhs=pbuf[c % 2][:, 512:1024], start=(c == 0), stop=(c == nck - 1)))
                    av_last[c % 2] = spe_av
                av_q.append(do_av)
            while av_q:
                av_q.pop(0)()
            if norm_q:
                norm_q.pop(0)()
            norm_q.append(lambda ep=e_pair, pidx=pidx: do_norm(ep, pidx))
        while norm_q:
            norm_q.pop(0)()
        attn_done = sdv.v
        woev_rd = [0, 0]
        wob_rd = [0, 0]
        for s in range(8):
            wot = wob[s % 2]
            if s >= 2:
                wait("sync", spe, wob_rd[s % 2])
                dma("sync", wot[:, :, :], scr_wo[s])
                ld_done = sl.v
            else:
                ld_done = wo_ld01
            for mj in range(2):
                mu = s * 2 + mj
                pw = psc[mu % 2]
                wait("tensor", sl, ld_done)
                wait("tensor", sdv, attn_done)
                if (mu % 2) in exp_read:
                    wait("tensor", sac, exp_read.pop(mu % 2))
                last = 0
                for dv in range(16):
                    last = inc("tensor", lambda e, dv=dv, pw=pw, wot=wot, mj=mj: e.matmul(
                        out=pw[:, 0:512], lhsT=wot[:, dv, mj * 128:(mj + 1) * 128], rhs=afm[:, dv, :],
                        start=(dv == 0), stop=(dv == 15)))
                wob_rd[s % 2] = last
                wait("scalar", spe, last)
                wait("scalar", ss, woev_rd[mu % 2])
                sac_ev = inc("scalar", lambda e, pw=pw, mu=mu: e.activation(
                    out=woev[mu % 2][:], in_=pw[:, 0:512], func=AF.Copy))
                exp_read[mu % 2] = sac_ev
                wait("scalar", sac, sac_ev)
                dma("scalar", scr_at[mu * 128:(mu + 1) * 128, Bsl], woev[mu % 2][:])
                woev_rd[mu % 2] = ss.v
        if dbg and B == NTB - 1:
            wait("gpsimd", sdv, attn_done)
            sout.v += 16; emit("gpsimd", lambda e, s=sout.h: e.dma_start(out=d_afm, in_=afm[:]).then_inc(s, 16))
        if B == NTB - 1:
            drain_dmas()
        flush_block()

    pes2.close()
    es2.close()
    es1.close()

    es3 = ExitStack()
    h_sb = sb_in(es3, "h_sb", [128, 64, 512], BF16)
    xnf = sb_in(es3, "xnf", [128, KC, 512], BF16)
    gwide = [sb_in(es3, f"gwide{i}", [128, KC, 256], BF16) for i in range(2)]
    uwide = [sb_in(es3, f"uwide{i}", [128, KC, 256], BF16) for i in range(2)]
    dbw = [sb_in(es3, f"dbw{i}", [128, 8, 256], BF16) for i in range(2)]
    sgt = [sb_in(es3, f"sgt{i}", [128, 512], BF16) for i in range(2)]
    sut = [sb_in(es3, f"sut{i}", [128, 512], BF16) for i in range(2)]
    fdev = [sb_in(es3, f"fdev{i}", [128, 512], BF16) for i in range(2)]
    xt3 = [sb_in(es3, f"xt3{i}", [128, D], F32) for i in range(2)]
    at3 = [sb_in(es3, f"at3{i}", [128, D], BF16) for i in range(2)]
    ft3 = [sb_in(es3, f"ft3{i}", [128, D], BF16) for i in range(2)]
    out3 = [sb_in(es3, f"out3{i}", [128, D], F32) for i in range(2)]

    pes3 = ExitStack()
    pg = [pes3.enter_context(nc.psum_tensor(f"pg{i}", [128, 512], F32)) for i in range(4)]
    pu = [pes3.enter_context(nc.psum_tensor(f"pu{i}", [128, 512], F32)) for i in range(4)]

    gu_mm = [0, 0]
    sgt_rd = [0, 0]
    gu_ev_sac = [0, 0]
    gu_ev_sdv = [0, 0]
    dn_mm = {}
    out3_rd = [0, 0]
    xt3_rd = [0, 0]
    at3_rd = [0, 0]
    fdev_rd = [0, 0]

    for b in range(NTB):
        bsl = slice(b * TB, (b + 1) * TB)
        for c in range(KC):
            dma("sync", xnf[:, c, :], scr_xn[bsl, c * 128:(c + 1) * 128], transpose=True)
        xn_ld = sl.v
        for sp in range(32):
            set_ = sp % 2
            gw_t, uw_t = gwide[set_], uwide[set_]
            wait("sync", spe, gu_mm[set_])
            dma("sync", gw_t[:, :, :], scr_g[sp])
            dma("sync", uw_t[:, :, :], scr_u[sp])
            ld_done = sl.v
            pga, pgb_ = pg[2 * set_], pg[2 * set_ + 1]
            pua, pub = pu[2 * set_], pu[2 * set_ + 1]
            wait("tensor", sl, ld_done)
            wait("tensor", sl, xn_ld)
            wait("tensor", sac, gu_ev_sac[set_])
            wait("tensor", sdv, gu_ev_sdv[set_])
            last = 0
            for k in range(KC):
                inc("tensor", lambda e, k=k, gw_t=gw_t, pga=pga: e.matmul(
                    out=pga[:], lhsT=gw_t[:, k, 0:128], rhs=xnf[:, k, :],
                    start=(k == 0), stop=(k == KC - 1)))
                inc("tensor", lambda e, k=k, gw_t=gw_t, pgb_=pgb_: e.matmul(
                    out=pgb_[:], lhsT=gw_t[:, k, 128:256], rhs=xnf[:, k, :],
                    start=(k == 0), stop=(k == KC - 1)))
                inc("tensor", lambda e, k=k, uw_t=uw_t, pua=pua: e.matmul(
                    out=pua[:], lhsT=uw_t[:, k, 0:128], rhs=xnf[:, k, :],
                    start=(k == 0), stop=(k == KC - 1)))
                last = inc("tensor", lambda e, k=k, uw_t=uw_t, pub=pub: e.matmul(
                    out=pub[:], lhsT=uw_t[:, k, 128:256], rhs=xnf[:, k, :],
                    start=(k == 0), stop=(k == KC - 1)))
            gu_mm[set_] = last
            wait("scalar", spe, last)
            wait("vector", spe, last)
            e_sac = e_sdv = 0
            for d in range(2):
                ci = sp * 2 + d
                pgx = pga if d == 0 else pgb_
                pux = pua if d == 0 else pub
                wait("scalar", sdv, sgt_rd[d])
                e_sac = inc("scalar", lambda e, pgx=pgx, ci=ci, d=d: e.activation(
                    out=sgt[d][:], in_=pgx[:], func=AF.Silu, scale=gs_sb[:, ci:ci + 1]))
                e_sdv0 = inc("vector", lambda e, pux=pux, ci=ci, d=d: e.tensor_single_scalar(
                    out=sut[d][:], in_=pux[:], scalar=us_sb[:, ci:ci + 1], op=ALU.mult))
                wait("vector", sac, e_sac)
                wait("vector", sdv, e_sdv0)
                e_sdv = inc("vector", lambda e, ci=ci, d=d: e.tensor_tensor(
                    out=h_sb[:, ci, :], in0=sgt[d][:], in1=sut[d][:], op=ALU.mult))
                sgt_rd[d] = e_sdv
            gu_ev_sac[set_] = e_sac
            gu_ev_sdv[set_] = e_sdv
        h_done = sdv.v
        gu_done_sac = sac.v
        dn_ev = {}
        for ms in range(8):
            pset = (ms % 2) * 2
            last = 0
            for grp in range(8):
                gbuf = dbw[grp % 2]
                wait("sync", spe, dn_mm.get(grp % 2, 0))
                dma("sync", gbuf[:, :, :],
                    scr_d[ms][:, grp * 8 * 256:(grp + 1) * 8 * 256])
                ld_done = sl.v
                wait("tensor", sl, ld_done)
                if grp == 0:
                    if ms < 2:
                        wait("tensor", sdv, h_done)
                        wait("tensor", sac, gu_done_sac)
                    for j in range(2):
                        if pset + j in dn_ev:
                            wait("tensor", sac, dn_ev[pset + j])
                for q in range(8):
                    ic = grp * 8 + q
                    for j in range(2):
                        last = inc("tensor", lambda e, j=j, pset=pset, gbuf=gbuf, q=q, ic=ic: e.matmul(
                            out=pg[pset + j][:], lhsT=gbuf[:, q, j * 128:(j + 1) * 128],
                            rhs=h_sb[:, ic, :], start=(ic == 0), stop=(ic == 63)))
                dn_mm[grp % 2] = last
            wait("scalar", spe, last)
            for j in range(2):
                mu = ms * 2 + j
                wait("scalar", ss, fdev_rd[j % 2])
                sac_d = inc("scalar", lambda e, j=j, pset=pset, mu=mu: e.activation(
                    out=fdev[j % 2][:], in_=pg[pset + j][:], func=AF.Identity, scale=ds_sb[:, mu:mu + 1]))
                dn_ev[pset + j] = sac_d
                wait("scalar", sac, sac_d)
                dma("scalar", scr_ffn[mu * 128:(mu + 1) * 128, bsl], fdev[j % 2][:])
                fdev_rd[j % 2] = ss.v
        fst_done = ss.v
        if dbg:
            wait("gpsimd", sdv, h_done)
            sout.v += 16
            emit("gpsimd", lambda e, b=b, s=sout.h: e.dma_start(out=d_hb[b], in_=h_sb[:]).then_inc(s, 16))
        wait("sync", ss, fst_done)
        for tau in range(4):
            t0 = (4 * b + tau) * 128
            wait("sync", sdv, xt3_rd[tau % 2])
            lx = dma("sync", xt3[tau % 2][:], x_h[t0:t0 + 128, :])
            wait("sync", sdv, at3_rd[tau % 2])
            la = dma("sync", at3[tau % 2][:], scr_at[:, t0:t0 + 128], transpose=True)
            lf = dma("sync", ft3[tau % 2][:], scr_ffn[:, t0:t0 + 128], transpose=True)
            wait("vector", sl, lf)
            wait("vector", sout, out3_rd[tau % 2])
            sdv_a = inc("vector", lambda e, tau=tau: e.tensor_tensor(
                out=out3[tau % 2][:], in0=xt3[tau % 2][:], in1=at3[tau % 2][:], op=ALU.add))
            xt3_rd[tau % 2] = sdv_a
            at3_rd[tau % 2] = sdv_a
            wait("vector", sdv, sdv_a)
            sdv_o = inc("vector", lambda e, tau=tau: e.tensor_tensor(
                out=out3[tau % 2][:], in0=out3[tau % 2][:], in1=ft3[tau % 2][:], op=ALU.add))
            wait("gpsimd", sdv, sdv_o)
            sout.v += 16
            emit("gpsimd", lambda e, t0=t0, tau=tau, s=sout.h: e.dma_start(
                out=out_h[t0:t0 + 128, :], in_=out3[tau % 2][:]).then_inc(s, 16))
            out3_rd[tau % 2] = sout.v
        flush_block()

    if dbg:
        sout.v += 16; emit("gpsimd", lambda e, s=sout.h: e.dma_start(out=d_at, in_=scr_at).then_inc(s, 16))
        sout.v += 16; emit("gpsimd", lambda e, s=sout.h: e.dma_start(out=d_ffn, in_=scr_ffn).then_inc(s, 16))
        sout.v += 16; emit("gpsimd", lambda e, s=sout.h: e.dma_start(out=d_h, in_=h_sb[:]).then_inc(s, 16))
    wait("gpsimd", sout, sout.v)
    flush_block()

    es3.close()
    pes3.close()
    es.close()
    return nc


_NC_CACHE = None


def kernel(L0_attn_x, L0_attn_wq, L0_attn_wk, L0_attn_wv, L0_attn_wo,
           L0_ffn_gate_w, L0_ffn_gate_scale, L0_ffn_up_w, L0_ffn_up_scale,
           L0_ffn_down_w, L0_ffn_down_scale):
    global _NC_CACHE
    x = np.ascontiguousarray(np.asarray(L0_attn_x, dtype=np.float32))
    wq = np.ascontiguousarray(np.asarray(L0_attn_wq, dtype=np.float32))
    wk = np.ascontiguousarray(np.asarray(L0_attn_wk, dtype=np.float32))
    wv = np.ascontiguousarray(np.asarray(L0_attn_wv, dtype=np.float32))
    wo = np.ascontiguousarray(np.asarray(L0_attn_wo, dtype=np.float32))
    gw = np.ascontiguousarray(np.asarray(L0_ffn_gate_w, dtype=np.int32))
    uw = np.ascontiguousarray(np.asarray(L0_ffn_up_w, dtype=np.int32))
    dw = np.ascontiguousarray(np.asarray(L0_ffn_down_w, dtype=np.int32))
    gs = np.ascontiguousarray(np.asarray(L0_ffn_gate_scale, dtype=np.float32))
    us = np.ascontiguousarray(np.asarray(L0_ffn_up_scale, dtype=np.float32))
    ds = np.ascontiguousarray(np.asarray(L0_ffn_down_scale, dtype=np.float32))
    if _NC_CACHE is None:
        _NC_CACHE = build_nc()
    nc = _NC_CACHE
    in_maps = []
    for b in range(8):
        in_maps.append(dict(x=x[b], wq=wq, wk=wk, wv=wv, wo=wo,
                            gw=gw, uw=uw, dw=dw, gs=gs, us=us, ds=ds))
    res = run_bass_kernel_spmd(nc, in_maps, core_ids=list(range(8)))
    out = np.stack([np.asarray(r["out"], dtype=np.float32) for r in res.results], axis=0)
    return out
